# version 23
# baseline (speedup 1.0000x reference)
import numpy as np
from contextlib import ExitStack
import concourse.bass as bass
import concourse.mybir as mybir
from concourse.bass_utils import run_bass_kernel_spmd

F32 = mybir.dt.float32
BF16 = mybir.dt.bfloat16
FP16 = mybir.dt.float16
ALU = mybir.AluOpType
ACT = mybir.ActivationFunctionType
AX = mybir.AxisListType

NEG_BIG = -1.0e30
MAGIC = 12582912.0
EPS_LN = 1.0e-5
EPS_RMS = 1.0e-5
EPS_Z = 1.0e-9
G_CLAMP = -60.0


class Cfg:
    def __init__(self, n_cores=8, cores_per_batch=4, TPC=2048, D=1024, I=2816,
                 H=8, CH=128):
        self.n_cores = n_cores
        self.cores_per_batch = cores_per_batch
        self.TPC = TPC
        self.D = D
        self.I = I
        self.H = H
        self.CH = CH
        self.ND = D // 128
        self.NI = I // 128
        self.NHB = D // 128
        self.NT = TPC // 128
        self.NCH = TPC // CH
        self.W = min(512, TPC)
        self.NW = TPC // self.W
        self.GS = -(-self.NI // n_cores)
        self.DS = -(-self.NHB // n_cores)


class Emit:
    ENGS = ("sync", "tensor", "vector", "scalar", "gpsimd")

    def __init__(self, nc, stack):
        self.nc = nc
        self.q = {e: [] for e in self.ENGS}
        self.sem = {}
        self.c = {}
        for name in ("dsy", "dgp", "mm", "vv", "ss", "gg", "cc"):
            self.sem[name] = stack.enter_context(nc.semaphore(name))
            self.c[name] = 0

    def _emit(self, eng, fn, inc=None, inck=1, wait=()):
        if inc is not None:
            self.c[inc] += inck
        wait = tuple((n, v) for (n, v) in wait if v > 0)
        sem = self.sem

        def body(e, fn=fn, inc=inc, inck=inck, wait=wait):
            for (n, v) in wait:
                e.wait_ge(sem[n], v)
            instr = fn(e)
            if inc is not None:
                instr.then_inc(sem[inc], inck)
        self.q[eng].append(body)
        return self.c[inc] if inc is not None else None

    def sy(self, fn, wait=()):
        w = [("dsy", self.c["dsy"])] + list(wait)
        return self._emit("sync", fn, "dsy", 16, w)

    def gd(self, fn, wait=()):
        w = [("dgp", self.c["dgp"])] + list(wait)
        return self._emit("gpsimd", fn, "dgp", 16, w)

    def t(self, fn, wait=()):
        return self._emit("tensor", fn, "mm", 1, wait)

    def v(self, fn, wait=()):
        return self._emit("vector", fn, "vv", 1, wait)

    def s(self, fn, wait=()):
        return self._emit("scalar", fn, "ss", 1, wait)

    def g(self, fn, wait=()):
        return self._emit("gpsimd", fn, "gg", 1, wait)

    def cc(self, fn, wait=()):
        return self._emit("gpsimd", fn, "cc", 1, wait)

    def vd(self):
        self.q["vector"].append(lambda v: v.drain())

    def sd(self):
        self.q["scalar"].append(lambda s: s.drain())

    def gdr(self):
        self.q["gpsimd"].append(lambda g: g.drain())

    def now(self, name):
        return self.c[name]

    def run(self):
        with self.nc.Block() as b:
            for eng in self.ENGS:
                fns = self.q[eng]
                if not fns:
                    continue
                getattr(b, eng)(lambda e, fns=fns: [f(e) for f in fns])


def build(cfg: Cfg, dbg=False, stop_stage=99, n_heads=None, do_relay=True,
          do_corr=True):
    nc = bass.Bass("TRN2", target_bir_lowering=False, debug=False)
    st = ExitStack()
    C = cfg
    TPC, D, I, H = C.TPC, C.D, C.I, C.H
    ND, NI, NT, NCH, NW, W, NB = C.ND, C.NI, C.NT, C.NCH, C.NW, C.W, C.NHB

    dx = nc.dram_tensor("xt", [D, TPC], F32, kind="ExternalInput").ap()
    dwq = nc.dram_tensor("wq", [D, D], F32, kind="ExternalInput").ap()
    dwk = nc.dram_tensor("wk", [D, D], F32, kind="ExternalInput").ap()
    dwv = nc.dram_tensor("wv", [D, D], F32, kind="ExternalInput").ap()
    dwr = nc.dram_tensor("wr", [D, D], F32, kind="ExternalInput").ap()
    dwo = nc.dram_tensor("wo", [D, D], F32, kind="ExternalInput").ap()
    dwa = nc.dram_tensor("wa", [D, H], F32, kind="ExternalInput").ap()
    dalog = nc.dram_tensor("alog", [H, 1], F32, kind="ExternalInput").ap()
    ddtb = nc.dram_tensor("dtb", [H, 1], F32, kind="ExternalInput").ap()
    dqnw = nc.dram_tensor("qnw", [1, 128], F32, kind="ExternalInput").ap()
    dknw = nc.dram_tensor("knw", [1, 128], F32, kind="ExternalInput").ap()
    dgnw = nc.dram_tensor("gnw", [128, ND], F32, kind="ExternalInput").ap()
    dclipx = nc.dram_tensor("clipx", [1, 1], F32, kind="ExternalInput").ap()
    dclipd = nc.dram_tensor("clipd", [1, 1], F32, kind="ExternalInput").ap()
    dwgm = nc.dram_tensor("wgm", [D, C.GS * 128], F32,
                          kind="ExternalInput").ap()
    dwum = nc.dram_tensor("wum", [D, C.GS * 128], F32,
                          kind="ExternalInput").ap()
    dwdm = nc.dram_tensor("wdm", [I, C.DS * 128], F32,
                          kind="ExternalInput").ap()
    drmask = nc.dram_tensor("rmask", [1, C.n_cores], F32, kind="ExternalInput").ap()
    dxout = nc.dram_tensor("xout", [D, TPC], F32, kind="ExternalOutput").ap()

    GS, DS, R = C.GS, C.DS, C.n_cores
    PAYR = 2 * GS * D + DS * I
    NSC = 2 * GS + DS
    dsci = nc.dram_tensor("scin", [128, NSC], F32)
    dsco = nc.dram_tensor("scout", [R * 128, NSC], F32)
    dwin = nc.dram_tensor("wqin", [PAYR, 128], BF16)
    dgath = nc.dram_tensor("wqgath", [R * PAYR, 128], BF16)
    dcd = nc.dram_tensor("cache_d", [NB * I, 128], BF16)
    dcg = nc.dram_tensor("cache_g", [NI * D, 128], BF16)
    dcu = nc.dram_tensor("cache_u", [NI * D, 128], BF16)
    drin = nc.dram_tensor("rin", [129, 128], F32)
    drout = nc.dram_tensor("rout", [C.n_cores * 129, 128], F32)
    dgin = nc.dram_tensor("gin", [1, 1], F32)
    dgout = nc.dram_tensor("gout", [1, 1], F32)
    taps = {}
    if dbg:
        def mktap(name, shape, dt):
            taps[name] = nc.dram_tensor("tap_" + name, shape, dt,
                                        kind="ExternalOutput").ap()
        mktap("intx", [128, cfg.ND * TPC], BF16)
        mktap("sclx", [1, 2], F32)
        mktap("h", [128, cfg.NI * TPC], FP16)
        mktap("sclh", [1, 2], F32)
        mktap("x1", [128, cfg.ND * TPC], F32)
        mktap("xln2", [128, cfg.ND * TPC], BF16)
        mktap("fhead", [H, TPC], F32)
        mktap("f0", [128, cfg.NT * 128], FP16)
        mktap("q0", [128, TPC], BF16)
        mktap("wp0", [128, cfg.ND * 128], BF16)
        mktap("qs0", [128, TPC], F32)
        mktap("rsq0", [1, TPC], F32)
        mktap("k0", [128, TPC], BF16)
        mktap("v0", [128, cfg.NT * 128], BF16)
        mktap("eg0", [128, TPC], BF16)
        mktap("eG0", [128, TPC], BF16)
        mktap("S0", [128, 128], F32)
        mktap("At0", [128, 1], F32)
        mktap("oloc", [128, H * TPC], BF16)
        mktap("o", [128, H * TPC], BF16)
        mktap("x2", [128, cfg.ND * TPC], F32)

    def tap(name, src_ap, waits):
        if not dbg or name not in taps:
            return
        tv = em.sy(lambda e: e.dma_start(out=taps[name], in_=src_ap),
                   wait=waits)
        em.v(lambda e: e.tensor_copy(gmx[:], gmx[:]), wait=[("dsy", tv)])
        em.vd()

    em = Emit(nc, st)
    sb = lambda name, shape, dt: st.enter_context(nc.sbuf_tensor(name, shape, dt))

    open_stacks = []

    def _finish(extra_wait=()):
        for s_ in reversed(open_stacks):
            s_.close()
        del open_stacks[:]
        w = list(extra_wait) + [(n, em.now(n)) for n in
                                ("vv", "ss", "mm", "gg", "cc", "dgp")]
        stf = em.sy(lambda e: e.dma_start(
            out=dxout.rearrange("(c p) t -> p c t", p=128),
            in_=x_sb[:].rearrange("p (c t) -> p c t", c=ND)), wait=w)
        em._emit("sync", lambda e: e.nop(), None, 0, [("dsy", stf)])
        em.run()
        st.close()
        return nc

    I2w = (I + 1) // 2
    MBLK = max(ND, NI) * 128
    HSIZE = max(NI * TPC,
                2 * (ND * TPC + 3 * TPC),
                2 * (ND * I2w + 3 * max(I2w, D // 2)) + 2 * MBLK,
                (cfg.H + 11) * TPC + max(ND * 128, TPC) + 2 * TPC,
                max((cfg.H + 11) * TPC + max(ND * 128, TPC),
                    cfg.H * TPC + 3 * ND * D) + 2 * TPC)
    x_sb = sb("x", [128, ND * TPC], F32)
    arH = sb("arH", [128, HSIZE], FP16)
    arB = sb("arB", [128, ND * TPC], BF16)
    ARW = max(TPC, 4 * W)
    ar1 = sb("ar1", [128, ARW], F32)
    ar2 = sb("ar2", [128, ARW], F32)
    ident_bf = sb("ident_bf", [128, 128], BF16)
    ident_f = sb("ident_f", [128, 128], F32)
    triu_f = sb("triu_f", [128, 128], F32)
    triu_h = sb("triu_h", [128, 128], FP16)
    ones_col_f = sb("ones_col_f", [128, 1], F32)
    ones_col_b = sb("ones_col_b", [128, 1], BF16)
    qnw_sb = sb("qnw_s", [1, 128], F32)
    knw_sb = sb("knw_s", [1, 128], F32)
    gnw_sb = sb("gnw_s", [128, ND], F32)
    alog_sb = sb("alog_s", [H, 1], F32)
    dtb_sb = sb("dtb_s", [H, 1], F32)
    negea_sb = sb("negea", [H, 1], F32)
    clip_sb = sb("clip", [1, 2], F32)
    rmask_sb = sb("rmask_s", [1, C.n_cores], F32)
    mrep_sb = sb("mrep", [128, C.n_cores], F32)
    sc_g = sb("sc_g", [128, NI], F32)
    sc_u = sb("sc_u", [128, NI], F32)
    sc_d = sb("sc_d", [128, NB], F32)
    cmbg = sb("cmbg", [128, NI], F32)
    cmbu = sb("cmbu", [128, NI], F32)
    cmb8 = sb("cmb8", [128, NB], F32)
    rq_x = sb("rq_x", [128, 1], F32)
    rq_h = sb("rq_h", [128, 1], F32)
    sclx = sb("sclx", [128, 1], F32)
    sclh = sb("sclh", [128, 1], F32)
    gmx = sb("gmx", [1, 1], F32)
    acol = sb("acol", [128, 1], F32)
    bcol = sb("bcol", [128, 1], F32)
    amx8 = sb("amx8", [128, max(ND, 1)], F32)
    habs = sb("habs", [128, NI * NW], F32)
    fheadT = sb("fheadT", [128, NT * H], F32)
    gprev = sb("gprev", [128, NCH], F32)
    egp = sb("egp", [128, NCH], F32)
    S_f = [sb(f"Sf{i}", [128, 128], F32) for i in range(2)]
    striu_s = sb("striu", [NCH, NCH], F32)
    glt = sb("glt", [NCH, 128], F32)
    egc_f = sb("egcf", [128, NCH], F32)
    _sbf = sb("Sbf0", [128, 128], BF16)
    S_bf = [_sbf, _sbf]
    Aq_sb = sb("Aq", [128, 1], F32)
    Atot = sb("Atot", [128, 1], F32)
    _kh = sb("khT0", [128, 128], BF16)
    khT = [_kh, _kh]
    _pm = sb("pm0", [128, 128], BF16)
    pm_sb = [_pm, _pm]


    PS = [st.enter_context(nc.psum_tensor(f"ps{i}", [128, 512], F32)) for i in range(6)]
    ones_row_f = triu_f[0:1, 0:128]
    arow = ar1[0:1, 0:128]

    ceps = sb("ceps", [128, 1], F32)
    cone = sb("cone", [128, 1], F32)
    cez = sb("cez", [128, 1], F32)
    mc1_sb = sb("mc1", [128, C.n_cores], F32)
    nc.const_aps.register(F32, EPS_LN, ceps[:, 0:1])
    nc.const_aps.register(F32, 1.0, cone[:, 0:1])
    nc.const_aps.register(F32, EPS_Z, cez[:, 0:1])
    cmagic = sb("cmagic", [128, 1], F32)
    cnmagic = sb("cnmagic", [128, 1], F32)
    nc.const_aps.register(F32, MAGIC, cmagic[:, 0:1])
    nc.const_aps.register(F32, -MAGIC, cnmagic[:, 0:1])

    Hb = arH[:, :].bitcast(BF16)
    Hf = arH[:, :].bitcast(F32)
    HF_TOT = HSIZE // 2

    def hfv(o, n):
        assert o + n <= HF_TOT, (o, n)
        return Hf[:, o:o + n]

    def hbv(o, n):
        assert o + n <= HSIZE, (o, n)
        return Hb[:, o:o + n]

    xln = hfv(0, ND * TPC)
    XT = ND * TPC
    row_a = hfv(XT, TPC)[0:1, :]
    row_b = hfv(XT + TPC, TPC)[0:1, :]
    row_c = hfv(XT + 2 * TPC, TPC)[0:1, :]
    I2 = (I + 1) // 2
    wbuf = hfv(0, ND * I2)
    WAC = max(I2, D // 2)
    wacc = hfv(ND * I2, WAC)
    wtmp = [hfv(ND * I2 + WAC + i * WAC, WAC) for i in range(2)]
    wblk = [hbv(2 * (ND * I2 + 3 * WAC) + i * max(ND, NI) * 128,
                max(ND, NI) * 128) for i in range(2)]
    OFF_O = 0
    o_sb = hbv(OFF_O, H * TPC)
    q_pp = [hbv(H * TPC + i * TPC, TPC) for i in range(2)]
    eG_pp = [hbv((H + 2) * TPC + i * TPC, TPC) for i in range(2)]
    k_bf = hbv((H + 4) * TPC, TPC)
    v_tm = hbv((H + 5) * TPC, TPC)
    eg_sb = hbv((H + 6) * TPC, TPC)
    em_sb = hbv((H + 7) * TPC, TPC)
    kd_sb = hbv((H + 8) * TPC, TPC)
    khat_sb = hbv((H + 9) * TPC, TPC)
    stage = hbv((H + 10) * TPC, TPC)
    f_h = arH[:, (H + 10) * TPC:(H + 11) * TPC]
    wproj = hbv((H + 11) * TPC, max(ND * 128, H * ND))
    wproj_B = hbv((H + 11) * TPC + ND * 128, ND * 128) if TPC >= 2 * ND * 128 \
        else wproj
    qg_scr = hbv((H + 11) * TPC, TPC)
    ATT_END_B = (H + 11) * TPC + max(ND * 128, TPC)
    ROW_B = max(ATT_END_B, H * TPC + 3 * ND * D)
    row_att = hfv(ROW_B // 2, TPC)[0:1, :]
    assert ROW_B // 2 + TPC <= HF_TOT
    fhead = hfv(0, TPC)[0:H, :]
    wo_bf = hbv(H * TPC, ND * D)
    wo_stage = hfv((H * TPC + ND * D) // 2, ND * D)

    sq2 = [ar1[:, 0:W], ar1[:, W:2 * W]]
    rep_a = ar1[:, 0:TPC]
    rep_b = ar2[:, 0:TPC]
    scr_u = ar1[:, 0:TPC]
    scr_m = ar2[:, 0:TPC]
    wg_pp = [ar1[:, 0:W].bitcast(BF16), ar1[:, W:2 * W].bitcast(BF16)]
    wu_pp = [ar1[:, 2 * W:3 * W].bitcast(BF16), ar1[:, 3 * W:4 * W].bitcast(BF16)]
    sg_pp = [ar2[:, 0:W], ar2[:, W:2 * W]]
    wd_pp = [ar2[:, 2 * W:4 * W].bitcast(BF16)[:, 0:NB * 128],
             ar2[:, 2 * W:4 * W].bitcast(BF16)[:, NB * 128:2 * NB * 128]]
    hb_pp = [ar1[:, 0:W // 2].bitcast(BF16), ar1[:, W // 2:W].bitcast(BF16)]
    hq_u = ar1[:, 2 * W:3 * W]
    hq_m = ar1[:, 3 * W:4 * W]
    g_scr = ar2[:, 0:TPC]
    fdm = ar1[:, 0:TPC]
    qs_f = ar1[:, 0:TPC]
    kkt = ar2[:, 0:TPC]
    rt_pp = [ar2[:, 0:128], ar2[:, 128:256]]
    rko_pp = [ar2[:, 256:384], ar2[:, 384:512]]
    scorz_pp = [ar2[:, 512:640], ar2[:, 640:768]]
    m8 = ar2[:, 776:808]
    zall = ar2[:, 808:824]
    zf = ar2[:, 824:840]
    sq_pp = [ar2[:, 1184:1440].bitcast(BF16)[:, 0:W],
             ar2[:, 1440:1696].bitcast(BF16)[:, 0:W]]
    rep_g = ar1[:, 0:TPC]
    khT_full = ar2[:, 0:TPC // 2].bitcast(BF16)
    NQ = C.n_cores
    Sq_all = ar2[:, 0:NQ * 128]
    wS_scr = ar2[:, 1024:1024 + NQ * 128]
    fold1 = ar2[:, 0:512]
    fold2 = ar2[:, 512:768]
    Sin_f = ar2[:, 768:896]
    Sin_bf = ar2[:, 896:960].bitcast(BF16)[:, 0:128]
    Aq8 = ar2[:, 960:960 + NQ]
    g1c = ar2[:, 968:968 + NQ]
    e8c = ar2[:, 976:976 + NQ]
    w8c = ar2[:, 984:984 + NQ]
    a8_sb = ar1[0:NQ, 0:128]

    assert NB * 128 * 2 <= 2 * W * 2

    em.sy(lambda e: e.dma_start(out=qnw_sb[:], in_=dqnw[:]))
    em.sy(lambda e: e.dma_start(out=knw_sb[:], in_=dknw[:]))
    em.sy(lambda e: e.dma_start(out=gnw_sb[:], in_=dgnw[:]))
    ld_ab = em.sy(lambda e: e.dma_start(out=alog_sb[:], in_=dalog[:]))
    em.sy(lambda e: e.dma_start(out=dtb_sb[:], in_=ddtb[:]))
    em.sy(lambda e: e.dma_start(out=clip_sb[:, 0:1], in_=dclipx[:]))
    ld_cl = em.sy(lambda e: e.dma_start(out=clip_sb[:, 1:2], in_=dclipd[:]))
    ld_small = em.sy(lambda e: e.dma_start(out=rmask_sb[:], in_=drmask[:]))
    ld_x = em.sy(lambda e: e.dma_start(
        out=x_sb[:].rearrange("p (c t) -> p c t", c=ND),
        in_=dx.rearrange("(c p) t -> p c t", p=128)))

    em.g(lambda e: e.iota(triu_f[:], [[1, 128]], base=0, channel_multiplier=-1,
                          allow_small_or_imprecise_dtypes=True))
    em.gdr()
    em.g(lambda e: e.tensor_scalar(ident_bf[:], triu_f[:], 0.0, None, ALU.is_equal))
    em.gdr()
    em.g(lambda e: e.tensor_scalar(ident_f[:], triu_f[:], 0.0, None, ALU.is_equal))
    em.gdr()
    em.g(lambda e: e.tensor_scalar(triu_h[:], triu_f[:], 0.0, None, ALU.is_ge))
    em.gdr()
    em.g(lambda e: e.tensor_scalar(triu_f[:], triu_f[:], 0.0, None, ALU.is_ge))
    em.gdr()
    em.g(lambda e: e.tensor_tensor(striu_s[:], triu_f[0:NCH, 0:NCH],
                                   ident_f[0:NCH, 0:NCH], ALU.subtract))
    em.gdr()
    em.g(lambda e: e.memset(ones_col_f[:], 1.0))
    em.gdr()
    em.v(lambda e: e.memset(ceps[:], EPS_LN))
    em.vd()
    em.v(lambda e: e.memset(cone[:], 1.0))
    em.vd()
    em.v(lambda e: e.memset(cez[:], EPS_Z))
    em.vd()
    em.v(lambda e: e.memset(cmagic[:], MAGIC))
    em.vd()
    em.v(lambda e: e.memset(cnmagic[:], -MAGIC))
    em.vd()
    GGK = em.g(lambda e: e.memset(ones_col_b[:], 1.0))
    gg_k = GGK

    sA = em.s(lambda e: e.activation(negea_sb[:], alog_sb[:], ACT.Exp),
              wait=[("dsy", ld_ab)])
    em.v(lambda e: e.tensor_scalar(negea_sb[:], negea_sb[:], -1.0, None, ALU.mult),
         wait=[("ss", sA)])
    ss_clip = em.s(lambda e: e.activation(clip_sb[:], clip_sb[:], ACT.Sigmoid),
                   wait=[("dsy", ld_cl)])
    tmr = em.t(lambda e: e.matmul(PS[5][:, 0:C.n_cores], ones_row_f[:],
                                  rmask_sb[:], start=True, stop=True),
               wait=[("dsy", ld_small), ("gg", gg_k)])
    v_mrep = em.v(lambda e: e.tensor_copy(mrep_sb[:], PS[5][:, 0:C.n_cores]),
                  wait=[("mm", tmr)])
    em.vd()
    v_mrep = em.v(lambda e: e.tensor_scalar(mc1_sb[:], mrep_sb[:], -1.0, 1.0,
                                            ALU.mult, ALU.add))

    def gmax_finish(vv_gmx, clip_idx, rq_out, scl_out):
        d1 = em.gd(lambda e: e.dma_start(out=dgin[:], in_=gmx[:]),
                   wait=[("vv", vv_gmx)])
        cc1 = em.cc(lambda e: e.collective_compute(
            "AllReduce", ALU.max, replica_groups=[list(range(C.n_cores))],
            ins=[dgin.ap()], outs=[dgout.ap()]), wait=[("dgp", d1)])
        d2 = em.gd(lambda e: e.dma_start(out=gmx[:], in_=dgout[:]),
                   wait=[("cc", cc1)])
        vC = em.v(lambda e: e.scalar_tensor_tensor(
            gmx[:], gmx[:], 1.0 / 127.0,
            clip_sb[:, clip_idx:clip_idx + 1], ALU.mult, ALU.mult),
            wait=[("dgp", d2), ("ss", ss_clip)])
        tB = em.t(lambda e: e.matmul(PS[4][:, 0:1], ones_row_f[:], gmx[:],
                                     start=True, stop=True),
                  wait=[("vv", vC), ("gg", GGK)])
        em.v(lambda e: e.tensor_copy(scl_out[:], PS[4][:, 0:1]),
             wait=[("mm", tB)])
        em.vd()
        return em.v(lambda e: e.reciprocal(rq_out[:], scl_out[:]))

    def global_scale(ready_vv, clip_idx, rq_out, scl_out):
        tA = em.t(lambda e: e.transpose(PS[4][0:1, 0:128], acol[:], ident_f[:]),
                  wait=[("vv", ready_vv), ("gg", GGK)])
        vB = em.v(lambda e: e.reduce_max(gmx[:], PS[4][0:1, 0:128], AX.X),
                  wait=[("mm", tA)])
        return gmax_finish(vB, clip_idx, rq_out, scl_out)

    def ln_stats(wait_pre):
        wait_pre = list(wait_pre) + [("gg", GGK)]
        for nw in range(NW):
            bank_free = em.now("vv")
            for c in range(ND):
                xsl = x_sb[:, c * TPC + nw * W: c * TPC + nw * W + W]
                ssq = em.s(lambda e, xsl=xsl, c=c: e.activation(
                    sq2[c % 2], xsl, ACT.Square),
                    wait=(wait_pre + [("mm", em.now("mm"))]) if c < 2 else
                    [("mm", em.now("mm"))])
                em.t(lambda e, xsl=xsl, c=c: e.matmul(
                    PS[0][0:1, 0:W], ones_col_f[:], xsl,
                    start=(c == 0), stop=(c == ND - 1)),
                    wait=(wait_pre + [("vv", bank_free)]) if c == 0 else [])
                em.t(lambda e, c=c: e.matmul(
                    PS[1][0:1, 0:W], ones_col_f[:], sq2[c % 2],
                    start=(c == 0), stop=(c == ND - 1)),
                    wait=[("ss", ssq)] + ([("vv", bank_free)] if c == 0 else []))
            mmv = em.now("mm")
            em.v(lambda e, nw=nw: e.tensor_copy(
                row_a[:, nw * W:(nw + 1) * W], PS[0][0:1, 0:W]),
                wait=[("mm", mmv)])
            em.v(lambda e, nw=nw: e.tensor_copy(
                row_b[:, nw * W:(nw + 1) * W], PS[1][0:1, 0:W]),
                wait=[("mm", mmv)])
        em.vd()
        em.v(lambda e: e.tensor_scalar(row_a[:], row_a[:], 1.0 / D, None, ALU.mult))
        em.vd()
        em.v(lambda e: e.tensor_scalar(row_b[:], row_b[:], 1.0 / D, None, ALU.mult))
        em.vd()
        em.v(lambda e: e.tensor_mul(row_c[:], row_a[:], row_a[:]))
        em.vd()
        vv1 = em.v(lambda e: e.tensor_sub(row_b[:], row_b[:], row_c[:]))
        em.s(lambda e: e.activation(row_b[:], row_b[:], ACT.Ln,
                                    bias=EPS_LN), wait=[("vv", vv1)])
        em.sd()
        ss2 = em.s(lambda e: e.activation(row_b[:], row_b[:], ACT.Exp,
                                          scale=-0.5))
        return [("ss", ss2)]

    def replicate_row(row, psum_idx, out_sb, wait_pre):
        last = None
        for nw in range(NW):
            tm = em.t(lambda e, nw=nw: e.matmul(
                PS[psum_idx][:, 0:W], ones_row_f[:],
                row[:, nw * W:(nw + 1) * W], start=True, stop=True),
                wait=list(wait_pre) + [("vv", em.now("vv")), ("gg", GGK)])
            last = em.v(lambda e, nw=nw: e.tensor_copy(
                out_sb[:, nw * W:(nw + 1) * W], PS[psum_idx][:, 0:W]),
                wait=[("mm", tm)])
        return last

    def quant_chain(dst, src, rq_col, u_scr, m_scr, wait):
        em.v(lambda e: e.tensor_scalar(u_scr, src, rq_col[:], MAGIC,
                                       ALU.mult, ALU.add), wait=wait)
        em.vd()
        em.v(lambda e: e.tensor_scalar(m_scr, u_scr, MAGIC, -128.0,
                                       ALU.subtract, ALU.max))
        em.vd()
        return em.v(lambda e: e.tensor_scalar(dst, m_scr, 127.0, None,
                                              ALU.min))

    def ln_quant(wait_pre):
        rs_w = ln_stats(wait_pre)
        vv_mr = em.v(lambda e: e.tensor_mul(row_c[:], row_a[:], row_b[:]),
                     wait=rs_w)
        replicate_row(row_b, 2, rep_a, [("vv", vv_mr)])
        vb = replicate_row(row_c, 3, rep_b, [("vv", vv_mr)])
        for c in range(ND):
            em.v(lambda e, c=c: e.tensor_mul(
                xln[:, c * TPC:(c + 1) * TPC], x_sb[:, c * TPC:(c + 1) * TPC],
                rep_a[:]), wait=[("vv", vb)] if c == 0 else [])
            em.vd()
            em.v(lambda e, c=c: e.tensor_sub(
                xln[:, c * TPC:(c + 1) * TPC], xln[:, c * TPC:(c + 1) * TPC],
                rep_b[:]))
            em.vd()
            em.v(lambda e, c=c: e.tensor_reduce(
                amx8[:, c:c + 1], xln[:, c * TPC:(c + 1) * TPC], AX.X,
                ALU.max, apply_absolute_value=True))
            em.vd()
        vv_am = em.v(lambda e: e.tensor_reduce(
            acol[:], amx8[:, 0:ND], AX.X, ALU.max, apply_absolute_value=True))
        vq = global_scale(vv_am, 0, rq_x, sclx)
        last = None
        for c in range(ND):
            last = quant_chain(arB[:, c * TPC:(c + 1) * TPC],
                               xln[:, c * TPC:(c + 1) * TPC],
                               rq_x, scr_u, scr_m,
                               [("vv", vq)] if c == 0 else [])
        return last

    scmy = sb("scmy", [128, NSC], F32)
    scstg = sb("scstg", [128, R * NSC], F32)

    def wquant_my(src_my, KD, NS, pay_off, sc_off):
        NK = KD // 128
        prev_tr = em.now("mm")
        for s in range(NS):
            dld = em.sy(lambda e, s=s, NK=NK: e.dma_start(
                out=wbuf[:, 0:NK * 128].rearrange("p (c w) -> p c w", c=NK),
                in_=src_my[:, s * 128:(s + 1) * 128]
                .rearrange("(c p) w -> p c w", p=128)),
                wait=[("vv", em.now("vv"))])
            vsteps = {}
            vv0 = None
            for c in range(NK):
                sA = em.s(lambda e, c=c: e.activation(
                    wtmp[c % 2][:, 0:128], wbuf[:, c * 128:(c + 1) * 128],
                    ACT.Abs),
                    wait=[("dsy", dld), ("vv", vsteps.get(c - 2, em.now("vv")))])
                if c == 0:
                    vv0 = em.v(lambda e: e.tensor_copy(
                        wacc[:, 0:128], wtmp[0][:, 0:128]), wait=[("ss", sA)])
                else:
                    em.vd()
                    vv0 = em.v(lambda e, c=c: e.tensor_tensor(
                        wacc[:, 0:128], wacc[:, 0:128], wtmp[c % 2][:, 0:128],
                        ALU.max), wait=[("ss", sA)])
                vsteps[c] = vv0
            sg = sc_off + s
            tj = em.t(lambda e: e.transpose(
                PS[4][:, 0:128], wacc[:, 0:128], ident_f[:]),
                wait=[("vv", em.now("vv")), ("gg", GGK)])
            em.v(lambda e, sg=sg: e.tensor_reduce(
                scmy[:, sg:sg + 1], PS[4][:, 0:128], AX.X, ALU.max),
                wait=[("mm", tj)])
            em.vd()
            vts = em.v(lambda e, sg=sg: e.tensor_scalar(
                scmy[:, sg:sg + 1], scmy[:, sg:sg + 1],
                1.0 / 127.0, None, ALU.mult))
            em.vd()
            em.s(lambda e, sg=sg: e.activation(
                acol[:], scmy[:, sg:sg + 1], ACT.Ln),
                wait=[("vv", vts), ("mm", prev_tr)])
            em.sd()
            sex = em.s(lambda e: e.activation(acol[:], acol[:], ACT.Exp,
                                              scale=-1.0))
            tr = em.t(lambda e: e.transpose(PS[3][0:1, 0:128], acol[:],
                                            ident_f[:]),
                      wait=[("ss", sex), ("gg", GGK)])
            prev_tr = tr
            vrow = em.v(lambda e: e.tensor_copy(arow[:], PS[3][0:1, 0:128]),
                        wait=[("mm", tr)])
            trep = em.t(lambda e: e.matmul(PS[4][:, 0:128], ones_row_f[:],
                                           arow[:], start=True, stop=True),
                        wait=[("vv", vrow)])
            bsl = wbuf[:, 0:NK * 128].rearrange("p (c w) -> p c w", c=NK)
            repb = PS[4][:, 0:128][:, None, :].broadcast_to([128, NK, 128])
            dst3 = wblk[s % 2][:, 0:NK * 128].rearrange("p (c w) -> p c w",
                                                        c=NK)
            nhalf = 2 if NK * 128 > TPC else 1
            kk = NK // nhalf
            vq = None
            for hh in range(nhalf):
                b_h = bsl[:, hh * kk:(hh + 1) * kk, :]
                r_h = repb[:, hh * kk:(hh + 1) * kk, :]
                d_h = dst3[:, hh * kk:(hh + 1) * kk, :]
                u_h = scr_u[:, 0:kk * 128].rearrange("p (c w) -> p c w", c=kk)
                em.v(lambda e, b_h=b_h, r_h=r_h, u_h=u_h: e.tensor_tensor(
                    u_h, b_h, r_h, ALU.mult),
                    wait=[("mm", trep), ("dgp", em.now("dgp"))])
                em.vd()
                vq = em.v(lambda e, u_h=u_h, d_h=d_h: e.tensor_scalar(
                    d_h, u_h, MAGIC, MAGIC, ALU.add, ALU.subtract))
                em.vd()
            em.gd(lambda e, s=s, NK=NK, pay_off=pay_off: e.dma_start(
                out=dwin[pay_off + s * NK * 128:pay_off + (s + 1) * NK * 128, :]
                .rearrange("(c p) w -> p c w", p=128),
                in_=wblk[s % 2][:, 0:NK * 128]
                .rearrange("p (c w) -> p c w", c=NK)), wait=[("vv", vq)])

    wquant_my(dwgm, D, GS, 0, 0)
    wquant_my(dwum, D, GS, GS * D, GS)
    wquant_my(dwdm, I, DS, 2 * GS * D, 2 * GS)
    wsc = em.gd(lambda e: e.dma_start(out=dsci[:], in_=scmy[:]),
                wait=[("vv", em.now("vv"))])
    ag_w = em.cc(lambda e: e.collective_compute(
        "AllGather", ALU.bypass, replica_groups=[list(range(R))],
        ins=[dwin.ap()], outs=[dgath.ap()]),
        wait=[("dgp", em.now("dgp"))])
    ag_s = em.cc(lambda e: e.collective_compute(
        "AllGather", ALU.bypass, replica_groups=[list(range(R))],
        ins=[dsci.ap()], outs=[dsco.ap()]))
    rdsc = None
    for r in range(R):
        rdsc = em.gd(lambda e, r=r: e.dma_start(
            out=scstg[:, r * NSC:(r + 1) * NSC],
            in_=dsco[r * 128:(r + 1) * 128, :]),
            wait=[("cc", ag_s)] if r == 0 else [])
    vv_sc = None
    for r in range(R):
        gw = min(GS, NI - r * GS)
        dw_ = min(DS, NB - r * DS)
        if gw > 0:
            em.v(lambda e, r=r, gw=gw: e.tensor_copy(
                sc_g[:, r * GS:r * GS + gw], scstg[:, r * NSC:r * NSC + gw]),
                wait=[("dgp", rdsc)] if r == 0 else [])
            em.v(lambda e, r=r, gw=gw: e.tensor_copy(
                sc_u[:, r * GS:r * GS + gw],
                scstg[:, r * NSC + GS:r * NSC + GS + gw]))
        if dw_ > 0:
            vv_sc = em.v(lambda e, r=r, dw_=dw_: e.tensor_copy(
                sc_d[:, r * DS:r * DS + dw_],
                scstg[:, r * NSC + 2 * GS:r * NSC + 2 * GS + dw_]))
    em.vd()
    for r in range(R):
        gr = min(GS, NI - r * GS)
        nreal = min(DS, NB - r * DS)
        if gr > 0:
            em.gd(lambda e, r=r, gr=gr: e.dma_start(
                out=dcg[r * GS * D:(r * GS + gr) * D, :],
                in_=dgath[r * PAYR:r * PAYR + gr * D, :]),
                wait=[("cc", ag_w)] if r == 0 else [])
            em.gd(lambda e, r=r, gr=gr: e.dma_start(
                out=dcu[r * GS * D:(r * GS + gr) * D, :],
                in_=dgath[r * PAYR + GS * D:r * PAYR + GS * D + gr * D, :]))
        if nreal > 0:
            em.gd(lambda e, r=r, nreal=nreal: e.dma_start(
                out=dcd[r * DS * I:(r * DS + nreal) * I, :],
                in_=dgath[r * PAYR + 2 * GS * D:
                          r * PAYR + 2 * GS * D + nreal * I, :]))
    wq_done = em.now("dgp")

    vv_intx = ln_quant([("dsy", ld_x), ("vv", em.now("vv"))])
    if stop_stage <= 1:
        return _finish()
    tap("intx", arB[:], [("vv", vv_intx)])
    tap("sclx", clip_sb[:], [("vv", vv_intx)])

    if stop_stage <= 2:
        return _finish()

    def ffn_gateup(wait_intx_vv, px, entry_ss):
        silu_d = {}
        h_d = {}
        gate_done = {}
        up_done = {}
        banksG = [PS[0], PS[1], PS[2], PS[3]]
        banksU = [PS[4], PS[5], px[0], px[1]]
        em.v(lambda e: e.tensor_scalar(cmbg[:], sc_g[:], sclx[:], None,
                                       ALU.mult), wait=[("vv", wait_intx_vv)])
        em.vd()
        vv_cmb = em.v(lambda e: e.tensor_scalar(cmbu[:], sc_u[:], sclx[:],
                                                None, ALU.mult))
        em.vd()
        for j in range(NI):
            wait_w = [("mm", up_done[(j - 2, NW - 1)])] if j >= 2 else \
                [("vv", em.now("vv"))]
            dg = em.sy(lambda e, j=j: e.dma_start(
                out=wg_pp[j % 2][:].rearrange("p (c w) -> p c w", c=ND),
                in_=dcg[j * D:(j + 1) * D, :]
                .rearrange("(c p) w -> p c w", p=128)),
                wait=wait_w + [("dgp", wq_done)])
            du = em.sy(lambda e, j=j: e.dma_start(
                out=wu_pp[j % 2][:].rearrange("p (c w) -> p c w", c=ND),
                in_=dcu[j * D:(j + 1) * D, :]
                .rearrange("(c p) w -> p c w", p=128)))
            for c in range(ND):
                for nw in range(NW):
                    if c == 0:
                        wt = [("dsy", dg), ("vv", wait_intx_vv)]
                        if j >= 1:
                            wt += [("ss", silu_d[(j - 1, nw)])]
                        elif entry_ss:
                            wt += [("ss", entry_ss)]
                    else:
                        wt = []
                    gate_done[(j, nw)] = em.t(
                        lambda e, j=j, nw=nw, c=c: e.matmul(
                            banksG[nw][:, 0:W],
                            wg_pp[j % 2][:, c * 128:(c + 1) * 128],
                            arB[:, c * TPC + nw * W: c * TPC + nw * W + W],
                            start=(c == 0), stop=(c == ND - 1)), wait=wt)
                for nw in range(NW):
                    if c == 0:
                        wt = [("dsy", du)]
                        if j >= 1:
                            wt += [("vv", h_d[(j - 1, nw)])]
                        elif entry_ss:
                            wt += [("ss", entry_ss)]
                    else:
                        wt = []
                    up_done[(j, nw)] = em.t(
                        lambda e, j=j, nw=nw, c=c: e.matmul(
                            banksU[nw][:, 0:W],
                            wu_pp[j % 2][:, c * 128:(c + 1) * 128],
                            arB[:, c * TPC + nw * W: c * TPC + nw * W + W],
                            start=(c == 0), stop=(c == ND - 1)), wait=wt)
            for nw in range(NW):
                gi = j * NW + nw
                if gi >= 2:
                    sg_war = [("vv", h_d[divmod(gi - 2, NW)])]
                else:
                    sg_war = []
                ssx = em.s(lambda e, j=j, nw=nw, gi=gi: e.activation(
                    sg_pp[gi % 2], banksG[nw][:, 0:W], ACT.Silu,
                    scale=cmbg[:, j:j + 1]),
                    wait=[("mm", gate_done[(j, nw)]), ("vv", vv_cmb)] + sg_war)
                silu_d[(j, nw)] = ssx
                hv = em.v(lambda e, j=j, nw=nw, gi=gi: e.scalar_tensor_tensor(
                    arH[:, j * TPC + nw * W: j * TPC + nw * W + W],
                    sg_pp[gi % 2], cmbu[:, j:j + 1], banksU[nw][:, 0:W],
                    ALU.mult, ALU.mult),
                    wait=[("ss", ssx), ("mm", up_done[(j, nw)])])
                h_d[(j, nw)] = hv
                em.g(lambda e, j=j, nw=nw: e.tensor_reduce(
                    habs[0:1, (j * NW + nw):(j * NW + nw) + 1],
                    arH[:, j * TPC + nw * W: j * TPC + nw * W + W],
                    AX.XYZWC, ALU.max, apply_absolute_value=True),
                    wait=[("vv", hv)])
        em.gdr()
        gg_h = em.now("gg")
        vqh_v = em.v(lambda e: e.tensor_reduce(
            gmx[:], habs[0:1, 0:NI * NW], AX.X, ALU.max,
            apply_absolute_value=True), wait=[("gg", gg_h)])
        vqh = gmax_finish(vqh_v, 1, rq_h, sclh)
        return em.v(lambda e: e.tensor_scalar(
            cmb8[:], sc_d[:], sclh[:], 0.5, ALU.mult, ALU.mult),
            wait=[("vv", vqh)])

    def ffn_down(vcmb, px):
        banks = PS + px
        assert NB <= len(banks)
        prev_consumed = {}
        for nw in range(NW):
            epi_prev = em.now("vv")
            consumed = {}
            for i in range(NI):
                buf_war = consumed.get(i - 2, prev_consumed.get(NI - 2 + i, 0))
                dd = em.sy(lambda e, i=i: e.dma_start(
                    out=wd_pp[i % 2][:].rearrange("p (b w) -> p b w", b=NB),
                    in_=dcd.rearrange("(b k p) w -> b k p w", b=NB, p=128)
                    [:, i, :, :].rearrange("b p w -> p b w")),
                    wait=[("mm", buf_war), ("dgp", wq_done)])
                qv = quant_chain(
                    hb_pp[i % 2][:],
                    arH[:, i * TPC + nw * W: i * TPC + nw * W + W],
                    rq_h, hq_u, hq_m,
                    [("vv", vcmb), ("mm", buf_war),
                     ("vv", epi_prev if i == 0 else 0)])
                last = None
                for b in range(NB):
                    last = em.t(lambda e, i=i, b=b: e.matmul(
                        banks[b][:, 0:W], wd_pp[i % 2][:, b * 128:(b + 1) * 128],
                        hb_pp[i % 2][:], start=(i == 0), stop=(i == NI - 1)),
                        wait=([("dsy", dd), ("vv", qv)] +
                              ([("vv", epi_prev)] if i == 0 else []))
                        if b == 0 else [])
                consumed[i] = last
            prev_consumed = consumed
            mm_all = em.now("mm")
            for b in range(NB):
                em.v(lambda e, b=b, nw=nw: e.scalar_tensor_tensor(
                    x_sb[:, b * TPC + nw * W: b * TPC + nw * W + W],
                    banks[b][:, 0:W], cmb8[:, b:b + 1],
                    x_sb[:, b * TPC + nw * W: b * TPC + nw * W + W],
                    ALU.mult, ALU.add), wait=[("mm", mm_all)])
        return em.now("vv")

    psA = ExitStack()
    pxA = [psA.enter_context(nc.psum_tensor(f"pxa{i}", [128, 512], F32))
           for i in range(2)]
    vcmb1 = ffn_gateup(vv_intx, pxA, 0)
    tap("h", arH[:, 0:NI * TPC], [("vv", em.now("vv")), ("gg", em.now("gg"))])
    tap("sclh", clip_sb[:], [("vv", em.now("vv"))])
    vv_x1 = ffn_down(vcmb1, pxA)
    psA.close()
    psB = ExitStack()
    open_stacks.append(psB)
    PB = psB.enter_context(nc.psum_tensor("pb0", [128, 1024], BF16))
    pvx = psB.enter_context(nc.psum_tensor("pvx", [128, 512], F32))

    if stop_stage <= 3:
        return _finish([("vv", vv_x1)])
    tap("x1", x_sb[:], [("vv", vv_x1)])
    rs2_w = ln_stats([("vv", vv_x1)])
    vv_mr2 = em.v(lambda e: e.tensor_mul(row_c[:], row_a[:], row_b[:]),
                  wait=rs2_w)
    replicate_row(row_b, 2, rep_a, [("vv", vv_mr2)])
    vb2 = replicate_row(row_c, 3, rep_b, [("vv", vv_mr2)])
    vv_xln2 = None
    for c in range(ND):
        em.v(lambda e, c=c: e.tensor_mul(
            scr_u[:, 0:TPC] if False else xln[:, c * TPC:(c + 1) * TPC],
            x_sb[:, c * TPC:(c + 1) * TPC], rep_a[:]),
            wait=[("vv", vb2)] if c == 0 else [])
        em.vd()
        vv_xln2 = em.v(lambda e, c=c: e.tensor_sub(
            arB[:, c * TPC:(c + 1) * TPC], xln[:, c * TPC:(c + 1) * TPC],
            rep_b[:]))
        em.vd()

    tap("xln2", arB[:], [("vv", vv_xln2)])
    wa_sb = wproj[:, 0:ND * H]
    dwa_ld = em.gd(lambda e: e.dma_start(
        out=wa_sb[:].rearrange("p (c o) -> p c o", c=ND),
        in_=dwa.rearrange("(c p) o -> p c o", p=128)),
        wait=[("vv", em.now("vv")), ("mm", em.now("mm"))])
    vfh2 = None
    for nw in range(NW):
        tz = None
        for c in range(ND):
            tz = em.t(lambda e, c=c, nw=nw: e.matmul(
                PS[0][0:H, 0:W], wa_sb[:, c * H:(c + 1) * H],
                arB[:, c * TPC + nw * W: c * TPC + nw * W + W],
                start=(c == 0), stop=(c == ND - 1)),
                wait=[("dgp", dwa_ld), ("vv", vv_xln2),
                      ("ss", em.now("ss")), ("vv", em.now("vv"))]
                if c == 0 else [])
        sa = em.s(lambda e, nw=nw: e.activation(
            sq2[0][0:H, :], PS[0][0:H, 0:W], ACT.Abs, bias=dtb_sb[:]),
            wait=[("mm", tz)])
        em.sd()
        em.s(lambda e: e.activation(sq2[0][0:H, :], sq2[0][0:H, :], ACT.Exp,
                                    scale=-1.0))
        em.sd()
        em.s(lambda e: e.activation(sq2[0][0:H, :], sq2[0][0:H, :], ACT.Ln,
                                    bias=1.0))
        em.sd()
        sr = em.s(lambda e, nw=nw: e.activation(
            sq2[1][0:H, :], PS[0][0:H, 0:W], ACT.Relu, bias=dtb_sb[:]))
        em.v(lambda e: e.tensor_add(sq2[0][0:H, :], sq2[0][0:H, :],
                                    sq2[1][0:H, :]), wait=[("ss", sr)])
        em.vd()
        vfh2 = em.v(lambda e, nw=nw: e.tensor_scalar(
            fhead[:, nw * W:(nw + 1) * W], sq2[0][0:H, :], negea_sb[:],
            None, ALU.mult))
    tap("fhead", fhead[:, 0:TPC], [("vv", vfh2)])
    prev_ev = 0
    fhT3 = fheadT.rearrange("p (h t) -> p h t", h=H)
    for nt in range(NT):
        tf = em.t(lambda e, nt=nt: e.transpose(
            PS[1][0:128, 0:H], fhead[:, nt * 128:(nt + 1) * 128],
            ident_f[0:H, 0:H]),
            wait=[("vv", vfh2), ("vv", prev_ev), ("gg", GGK)])
        prev_ev = em.v(lambda e, nt=nt: e.tensor_copy(
            fhT3[:, :, nt:nt + 1], PS[1][0:128, 0:H][:, :, None]),
            wait=[("mm", tf)])
    vv_fhT = em.now("vv")
    sa_fh_last = em.now("ss")

    if stop_stage <= 4:
        return _finish()
    relay_sent = {}

    def correction(h, gate_v, gate_vss, gate_mm, gate_sbf, gate_sv_ss,
                   cur_vkd):
        pp = h % 2
        vqg = em.v(lambda e, pp=pp: e.tensor_tensor(
            qg_scr[:, 0:TPC], q_pp[pp][:], eG_pp[pp][:], ALU.mult),
            wait=[("vv", gate_v), ("ss", gate_vss), ("mm", gate_mm)])
        em.vd()
        d1 = em.gd(lambda e: e.dma_start(
            out=Sq_all.rearrange("p (q w) -> p q w", q=NQ),
            in_=drout.rearrange("(q r) w -> r q w", r=129)[0:128, :, :]),
            wait=[("cc", relay_sent[h]), ("vv", cur_vkd),
                  ("ss", em.now("ss")), ("mm", em.now("mm"))])
        d2 = em.gd(lambda e: e.dma_start(
            out=a8_sb,
            in_=drout.rearrange("(q r) w -> q r w", r=129)[:, 128:129, :]
            .rearrange("q a w -> (q a) w")))
        tqa = em.t(lambda e: e.transpose(
            PS[5][0:128, 0:NQ], a8_sb, ident_f[0:NQ, 0:NQ]),
            wait=[("dgp", d2), ("vv", gate_sbf), ("ss", gate_sv_ss),
                  ("gg", GGK)])
        em.v(lambda e: e.tensor_copy(Aq8[:], PS[5][:, 0:NQ]),
             wait=[("mm", tqa)])
        em.vd()
        em.v(lambda e: e.tensor_tensor(g1c[:], Aq8[:], mrep_sb[:], ALU.mult),
             wait=[("vv", v_mrep)])
        em.vd()
        em.v(lambda e: e.tensor_tensor(g1c[:], g1c[:], mc1_sb[:], ALU.add))
        em.vd()
        em.v(lambda e: e.memset(e8c[:, NQ - 1:NQ], 1.0))
        em.vd()
        for qc in range(NQ - 2, -1, -1):
            em.v(lambda e, qc=qc: e.tensor_mul(
                e8c[:, qc:qc + 1], e8c[:, qc + 1:qc + 2],
                g1c[:, qc + 1:qc + 2]))
            em.vd()
        vw8 = em.v(lambda e: e.tensor_tensor(w8c[:], e8c[:], mrep_sb[:],
                                             ALU.mult))
        em.vd()
        wb = w8c[:, :, None].broadcast_to([128, NQ, 128])
        em.v(lambda e, wb=wb: e.tensor_tensor(
            wS_scr[:, 0:NQ * 128].rearrange("p (q w) -> p q w", q=NQ),
            Sq_all.rearrange("p (q w) -> p q w", q=NQ), wb, ALU.mult),
            wait=[("dgp", d1), ("vv", vw8)])
        em.vd()
        wlen = NQ * 128
        src = wS_scr
        fold_dsts = [fold1, fold2]
        di = 0
        while wlen > 128:
            half = wlen // 2
            dst = Sin_f if half == 128 else fold_dsts[di]
            em.v(lambda e, src=src, dst=dst, half=half: e.tensor_tensor(
                dst[:, 0:half], src[:, 0:half], src[:, half:2 * half],
                ALU.add))
            em.vd()
            src, wlen, di = dst, half, di + 1
        vsb = em.v(lambda e: e.tensor_copy(Sin_bf[:], Sin_f[:, 0:128]))
        em.vd()
        vo = gate_sbf
        tmc = None
        for nw in range(NW):
            tmc = em.t(lambda e, nw=nw: e.matmul(
                PS[4][:, 0:W], Sin_bf[:], qg_scr[:, nw * W:(nw + 1) * W],
                start=True, stop=True),
                wait=[("vv", vsb), ("vv", vqg), ("vv", vo)])
            vo = em.v(lambda e, h=h, nw=nw: e.tensor_tensor(
                o_sb[:, h * TPC + nw * W: h * TPC + nw * W + W],
                o_sb[:, h * TPC + nw * W: h * TPC + nw * W + W],
                PS[4][:, 0:W], ALU.add), wait=[("mm", tmc)])
        return vo, tmc

    prev_sbf = vv_xln2
    prev_oc_last = 0
    prev_skt_last = 0
    prev_scan_td = 0
    prev_corr_mm = em.now("mm")
    prev_corr_v = 0
    prev_khT_mm = 0
    vq_ev = {}
    veg_ev = {}
    vkd_ev = {}
    vf_last_d = 0
    sbf_last = {}
    sv_last_d = {}
    vmm_last = {}

    for h in range(H if n_heads is None else n_heads):
        pp = h % 2
        fbit = f_h
        head_entry_v = max(prev_sbf, prev_corr_v)

        two_buf = wproj_B is not wproj
        wr_ld = em.gd(lambda e, h=h: e.dma_start(
            out=wproj[:, 0:ND * 128].rearrange("p (c o) -> p c o", c=ND),
            in_=dwr[:, h * 128:(h + 1) * 128]
            .rearrange("(c p) o -> p c o", p=128)),
            wait=[("vv", em.now("vv")), ("mm", prev_corr_mm)])
        st_local = {}

        def wload(wdram, gate_mm):
            return em.gd(lambda e, h=h, wdram=wdram: e.dma_start(
                out=wproj_B[:, 0:ND * 128].rearrange("p (c o) -> p c o", c=ND),
                in_=wdram[:, h * 128:(h + 1) * 128]
                .rearrange("(c p) o -> p c o", p=128)),
                wait=[("mm", gate_mm)] +
                ([("dsy", em.now("dsy"))] if dbg else []))

        if two_buf:
            st_local["wq_ld"] = wload(dwq, prev_corr_mm)

        q_mm = {}
        q_silu = {}
        k_mm = {}
        k_silu = {}
        sv_ev = {}

        def emit_qk_group(mm_d, silu_d_, nws, wl_key, qsf_g, bank_war):
            for nw in nws:
                tq = None
                for c in range(ND):
                    if c == 0:
                        wt = [("dgp", st_local[wl_key])] + bank_war(nw)
                    else:
                        wt = []
                    tq = em.t(lambda e, nw=nw, c=c: e.matmul(
                        PS[3 + nw % 2][:, 0:W],
                        wproj_B[:, c * 128:(c + 1) * 128],
                        arB[:, c * TPC + nw * W: c * TPC + nw * W + W],
                        start=(c == 0), stop=(c == ND - 1)), wait=wt)
                mm_d[nw] = tq
                silu_d_[nw] = em.s(lambda e, nw=nw: e.activation(
                    qs_f[:, nw * W:(nw + 1) * W], PS[3 + nw % 2][:, 0:W],
                    ACT.Silu),
                    wait=[("mm", tq)] + ([("vv", qsf_g)] if nw == 0 else []))

        def emit_stats(silu_ev_d, norm_row, dst_bf, prev_t2, prev_vv,
                       sqb, s2_mm, tapme=False):
            em.sd()
            t2p = {}
            rows_v = {}
            for nw in range(NW):
                s2 = em.s(lambda e, nw=nw, sqb=sqb: e.activation(
                    sqb[nw % 2], qs_f[:, nw * W:(nw + 1) * W], ACT.Square),
                    wait=[("mm", t2p.get(nw - 2, prev_t2.get(NW - 2 + nw,
                                                             s2_mm))),
                          ("ss", silu_ev_d[nw])])
                t2p[nw] = em.t(lambda e, nw=nw, sqb=sqb: e.matmul(
                    pvx[0:1, 0:W], ones_col_b[:], sqb[nw % 2],
                    start=True, stop=True),
                    wait=[("ss", s2), ("gg", GGK)] +
                    ([("vv", rows_v[nw - 1])] if nw >= 1 else
                     [("vv", prev_vv)]))
                rows_v[nw] = em.v(lambda e, nw=nw: e.tensor_scalar(
                    row_att[:, nw * W:(nw + 1) * W], pvx[0:1, 0:W],
                    1.0 / 128.0, None, ALU.mult), wait=[("mm", t2p[nw])])
            em.s(lambda e: e.activation(row_att[:], row_att[:], ACT.Ln,
                                        bias=EPS_RMS),
                 wait=[("vv", rows_v[NW - 1])])
            em.sd()
            srr2 = em.s(lambda e: e.activation(row_att[:], row_att[:],
                                               ACT.Exp, scale=-0.5))
            if tapme:
                tap("rsq0", row_att[:], [("ss", srr2)])
            vtt = None
            for nw in range(NW):
                t3 = em.t(lambda e, nw=nw, norm_row=norm_row: e.matmul(
                    pvx[:, 0:W], norm_row[:],
                    row_att[:, nw * W:(nw + 1) * W],
                    start=True, stop=True),
                    wait=[("ss", srr2)] +
                    ([("vv", vtt)] if nw >= 1 else []))
                vtt = em.v(lambda e, nw=nw, dst_bf=dst_bf: e.tensor_tensor(
                    dst_bf[:, nw * W:(nw + 1) * W],
                    qs_f[:, nw * W:(nw + 1) * W], pvx[:, 0:W],
                    ALU.mult), wait=[("mm", t3)])
            em.vd()
            return vtt, t2p

        def emit_v_tiles(nts):
            for nt in nts:
                vb = PS[5] if nt % 2 == 0 else PS[2]
                tv_ = None
                for c in range(ND):
                    if c == 0:
                        wt = [("dgp", st_local["wv_ld"])]
                        if nt < 2:
                            wt += [("vv", head_entry_v), ("ss", prev_oc_last)]
                        else:
                            wt += [("ss", sv_ev[nt - 2])]
                    else:
                        wt = []
                    tv_ = em.t(lambda e, nt=nt, c=c, vb=vb: e.matmul(
                        vb[:, 0:128],
                        arB[:, c * TPC + nt * 128: c * TPC + (nt + 1) * 128],
                        wproj_B[:, c * 128:(c + 1) * 128],
                        start=(c == 0), stop=(c == ND - 1)), wait=wt)
                sv_ev[nt] = em.s(lambda e, nt=nt, vb=vb: e.activation(
                    v_tm[:, nt * 128:(nt + 1) * 128], vb[:, 0:128],
                    ACT.Silu), wait=[("mm", tv_)])
                st_local["tv_last"] = tv_

        def q_bank_war(nw):
            if nw == 0:
                return [("ss", prev_oc_last)]
            if nw == 1:
                return [("vv", head_entry_v)]
            return [("ss", q_silu[nw - 2])]

        def k_bank_war(nw):
            if nw < 2:
                par = [i for i in range(NW) if i % 2 == nw % 2]
                return [("ss", q_silu[par[-1]])]
            return [("ss", k_silu[nw - 2])]

        def do_q_stats():
            vvq, t2q = emit_stats(q_silu, qnw_sb, q_pp[pp], {},
                                  head_entry_v, sq_pp, 0, tapme=(h == 0))
            st_local["vv_q"] = vvq
            st_local["t2q"] = t2q
            if h == 0:
                tap("wp0", wproj_B[:, 0:ND * 128],
                    [("dgp", st_local["wq_ld"])])
                tap("qs0", qs_f[:, 0:TPC], [("ss", em.now("ss"))])

        sched = {
            2: lambda: emit_qk_group(q_mm, q_silu, [0, 1], "wq_ld",
                                     head_entry_v, q_bank_war),
            3: lambda: emit_qk_group(q_mm, q_silu, [2, 3], "wq_ld",
                                     head_entry_v, q_bank_war),
            5: do_q_stats,
            6: lambda: st_local.__setitem__("wk_ld", wload(dwk, q_mm[NW - 1])),
            7: lambda: emit_qk_group(k_mm, k_silu, [0, 1], "wk_ld",
                                     st_local["vv_q"], k_bank_war),
            9: lambda: emit_qk_group(k_mm, k_silu, [2, 3], "wk_ld",
                                     st_local["vv_q"], k_bank_war),
            10: lambda: st_local.__setitem__("wv_ld", wload(dwv, k_mm[NW - 1])),
            11: lambda: emit_v_tiles(range(0, NT // 2)),
            13: lambda: emit_v_tiles(range(NT // 2, NT)),
        } if (two_buf and NT >= 16 and NW == 4) else None

        ev_evac = {}
        stt_ev = {}
        for nt in range(NT):
            pb = nt % 2
            tr_ = None
            for c in range(ND):
                if c == 0:
                    wt = [("dgp", wr_ld)]
                    if nt < 2:
                        wt += [("vv", prev_sbf)]
                        if h == 0:
                            wt += [("vv", vv_fhT), ("ss", sa_fh_last),
                                   ("vv", vv_xln2)]
                    else:
                        wt += [("vv", ev_evac[nt - 2])]
                else:
                    wt = []
                tr_ = em.t(lambda e, nt=nt, c=c, pb=pb: e.matmul(
                    PS[pb][:, 0:128],
                    arB[:, c * TPC + nt * 128: c * TPC + (nt + 1) * 128],
                    wproj[:, c * 128:(c + 1) * 128],
                    start=(c == 0), stop=(c == ND - 1)), wait=wt)
            ev_evac[nt] = em.v(lambda e, pb=pb: e.tensor_copy(
                rt_pp[pb], PS[pb][:, 0:128]), wait=[("mm", tr_)])
            em.vd()
            ssg = em.s(lambda e, pb=pb, nt=nt: e.activation(
                scorz_pp[nt % 2], rt_pp[pb], ACT.Sigmoid),
                wait=[("vv", ev_evac[nt]),
                      ("vv", stt_ev.get(nt - 2, 0))])
            em.v(lambda e, pb=pb: e.max(out=m8[:, 0:8], in_=rt_pp[pb]))
            em.vd()
            em.v(lambda e, pb=pb: e.match_replace(
                out=rko_pp[0], in_to_replace=m8[:, 0:8], in_values=rt_pp[pb],
                imm_value=NEG_BIG))
            em.vd()
            em.v(lambda e: e.max(out=m8[:, 8:16], in_=rko_pp[0]))
            em.vd()
            em.v(lambda e: e.match_replace(
                out=rko_pp[1], in_to_replace=m8[:, 8:16], in_values=rko_pp[0],
                imm_value=NEG_BIG))
            em.vd()
            em.v(lambda e: e.max(out=m8[:, 16:24], in_=rko_pp[1]))
            em.vd()
            em.v(lambda e: e.match_replace(
                out=rko_pp[0], in_to_replace=m8[:, 16:24], in_values=rko_pp[1],
                imm_value=NEG_BIG))
            em.vd()
            em.v(lambda e: e.max(out=m8[:, 24:32], in_=rko_pp[0]))
            em.vd()
            stt_ev[nt] = em.v(lambda e, pb=pb, nt=nt: e.scalar_tensor_tensor(
                fbit[:, nt * 128:(nt + 1) * 128], rt_pp[pb], m8[:, 31:32],
                scorz_pp[nt % 2], ALU.is_ge, ALU.mult,
                accum_out=zall[:, nt:nt + 1]),
                wait=[("ss", ssg)] +
                ([("mm", prev_scan_td)] if nt == 0 else []))
            em.vd()
            if sched and nt in sched:
                sched[nt]()
        r_mm_last = tr_
        if sched is None:
            st_local["wq_ld"] = wload(dwq, r_mm_last)
            emit_qk_group(q_mm, q_silu, list(range(NW)), "wq_ld",
                          head_entry_v, q_bank_war)
            do_q_stats()
            st_local["wk_ld"] = wload(dwk, q_mm[NW - 1])
            emit_qk_group(k_mm, k_silu, list(range(NW)), "wk_ld",
                          st_local["vv_q"], k_bank_war)
            st_local["wv_ld"] = wload(dwv, k_mm[NW - 1])
            emit_v_tiles(range(NT))
        vv_q = st_local["vv_q"]
        vq_ev[h] = vv_q
        if stop_stage == 41:
            return _finish()
        em.s(lambda e: e.activation(zall[:, 0:NT], zall[:, 0:NT], ACT.Ln,
                                    bias=EPS_Z),
             wait=[("vv", stt_ev[NT - 1])])
        em.sd()
        sz2 = em.s(lambda e: e.activation(zall[:, 0:NT], zall[:, 0:NT],
                                          ACT.Exp, scale=-1.0))
        em.v(lambda e, h=h: e.tensor_tensor(
            zf[:, 0:NT], zall[:, 0:NT],
            fhT3[:, h:h + 1, :].rearrange("p a t -> p (a t)"), ALU.mult),
            wait=[("ss", sz2)] + ([("vv", vv_fhT)] if h == 0 else []))
        em.vd()
        vf_ = None
        for nt in range(NT):
            vf_ = em.v(lambda e, nt=nt: e.tensor_scalar(
                fbit[:, nt * 128:(nt + 1) * 128],
                fbit[:, nt * 128:(nt + 1) * 128],
                zf[:, nt:nt + 1], None, ALU.mult))
        vf_last_d = vf_

        vv_g = None
        tg_last = None
        g_ev = {}
        for cc_ in range(NCH):
            gb = PS[cc_ % 2]
            if cc_ < 2:
                wt = [("vv", vf_), ("vv", ev_evac[NT - 2 + cc_])] + \
                    ([("gg", GGK)] if cc_ == 0 else [])
            else:
                wt = [("vv", g_ev[cc_ - 2])]
            tg_last = em.t(lambda e, cc_=cc_, gb=gb: e.matmul(
                gb[:, 0:128], fbit[:, cc_ * 128:(cc_ + 1) * 128],
                triu_h[:], start=True, stop=True), wait=wt)
            vv_g = em.v(lambda e, cc_=cc_, gb=gb: e.tensor_scalar(
                g_scr[:, cc_ * 128:(cc_ + 1) * 128], gb[:, 0:128],
                G_CLAMP, None, ALU.max), wait=[("mm", tg_last)])
            g_ev[cc_] = vv_g
        em.vd()

        sg1 = em.s(lambda e: e.activation(eg_sb[:], g_scr[:], ACT.Exp),
                   wait=[("vv", vv_g)])
        sg2 = em.s(lambda e: e.activation(em_sb[:], g_scr[:], ACT.Exp,
                                          scale=-1.0))
        vgl = em.v(lambda e: e.tensor_copy(
            egp[:],
            g_scr.rearrange("p (c w) -> p c w", c=NCH)[:, :, 127:128]
            .rearrange("p c a -> p (c a)")), wait=[("vv", vv_g)])
        em.vd()
        segc = em.s(lambda e: e.activation(egc_f[:], egp[:], ACT.Exp),
                    wait=[("vv", vgl)])
        tgp1 = em.t(lambda e: e.transpose(
            PS[0][0:NCH, 0:128], egp[:], ident_f[:]),
            wait=[("vv", vgl), ("vv", g_ev[NCH - 2]), ("gg", GGK)])
        vge = em.v(lambda e: e.tensor_copy(glt[:], PS[0][0:NCH, 0:128]),
                   wait=[("mm", tgp1)])
        em.vd()
        tgp2 = em.t(lambda e: e.matmul(
            PS[0][:, 0:NCH], glt[:], striu_s[:], start=True, stop=True),
            wait=[("vv", vge)])
        vgp2 = em.v(lambda e: e.tensor_copy(gprev[:], PS[0][:, 0:NCH]),
                    wait=[("mm", tgp2)])
        em.vd()
        em.sd()
        sg4 = em.s(lambda e: e.activation(egp[:], gprev[:], ACT.Exp),
                   wait=[("vv", vgp2)])
        va_ = em.v(lambda e: e.tensor_tensor(
            acol[:], gprev[:, NCH - 1:NCH], g_scr[:, TPC - 1:TPC], ALU.add))
        sat = em.s(lambda e: e.activation(Atot[:], acol[:], ACT.Exp),
                   wait=[("vv", va_)])
        veg_last = None
        for c_ in range(NCH):
            veg_last = em.s(lambda e, c_=c_, pp=pp: e.activation(
                eG_pp[pp][:, c_ * 128:(c_ + 1) * 128],
                eg_sb[:, c_ * 128:(c_ + 1) * 128], ACT.Copy,
                scale=egp[:, c_:c_ + 1]),
                wait=([("ss", sg4), ("vv", prev_corr_v)] if c_ == 0 else []))
        veg_ev[h] = veg_last

        sq2k = [stage[:, 0:W], stage[:, W:2 * W]] if TPC >= 2 * W else sq_pp
        vv_k, _ = emit_stats(k_silu, knw_sb, k_bf, {}, vv_q, sq2k,
                             tg_last if sq2k is not sq_pp else
                             st_local["t2q"][NW - 1])
        sv_last = sv_ev[NT - 1]
        sv_last_d[h] = sv_last
        vmm_last[h] = st_local["tv_last"]
        if stop_stage in (42, 43):
            return _finish()

        em.v(lambda e: e.tensor_sub(fdm[:, 1:TPC], g_scr[:, 1:TPC],
                                    g_scr[:, 0:TPC - 1]), wait=[("vv", vv_g)])
        em.vd()
        vfdm = em.v(lambda e: e.tensor_copy(
            fdm.rearrange("p (c w) -> p c w", c=NCH)[:, :, 0:1],
            g_scr.rearrange("p (c w) -> p c w", c=NCH)[:, :, 0:1]))
        em.vd()
        sg3 = em.s(lambda e: e.activation(fdm[:], fdm[:], ACT.Exp),
                   wait=[("vv", vfdm)])
        vs_ = em.v(lambda e: e.tensor_scalar(fdm[:], fdm[:], -1.0, 1.0,
                                             ALU.mult, ALU.add),
                   wait=[("ss", sg3)])
        em.vd()
        em.v(lambda e: e.tensor_tensor(kkt[:], fdm[:], k_bf[:], ALU.mult),
             wait=[("ss", sg2)])
        em.vd()
        vkd = em.v(lambda e: e.tensor_tensor(kd_sb[:], kkt[:], em_sb[:],
                                             ALU.mult))
        em.vd()
        vkd_ev[h] = vkd
        khat_s = {}
        for c_ in range(NCH):
            khat_s[c_] = em.s(lambda e, c_=c_: e.activation(
                khat_sb[:, c_ * 128:(c_ + 1) * 128],
                kd_sb[:, c_ * 128:(c_ + 1) * 128], ACT.Copy,
                scale=egc_f[:, c_:c_ + 1]),
                wait=([("vv", vkd), ("mm", prev_khT_mm)] if c_ == 0 else []))
        if stop_stage == 44:
            return _finish()

        if h == 0:
            tap("q0", q_pp[pp][:], [("vv", vv_q)])
            tap("k0", k_bf[:], [("vv", vv_k)])
            tap("v0", v_tm[:], [("ss", sv_last)])
            tap("eg0", eg_sb[:], [("ss", sg1), ("vv", em.now("vv"))])
            tap("eG0", eG_pp[pp][:], [("ss", veg_ev[h])])
        vqd = em.v(lambda e, pp=pp: e.tensor_tensor(
            stage[:, 0:TPC], q_pp[pp][:], eg_sb[:], ALU.mult),
            wait=[("vv", vv_q), ("mm", tg_last)])
        em.vd()

        RK = min(8, NCH)
        khT_rnd = {}
        tk = None
        for r_ in range(NCH // RK):
            for ci in range(RK):
                c = r_ * RK + ci
                wt = [("ss", khat_s[c])]
                if ci == 0:
                    wt += [("gg", GGK)]
                    wt += ([("ss", khT_rnd[r_ - 1])] if r_ >= 1 else
                           [("ss", prev_skt_last)])
                tk = em.t(lambda e, c=c, ci=ci: e.transpose(
                    PB[:, ci * 128:(ci + 1) * 128],
                    khat_sb[:, c * 128:(c + 1) * 128], ident_bf[:]), wait=wt)
            khT_rnd[r_] = em.s(lambda e, r_=r_, RK=RK: e.copy(
                khT_full[:, r_ * RK * 128:(r_ + 1) * RK * 128],
                PB[:, 0:RK * 128]),
                wait=[("mm", tk)] + ([("vv", vkd)] if r_ == 0 else []))
        prev_khT_mm = tk
        if stop_stage == 46:
            return _finish()

        em.v(lambda e: e.memset(S_f[0][:], 0.0))
        em.vd()
        sbt_prev = em.v(lambda e: e.memset(S_bf[0][:], 0.0))
        em.vd()
        sbf_f = {}
        oc_ev = {}
        vpm_ev = {}
        s_idx = 0
        for c in range(NCH):
            pb = c % 2
            if c >= 2:
                wt_tp = [("vv", vpm_ev[c - 2])]
            else:
                wt_tp = []
            tp = em.t(lambda e, c=c, pb=pb: e.matmul(
                PS[pb][:, 0:128], kd_sb[:, c * 128:(c + 1) * 128],
                stage[:, c * 128:(c + 1) * 128], start=True, stop=True),
                wait=[("vv", vkd), ("vv", vqd)] + wt_tp)
            vpm_ev[c] = em.v(lambda e, pb=pb: e.tensor_tensor(
                pm_sb[pb][:], PS[pb][:, 0:128], triu_f[:], ALU.mult),
                wait=[("mm", tp), ("gg", GGK)])
            if c >= 2:
                wt_o1 = [("ss", oc_ev[c - 2])]
            elif pb == 0:
                wt_o1 = [("ss", sv_ev[NT - 1])]
            else:
                wt_o1 = [("vv", vv_k)]
            to1 = em.t(lambda e, c=c, pb=pb: e.matmul(
                PS[2 + pb][:, 0:128], v_tm[:, c * 128:(c + 1) * 128],
                pm_sb[pb][:], start=True, stop=False),
                wait=[("vv", vpm_ev[c]), ("ss", sv_ev[c])] + wt_o1)
            to2 = em.t(lambda e, c=c, pb=pb, s_idx=s_idx: e.matmul(
                PS[2 + pb][:, 0:128], S_bf[s_idx][:],
                stage[:, c * 128:(c + 1) * 128], start=False, stop=True),
                wait=[("vv", sbt_prev)])
            oc_ev[c] = em.s(lambda e, h=h, c=c, pb=pb: e.copy(
                o_sb[:, h * TPC + c * 128: h * TPC + (c + 1) * 128],
                PS[2 + pb][:, 0:128]), wait=[("mm", to2)])
            if c >= 2:
                wt_td = [("vv", sbf_f[c - 2])]
            elif pb == 0:
                wt_td = [("vv", vv_k)]
            else:
                wt_td = [("ss", sv_last)]
            td = em.t(lambda e, c=c, pb=pb: e.matmul(
                PS[4 + pb][:, 0:128], khT_full[:, c * 128:(c + 1) * 128],
                v_tm[:, c * 128:(c + 1) * 128], start=True, stop=True),
                wait=[("ss", khT_rnd[c // RK]), ("ss", sv_ev[c])] + wt_td)
            ns = 1 - s_idx
            aC = eg_sb[:, c * 128 + 127: c * 128 + 128]
            rawq = [("vv", sbf_f[c - 1])] if c >= 1 else []
            sbt_prev = em.v(lambda e, pb=pb, s_idx=s_idx, aC=aC:
                            e.scalar_tensor_tensor(
                                S_bf[0][:], S_f[s_idx][:], aC,
                                PS[4 + pb][:, 0:128], ALU.mult, ALU.add),
                            wait=[("mm", td)] + rawq)
            sbf_f[c] = em.v(lambda e, s_idx=s_idx, ns=ns, pb=pb, aC=aC:
                            e.scalar_tensor_tensor(
                                S_f[ns][:], S_f[s_idx][:], aC,
                                PS[4 + pb][:, 0:128], ALU.mult, ALU.add),
                            wait=rawq)
            s_idx = ns
        sbf_last[h] = sbf_f[NCH - 1]
        if stop_stage == 45:
            return _finish()

        if h == 0:
            tap("S0", S_f[s_idx][:], [("vv", sbf_f[NCH - 1])])
            tap("At0", Atot[:], [("ss", sat)])
            tap("oloc", o_sb[:], [("ss", em.now("ss")), ("vv", em.now("vv"))])
        if h > 0 and do_corr:
            prev_corr_v, prev_corr_mm = correction(
                h - 1, vq_ev[h - 1], veg_ev[h - 1], vmm_last[h],
                sbf_f[NCH - 1], sv_last, sbf_f[NCH - 1])

        if do_relay:
            d1r = em.gd(lambda e, s_idx=s_idx: e.dma_start(
                out=drin[0:128, :], in_=S_f[s_idx][:]),
                wait=[("vv", sbf_f[NCH - 1]),
                      ("cc", relay_sent.get(h - 1, 0))])
            d2r = em.gd(lambda e: e.dma_start(
                out=drin[128:129, :].rearrange("a p -> p a"),
                in_=Atot[:]), wait=[("ss", sat)])
            relay_sent[h] = em.cc(lambda e: e.collective_compute(
                "AllGather", ALU.bypass,
                replica_groups=[list(range(C.n_cores))],
                ins=[drin.ap()], outs=[drout.ap()]), wait=[("dgp", d2r)])

        prev_sbf = sbf_f[NCH - 1]
        prev_oc_last = oc_ev[NCH - 1]
        prev_skt_last = khT_rnd[NCH // RK - 1]
        prev_scan_td = td

    if not (do_relay and do_corr):
        return _finish()
    hh_ = (H if n_heads is None else n_heads) - 1
    vv_o, _ = correction(hh_, vq_ev[hh_], veg_ev[hh_], em.now("mm"),
                         sbf_last[hh_], sv_last_d[hh_], em.now("vv"))
    if stop_stage <= 5:
        return _finish()

    tap("o", o_sb[:], [("vv", vv_o)])
    dwol = em.sy(lambda e: e.dma_start(
        out=wo_stage[:].rearrange("p (c o) -> p c o", c=ND),
        in_=dwo.rearrange("(c p) o -> p c o", p=128)),
        wait=[("vv", em.now("vv"))])
    vv_wo = None
    for c in range(ND):
        vv_wo = em.v(lambda e, c=c: e.tensor_scalar(
            wo_bf[:, c * D:(c + 1) * D], wo_stage[:, c * D:(c + 1) * D],
            gnw_sb[:, c:c + 1], None, ALU.mult), wait=[("dsy", dwol)])
    ost = [ar2[:, W:W + W // 2].bitcast(BF16),
           ar2[:, W + W // 2:2 * W].bitcast(BF16)]
    row_ev = {}
    ot_ev = {}
    for nw in range(NW):
        for hh in range(H):
            k_ = nw * H + hh
            s2 = em.s(lambda e, hh=hh, nw=nw: e.activation(
                ost[hh % 2][:, 0:W],
                o_sb[:, hh * TPC + nw * W: hh * TPC + nw * W + W], ACT.Square),
                wait=([("vv", vv_o)] if k_ < 2 else []) +
                [("mm", ot_ev.get(k_ - 2, 0))])
            ot_ev[k_] = em.t(lambda e, hh=hh, nw=nw: e.matmul(
                pvx[0:1, 0:W], ones_col_b[:], ost[hh % 2][:, 0:W],
                start=(hh == 0), stop=(hh == H - 1)),
                wait=[("ss", s2), ("gg", GGK)] +
                ([("vv", row_ev.get(nw - 1, vv_o))] if hh == 0 else []))
        mmo = em.now("mm")
        row_ev[nw] = em.v(lambda e, nw=nw: e.tensor_copy(
            row_att[:, nw * W:(nw + 1) * W], pvx[0:1, 0:W]),
            wait=[("mm", mmo)])
    em.vd()
    em.v(lambda e: e.tensor_scalar(row_att[:], row_att[:], 1.0 / D, None,
                                   ALU.mult))
    em.s(lambda e: e.activation(row_att[:], row_att[:], ACT.Ln,
                                bias=EPS_RMS),
         wait=[("vv", em.now("vv"))])
    em.sd()
    sro2 = em.s(lambda e: e.activation(row_att[:], row_att[:], ACT.Exp,
                                       scale=-0.5))
    vrg = replicate_row(row_att, 4, rep_g, [("ss", sro2)])
    scrm_ev = {}
    for dout in range(ND):
        for nw in range(NW):
            k_ = dout * NW + nw
            pbk = k_ % 2
            tw = None
            for hh in range(H):
                tw = em.t(lambda e, dout=dout, nw=nw, hh=hh, pbk=pbk: e.matmul(
                    PS[pbk][:, 0:W],
                    wo_bf[:, hh * D + dout * 128: hh * D + dout * 128 + 128],
                    o_sb[:, hh * TPC + nw * W: hh * TPC + nw * W + W],
                    start=(hh == 0), stop=(hh == H - 1)),
                    wait=[("vv", vv_wo), ("vv", vv_o),
                          ("vv", scrm_ev.get(k_ - 2, 0))]
                    if hh == 0 else [])
            scrm_ev[k_] = em.v(lambda e, pbk=pbk, nw=nw: e.tensor_tensor(
                scr_m[:, 0:W], PS[pbk][:, 0:W],
                rep_g[:, nw * W:(nw + 1) * W], ALU.mult),
                wait=[("mm", tw), ("vv", vrg)])
            em.vd()
            em.v(lambda e, dout=dout, nw=nw: e.tensor_tensor(
                x_sb[:, dout * TPC + nw * W: dout * TPC + nw * W + W],
                x_sb[:, dout * TPC + nw * W: dout * TPC + nw * W + W],
                scr_m[:, 0:W], ALU.add))
            em.vd()
    attn_done = em.now("vv")

    if stop_stage <= 6:
        return _finish([("vv", attn_done)])
    tap("x2", x_sb[:], [("vv", attn_done)])
    psB.close()
    open_stacks.remove(psB)
    vv_intx2 = ln_quant([("vv", attn_done)])
    psC = ExitStack()
    pxC = [psC.enter_context(nc.psum_tensor(f"pxc{i}", [128, 512], F32))
           for i in range(2)]
    vcmb2 = ffn_gateup(vv_intx2, pxC, em.now("ss"))
    vv_x2 = ffn_down(vcmb2, pxC)
    psC.close()

    return _finish([("vv", vv_x2)])


def _shard_w(wT, nblk, nslot, R):
    outs = []
    rows = wT.shape[0]
    for r in range(R):
        m = np.ones((rows, nslot * 128), np.float32)
        real = max(0, min(nslot, nblk - r * nslot))
        if real:
            m[:, :real * 128] = wT[:, r * nslot * 128:(r * nslot + real) * 128]
        outs.append(np.ascontiguousarray(m))
    return outs


def kernel(L0_attn_hidden_states, L0_attn_Wq, L0_attn_Wk, L0_attn_Wv,
           L0_attn_Wr, L0_attn_Wa, L0_attn_A_log, L0_attn_dt_bias,
           L0_attn_q_norm_w, L0_attn_k_norm_w, L0_attn_g_norm_w, L0_attn_Wo,
           L0_ffn_clip_x, L0_ffn_clip_down, L0_ffn_w_gate, L0_ffn_w_up,
           L0_ffn_w_down):
    cfg = Cfg()
    B, T, Dm = L0_attn_hidden_states.shape
    n = cfg.n_cores
    cpb = cfg.cores_per_batch
    TPC = cfg.TPC
    assert B * T == n * TPC and T == cpb * TPC and Dm == cfg.D

    x = np.ascontiguousarray(L0_attn_hidden_states, dtype=np.float32)
    com = {
        "wq": np.ascontiguousarray(L0_attn_Wq, np.float32),
        "wk": np.ascontiguousarray(L0_attn_Wk, np.float32),
        "wv": np.ascontiguousarray(L0_attn_Wv, np.float32),
        "wr": np.ascontiguousarray(L0_attn_Wr, np.float32),
        "wo": np.ascontiguousarray(L0_attn_Wo, np.float32),
        "wa": np.ascontiguousarray(L0_attn_Wa, np.float32),
        "alog": np.ascontiguousarray(
            np.reshape(L0_attn_A_log, (cfg.H, 1)), np.float32),
        "dtb": np.ascontiguousarray(
            np.reshape(L0_attn_dt_bias, (cfg.H, 1)), np.float32),
        "qnw": np.ascontiguousarray(
            np.reshape(L0_attn_q_norm_w, (1, 128)), np.float32),
        "knw": np.ascontiguousarray(
            np.reshape(L0_attn_k_norm_w, (1, 128)), np.float32),
        "gnw": np.ascontiguousarray(
            np.reshape(L0_attn_g_norm_w, (cfg.ND, 128)).T, np.float32),
        "clipx": np.ascontiguousarray(
            np.reshape(L0_ffn_clip_x, (1, 1)), np.float32),
        "clipd": np.ascontiguousarray(
            np.reshape(L0_ffn_clip_down, (1, 1)), np.float32),
    }
    wgt = np.ascontiguousarray(np.transpose(L0_ffn_w_gate), np.float32)
    wut = np.ascontiguousarray(np.transpose(L0_ffn_w_up), np.float32)
    wdt = np.ascontiguousarray(np.transpose(L0_ffn_w_down), np.float32)
    wgm = _shard_w(wgt, cfg.NI, cfg.GS, n)
    wum = _shard_w(wut, cfg.NI, cfg.GS, n)
    wdm = _shard_w(wdt, cfg.NHB, cfg.DS, n)
    in_maps = []
    for c in range(n):
        b = c // cpb
        pos = c % cpb
        t0 = pos * TPC
        mask = np.zeros((1, n), np.float32)
        mask[0, b * cpb:b * cpb + pos] = 1.0
        m = dict(com)
        m["xt"] = np.ascontiguousarray(x[b, t0:t0 + TPC, :].T)
        m["rmask"] = mask
        m["wgm"] = wgm[c]
        m["wum"] = wum[c]
        m["wdm"] = wdm[c]
        in_maps.append(m)

    nc = build(cfg)
    res = run_bass_kernel_spmd(nc, in_maps, core_ids=list(range(n)))
    out = np.empty((B, T, Dm), np.float32)
    for c in range(n):
        b = c // cpb
        t0 = (c % cpb) * TPC
        out[b, t0:t0 + TPC, :] = np.asarray(res.results[c]["xout"]).T
    return out


# version 24
# speedup vs baseline: 1.0230x; 1.0230x over previous
import numpy as np
from contextlib import ExitStack
import concourse.bass as bass
import concourse.mybir as mybir
from concourse.bass_utils import run_bass_kernel_spmd

F32 = mybir.dt.float32
BF16 = mybir.dt.bfloat16
FP16 = mybir.dt.float16
ALU = mybir.AluOpType
ACT = mybir.ActivationFunctionType
AX = mybir.AxisListType

NEG_BIG = -1.0e30
MAGIC = 12582912.0
EPS_LN = 1.0e-5
EPS_RMS = 1.0e-5
EPS_Z = 1.0e-9
G_CLAMP = -60.0


class Cfg:
    def __init__(self, n_cores=8, cores_per_batch=4, TPC=2048, D=1024, I=2816,
                 H=8, CH=128):
        self.n_cores = n_cores
        self.cores_per_batch = cores_per_batch
        self.TPC = TPC
        self.D = D
        self.I = I
        self.H = H
        self.CH = CH
        self.ND = D // 128
        self.NI = I // 128
        self.NHB = D // 128
        self.NT = TPC // 128
        self.NCH = TPC // CH
        self.W = min(512, TPC)
        self.NW = TPC // self.W
        self.GS = -(-self.NI // n_cores)
        self.DS = -(-self.NHB // n_cores)


class Emit:
    ENGS = ("sync", "tensor", "vector", "scalar", "gpsimd")

    def __init__(self, nc, stack):
        self.nc = nc
        self.q = {e: [] for e in self.ENGS}
        self.sem = {}
        self.c = {}
        for name in ("dsy", "dgp", "dsc", "mm", "vv", "ss", "gg", "cc"):
            self.sem[name] = stack.enter_context(nc.semaphore(name))
            self.c[name] = 0

    def _emit(self, eng, fn, inc=None, inck=1, wait=()):
        if inc is not None:
            self.c[inc] += inck
        wait = tuple((n, v) for (n, v) in wait if v > 0)
        sem = self.sem

        def body(e, fn=fn, inc=inc, inck=inck, wait=wait):
            for (n, v) in wait:
                e.wait_ge(sem[n], v)
            instr = fn(e)
            if inc is not None:
                instr.then_inc(sem[inc], inck)
        self.q[eng].append(body)
        return self.c[inc] if inc is not None else None

    def sy(self, fn, wait=()):
        w = [("dsy", self.c["dsy"])] + list(wait)
        return self._emit("sync", fn, "dsy", 16, w)

    def gd(self, fn, wait=()):
        w = [("dgp", self.c["dgp"])] + list(wait)
        return self._emit("gpsimd", fn, "dgp", 16, w)

    def sc(self, fn, wait=()):
        w = [("dsc", self.c["dsc"])] + list(wait)
        return self._emit("scalar", fn, "dsc", 16, w)

    def t(self, fn, wait=()):
        return self._emit("tensor", fn, "mm", 1, wait)

    def v(self, fn, wait=()):
        return self._emit("vector", fn, "vv", 1, wait)

    def s(self, fn, wait=()):
        return self._emit("scalar", fn, "ss", 1, wait)

    def g(self, fn, wait=()):
        return self._emit("gpsimd", fn, "gg", 1, wait)

    def cc(self, fn, wait=()):
        return self._emit("gpsimd", fn, "cc", 1, wait)

    def vd(self):
        self.q["vector"].append(lambda v: v.drain())

    def sd(self):
        self.q["scalar"].append(lambda s: s.drain())

    def gdr(self):
        self.q["gpsimd"].append(lambda g: g.drain())

    def now(self, name):
        return self.c[name]

    def run(self):
        with self.nc.Block() as b:
            for eng in self.ENGS:
                fns = self.q[eng]
                if not fns:
                    continue
                getattr(b, eng)(lambda e, fns=fns: [f(e) for f in fns])


def build(cfg: Cfg, dbg=False, stop_stage=99, n_heads=None, do_relay=True,
          do_corr=True):
    nc = bass.Bass("TRN2", target_bir_lowering=False, debug=False)
    st = ExitStack()
    C = cfg
    TPC, D, I, H = C.TPC, C.D, C.I, C.H
    ND, NI, NT, NCH, NW, W, NB = C.ND, C.NI, C.NT, C.NCH, C.NW, C.W, C.NHB

    dx = nc.dram_tensor("xt", [D, TPC], F32, kind="ExternalInput").ap()
    dwq = nc.dram_tensor("wq", [D, D], F32, kind="ExternalInput").ap()
    dwk = nc.dram_tensor("wk", [D, D], F32, kind="ExternalInput").ap()
    dwv = nc.dram_tensor("wv", [D, D], F32, kind="ExternalInput").ap()
    dwr = nc.dram_tensor("wr", [D, D], F32, kind="ExternalInput").ap()
    dwo = nc.dram_tensor("wo", [D, D], F32, kind="ExternalInput").ap()
    dwa = nc.dram_tensor("wa", [D, H], F32, kind="ExternalInput").ap()
    dalog = nc.dram_tensor("alog", [H, 1], F32, kind="ExternalInput").ap()
    ddtb = nc.dram_tensor("dtb", [H, 1], F32, kind="ExternalInput").ap()
    dqnw = nc.dram_tensor("qnw", [1, 128], F32, kind="ExternalInput").ap()
    dknw = nc.dram_tensor("knw", [1, 128], F32, kind="ExternalInput").ap()
    dgnw = nc.dram_tensor("gnw", [128, ND], F32, kind="ExternalInput").ap()
    dclipx = nc.dram_tensor("clipx", [1, 1], F32, kind="ExternalInput").ap()
    dclipd = nc.dram_tensor("clipd", [1, 1], F32, kind="ExternalInput").ap()
    dwgm = nc.dram_tensor("wgm", [D, C.GS * 128], F32,
                          kind="ExternalInput").ap()
    dwum = nc.dram_tensor("wum", [D, C.GS * 128], F32,
                          kind="ExternalInput").ap()
    dwdm = nc.dram_tensor("wdm", [I, C.DS * 128], F32,
                          kind="ExternalInput").ap()
    drmask = nc.dram_tensor("rmask", [1, C.n_cores], F32, kind="ExternalInput").ap()
    dxout = nc.dram_tensor("xout", [D, TPC], F32, kind="ExternalOutput").ap()

    GS, DS, R = C.GS, C.DS, C.n_cores
    PAYR = 2 * GS * D + DS * I
    NSC = 2 * GS + DS
    dsci = nc.dram_tensor("scin", [128, NSC], F32)
    dsco = nc.dram_tensor("scout", [R * 128, NSC], F32)
    dwin = nc.dram_tensor("wqin", [PAYR, 128], BF16)
    dgath = nc.dram_tensor("wqgath", [R * PAYR, 128], BF16)
    dcd = nc.dram_tensor("cache_d", [NB * I, 128], BF16)
    dcg = nc.dram_tensor("cache_g", [NI * D, 128], BF16)
    dcu = nc.dram_tensor("cache_u", [NI * D, 128], BF16)
    drin = nc.dram_tensor("rin", [129, 128], F32)
    drout = nc.dram_tensor("rout", [C.n_cores * 129, 128], F32)
    dgin = nc.dram_tensor("gin", [1, 1], F32)
    dgout = nc.dram_tensor("gout", [1, 1], F32)
    taps = {}
    if dbg:
        def mktap(name, shape, dt):
            taps[name] = nc.dram_tensor("tap_" + name, shape, dt,
                                        kind="ExternalOutput").ap()
        mktap("intx", [128, cfg.ND * TPC], BF16)
        mktap("sclx", [1, 2], F32)
        mktap("h", [128, cfg.NI * TPC], FP16)
        mktap("sclh", [1, 2], F32)
        mktap("x1", [128, cfg.ND * TPC], F32)
        mktap("xln2", [128, cfg.ND * TPC], BF16)
        mktap("fhead", [H, TPC], F32)
        mktap("f0", [128, cfg.NT * 128], FP16)
        mktap("q0", [128, TPC], BF16)
        mktap("wp0", [128, cfg.ND * 128], BF16)
        mktap("qs0", [128, TPC], F32)
        mktap("rsq0", [1, TPC], F32)
        mktap("k0", [128, TPC], BF16)
        mktap("v0", [128, cfg.NT * 128], BF16)
        mktap("eg0", [128, TPC], BF16)
        mktap("eG0", [128, TPC], BF16)
        mktap("S0", [128, 128], F32)
        mktap("At0", [128, 1], F32)
        mktap("oloc", [128, H * TPC], BF16)
        mktap("o", [128, H * TPC], BF16)
        mktap("x2", [128, cfg.ND * TPC], F32)

    def tap(name, src_ap, waits):
        if not dbg or name not in taps:
            return
        tv = em.sy(lambda e: e.dma_start(out=taps[name], in_=src_ap),
                   wait=waits)
        em.v(lambda e: e.tensor_copy(gmx[:], gmx[:]), wait=[("dsy", tv)])
        em.vd()

    em = Emit(nc, st)
    sb = lambda name, shape, dt: st.enter_context(nc.sbuf_tensor(name, shape, dt))

    open_stacks = []

    def _finish(extra_wait=()):
        for s_ in reversed(open_stacks):
            s_.close()
        del open_stacks[:]
        w = list(extra_wait) + [(n, em.now(n)) for n in
                                ("vv", "ss", "mm", "gg", "cc", "dgp")]
        stf = em.sy(lambda e: e.dma_start(
            out=dxout.rearrange("(c p) t -> p c t", p=128),
            in_=x_sb[:].rearrange("p (c t) -> p c t", c=ND)), wait=w)
        em._emit("sync", lambda e: e.nop(), None, 0, [("dsy", stf)])
        em.run()
        st.close()
        return nc

    I2w = (I + 1) // 2
    MBLK = max(ND, NI) * 128
    HSIZE = max(NI * TPC,
                2 * (ND * TPC + 3 * TPC),
                2 * (ND * I2w + 3 * max(I2w, D // 2)) + 2 * MBLK,
                (cfg.H + 11) * TPC + max(ND * 128, TPC) + 2 * TPC,
                max((cfg.H + 11) * TPC + max(ND * 128, TPC),
                    cfg.H * TPC + 3 * ND * D) + 2 * TPC)
    x_sb = sb("x", [128, ND * TPC], F32)
    arH = sb("arH", [128, HSIZE], FP16)
    arB = sb("arB", [128, ND * TPC], BF16)
    ARW = max(TPC, 4 * W)
    ar1 = sb("ar1", [128, ARW], F32)
    ar2 = sb("ar2", [128, ARW], F32)
    ident_bf = sb("ident_bf", [128, 128], BF16)
    ident_f = sb("ident_f", [128, 128], F32)
    triu_f = sb("triu_f", [128, 128], F32)
    triu_h = sb("triu_h", [128, 128], FP16)
    ones_col_f = sb("ones_col_f", [128, 1], F32)
    ones_col_b = sb("ones_col_b", [128, 1], BF16)
    qnw_sb = sb("qnw_s", [1, 128], F32)
    knw_sb = sb("knw_s", [1, 128], F32)
    gnw_sb = sb("gnw_s", [128, ND], F32)
    alog_sb = sb("alog_s", [H, 1], F32)
    dtb_sb = sb("dtb_s", [H, 1], F32)
    negea_sb = sb("negea", [H, 1], F32)
    clip_sb = sb("clip", [1, 2], F32)
    rmask_sb = sb("rmask_s", [1, C.n_cores], F32)
    mrep_sb = sb("mrep", [128, C.n_cores], F32)
    sc_g = sb("sc_g", [128, NI], F32)
    sc_u = sb("sc_u", [128, NI], F32)
    sc_d = sb("sc_d", [128, NB], F32)
    cmbg = sb("cmbg", [128, NI], F32)
    cmbu = sb("cmbu", [128, NI], F32)
    cmb8 = sb("cmb8", [128, NB], F32)
    rq_x = sb("rq_x", [128, 1], F32)
    rq_h = sb("rq_h", [128, 1], F32)
    sclx = sb("sclx", [128, 1], F32)
    sclh = sb("sclh", [128, 1], F32)
    gmx = sb("gmx", [1, 1], F32)
    acol = sb("acol", [128, 1], F32)
    bcol = sb("bcol", [128, 1], F32)
    amx8 = sb("amx8", [128, max(ND, 1)], F32)
    habs = sb("habs", [128, NI * NW], F32)
    fheadT = sb("fheadT", [128, NT * H], F32)
    gprev = sb("gprev", [128, NCH], F32)
    egp = sb("egp", [128, NCH], F32)
    S_f = [sb(f"Sf{i}", [128, 128], F32) for i in range(2)]
    striu_s = sb("striu", [NCH, NCH], F32)
    glt = sb("glt", [NCH, 128], F32)
    egc_f = sb("egcf", [128, NCH], F32)
    _sbf = sb("Sbf0", [128, 128], BF16)
    S_bf = [_sbf, _sbf]
    Aq_sb = sb("Aq", [128, 1], F32)
    Atot = sb("Atot", [128, 1], F32)
    _kh = sb("khT0", [128, 128], BF16)
    khT = [_kh, _kh]
    _pm = sb("pm0", [128, 128], BF16)
    pm_sb = [_pm, _pm]


    PS = [st.enter_context(nc.psum_tensor(f"ps{i}", [128, 512], F32)) for i in range(6)]
    ones_row_f = triu_f[0:1, 0:128]
    arow = ar1[0:1, 0:128]

    ceps = sb("ceps", [128, 1], F32)
    cone = sb("cone", [128, 1], F32)
    cez = sb("cez", [128, 1], F32)
    mc1_sb = sb("mc1", [128, C.n_cores], F32)
    nc.const_aps.register(F32, EPS_LN, ceps[:, 0:1])
    nc.const_aps.register(F32, 1.0, cone[:, 0:1])
    nc.const_aps.register(F32, EPS_Z, cez[:, 0:1])
    cmagic = sb("cmagic", [128, 1], F32)
    cnmagic = sb("cnmagic", [128, 1], F32)
    nc.const_aps.register(F32, MAGIC, cmagic[:, 0:1])
    nc.const_aps.register(F32, -MAGIC, cnmagic[:, 0:1])

    Hb = arH[:, :].bitcast(BF16)
    Hf = arH[:, :].bitcast(F32)
    HF_TOT = HSIZE // 2

    def hfv(o, n):
        assert o + n <= HF_TOT, (o, n)
        return Hf[:, o:o + n]

    def hbv(o, n):
        assert o + n <= HSIZE, (o, n)
        return Hb[:, o:o + n]

    xln = hfv(0, ND * TPC)
    XT = ND * TPC
    row_a = hfv(XT, TPC)[0:1, :]
    row_b = hfv(XT + TPC, TPC)[0:1, :]
    row_c = hfv(XT + 2 * TPC, TPC)[0:1, :]
    I2 = (I + 1) // 2
    wbuf = hfv(0, ND * I2)
    WAC = max(I2, D // 2)
    wacc = hfv(ND * I2, WAC)
    wtmp = [hfv(ND * I2 + WAC + i * WAC, WAC) for i in range(2)]
    wblk = [hbv(2 * (ND * I2 + 3 * WAC) + i * max(ND, NI) * 128,
                max(ND, NI) * 128) for i in range(2)]
    OFF_O = 0
    o_sb = hbv(OFF_O, H * TPC)
    q_pp = [hbv(H * TPC + i * TPC, TPC) for i in range(2)]
    eG_pp = [hbv((H + 2) * TPC + i * TPC, TPC) for i in range(2)]
    k_bf = hbv((H + 4) * TPC, TPC)
    v_tm = hbv((H + 5) * TPC, TPC)
    eg_sb = hbv((H + 6) * TPC, TPC)
    em_sb = hbv((H + 7) * TPC, TPC)
    kd_sb = hbv((H + 8) * TPC, TPC)
    khat_sb = hbv((H + 9) * TPC, TPC)
    stage = hbv((H + 10) * TPC, TPC)
    f_h = arH[:, (H + 10) * TPC:(H + 11) * TPC]
    wproj = hbv((H + 11) * TPC, max(ND * 128, H * ND))
    wproj_B = hbv((H + 11) * TPC + ND * 128, ND * 128) if TPC >= 2 * ND * 128 \
        else wproj
    qg_scr = hbv((H + 11) * TPC, TPC)
    ATT_END_B = (H + 11) * TPC + max(ND * 128, TPC)
    ROW_B = max(ATT_END_B, H * TPC + 3 * ND * D)
    row_att = hfv(ROW_B // 2, TPC)[0:1, :]
    assert ROW_B // 2 + TPC <= HF_TOT
    fhead = hfv(0, TPC)[0:H, :]
    wo_bf = hbv(H * TPC, ND * D)
    wo_stage = hfv((H * TPC + ND * D) // 2, ND * D)

    sq2 = [ar1[:, 0:W], ar1[:, W:2 * W]]
    rep_a = ar1[:, 0:TPC]
    rep_b = ar2[:, 0:TPC]
    scr_u = ar1[:, 0:TPC]
    scr_m = ar2[:, 0:TPC]
    wg_pp = [ar1[:, 0:W].bitcast(BF16), ar1[:, W:2 * W].bitcast(BF16)]
    wu_pp = [ar1[:, 2 * W:3 * W].bitcast(BF16), ar1[:, 3 * W:4 * W].bitcast(BF16)]
    sg_pp = [ar2[:, 0:W], ar2[:, W:2 * W]]
    wd_pp = [ar2[:, 2 * W:4 * W].bitcast(BF16)[:, 0:NB * 128],
             ar2[:, 2 * W:4 * W].bitcast(BF16)[:, NB * 128:2 * NB * 128]]
    hb_pp = [ar1[:, 0:W // 2].bitcast(BF16), ar1[:, W // 2:W].bitcast(BF16)]
    hq_u = ar1[:, 2 * W:3 * W]
    hq_m = ar1[:, 3 * W:4 * W]
    g_scr = ar2[:, 0:TPC]
    fdm = ar1[:, 0:TPC]
    qs_f = ar1[:, 0:TPC]
    kkt = ar2[:, 0:TPC]
    rt_pp = [ar2[:, 0:128], ar2[:, 128:256]]
    rko_pp = [ar2[:, 256:384], ar2[:, 384:512]]
    scorz_pp = [ar2[:, 512:640], ar2[:, 640:768]]
    m8 = ar2[:, 776:808]
    zall = ar2[:, 808:824]
    zf = ar2[:, 824:840]
    sq_pp = [ar2[:, 1184:1440].bitcast(BF16)[:, 0:W],
             ar2[:, 1440:1696].bitcast(BF16)[:, 0:W]]
    rep_g = ar1[:, 0:TPC]
    khT_full = ar2[:, 0:TPC // 2].bitcast(BF16)
    NQ = C.n_cores
    Sq_all = ar2[:, 0:NQ * 128]
    wS_scr = ar2[:, 1024:1024 + NQ * 128]
    fold1 = ar2[:, 0:512]
    fold2 = ar2[:, 512:768]
    Sin_f = ar2[:, 768:896]
    Sin_bf = ar2[:, 896:960].bitcast(BF16)[:, 0:128]
    Aq8 = ar2[:, 960:960 + NQ]
    g1c = ar2[:, 968:968 + NQ]
    e8c = ar2[:, 976:976 + NQ]
    w8c = ar2[:, 984:984 + NQ]
    a8_sb = ar1[0:NQ, 0:128]

    assert NB * 128 * 2 <= 2 * W * 2

    em.sy(lambda e: e.dma_start(out=qnw_sb[:], in_=dqnw[:]))
    em.sy(lambda e: e.dma_start(out=knw_sb[:], in_=dknw[:]))
    em.sy(lambda e: e.dma_start(out=gnw_sb[:], in_=dgnw[:]))
    ld_ab = em.sy(lambda e: e.dma_start(out=alog_sb[:], in_=dalog[:]))
    em.sy(lambda e: e.dma_start(out=dtb_sb[:], in_=ddtb[:]))
    em.sy(lambda e: e.dma_start(out=clip_sb[:, 0:1], in_=dclipx[:]))
    ld_cl = em.sy(lambda e: e.dma_start(out=clip_sb[:, 1:2], in_=dclipd[:]))
    ld_small = em.sy(lambda e: e.dma_start(out=rmask_sb[:], in_=drmask[:]))
    ld_x = em.sy(lambda e: e.dma_start(
        out=x_sb[:].rearrange("p (c t) -> p c t", c=ND),
        in_=dx.rearrange("(c p) t -> p c t", p=128)))

    em.g(lambda e: e.iota(triu_f[:], [[1, 128]], base=0, channel_multiplier=-1,
                          allow_small_or_imprecise_dtypes=True))
    em.gdr()
    em.g(lambda e: e.tensor_scalar(ident_bf[:], triu_f[:], 0.0, None, ALU.is_equal))
    em.gdr()
    em.g(lambda e: e.tensor_scalar(ident_f[:], triu_f[:], 0.0, None, ALU.is_equal))
    em.gdr()
    em.g(lambda e: e.tensor_scalar(triu_h[:], triu_f[:], 0.0, None, ALU.is_ge))
    em.gdr()
    em.g(lambda e: e.tensor_scalar(triu_f[:], triu_f[:], 0.0, None, ALU.is_ge))
    em.gdr()
    em.g(lambda e: e.tensor_tensor(striu_s[:], triu_f[0:NCH, 0:NCH],
                                   ident_f[0:NCH, 0:NCH], ALU.subtract))
    em.gdr()
    em.g(lambda e: e.memset(ones_col_f[:], 1.0))
    em.gdr()
    em.v(lambda e: e.memset(ceps[:], EPS_LN))
    em.vd()
    em.v(lambda e: e.memset(cone[:], 1.0))
    em.vd()
    em.v(lambda e: e.memset(cez[:], EPS_Z))
    em.vd()
    em.v(lambda e: e.memset(cmagic[:], MAGIC))
    em.vd()
    em.v(lambda e: e.memset(cnmagic[:], -MAGIC))
    em.vd()
    GGK = em.g(lambda e: e.memset(ones_col_b[:], 1.0))
    gg_k = GGK

    sA = em.s(lambda e: e.activation(negea_sb[:], alog_sb[:], ACT.Exp),
              wait=[("dsy", ld_ab)])
    em.v(lambda e: e.tensor_scalar(negea_sb[:], negea_sb[:], -1.0, None, ALU.mult),
         wait=[("ss", sA)])
    ss_clip = em.s(lambda e: e.activation(clip_sb[:], clip_sb[:], ACT.Sigmoid),
                   wait=[("dsy", ld_cl)])
    tmr = em.t(lambda e: e.matmul(PS[5][:, 0:C.n_cores], ones_row_f[:],
                                  rmask_sb[:], start=True, stop=True),
               wait=[("dsy", ld_small), ("gg", gg_k)])
    v_mrep = em.v(lambda e: e.tensor_copy(mrep_sb[:], PS[5][:, 0:C.n_cores]),
                  wait=[("mm", tmr)])
    em.vd()
    v_mrep = em.v(lambda e: e.tensor_scalar(mc1_sb[:], mrep_sb[:], -1.0, 1.0,
                                            ALU.mult, ALU.add))

    def gmax_finish(vv_gmx, clip_idx, rq_out, scl_out):
        d1 = em.gd(lambda e: e.dma_start(out=dgin[:], in_=gmx[:]),
                   wait=[("vv", vv_gmx)])
        cc1 = em.cc(lambda e: e.collective_compute(
            "AllReduce", ALU.max, replica_groups=[list(range(C.n_cores))],
            ins=[dgin.ap()], outs=[dgout.ap()]), wait=[("dgp", d1)])
        d2 = em.gd(lambda e: e.dma_start(out=gmx[:], in_=dgout[:]),
                   wait=[("cc", cc1)])
        vC = em.v(lambda e: e.scalar_tensor_tensor(
            gmx[:], gmx[:], 1.0 / 127.0,
            clip_sb[:, clip_idx:clip_idx + 1], ALU.mult, ALU.mult),
            wait=[("dgp", d2), ("ss", ss_clip)])
        tB = em.t(lambda e: e.matmul(PS[4][:, 0:1], ones_row_f[:], gmx[:],
                                     start=True, stop=True),
                  wait=[("vv", vC), ("gg", GGK)])
        em.v(lambda e: e.tensor_copy(scl_out[:], PS[4][:, 0:1]),
             wait=[("mm", tB)])
        em.vd()
        return em.v(lambda e: e.reciprocal(rq_out[:], scl_out[:]))

    def global_scale(ready_vv, clip_idx, rq_out, scl_out):
        tA = em.t(lambda e: e.transpose(PS[4][0:1, 0:128], acol[:], ident_f[:]),
                  wait=[("vv", ready_vv), ("gg", GGK)])
        vB = em.v(lambda e: e.reduce_max(gmx[:], PS[4][0:1, 0:128], AX.X),
                  wait=[("mm", tA)])
        return gmax_finish(vB, clip_idx, rq_out, scl_out)

    def ln_stats(wait_pre):
        wait_pre = list(wait_pre) + [("gg", GGK)]
        for nw in range(NW):
            bank_free = em.now("vv")
            for c in range(ND):
                xsl = x_sb[:, c * TPC + nw * W: c * TPC + nw * W + W]
                ssq = em.s(lambda e, xsl=xsl, c=c: e.activation(
                    sq2[c % 2], xsl, ACT.Square),
                    wait=(wait_pre + [("mm", em.now("mm"))]) if c < 2 else
                    [("mm", em.now("mm"))])
                em.t(lambda e, xsl=xsl, c=c: e.matmul(
                    PS[0][0:1, 0:W], ones_col_f[:], xsl,
                    start=(c == 0), stop=(c == ND - 1)),
                    wait=(wait_pre + [("vv", bank_free)]) if c == 0 else [])
                em.t(lambda e, c=c: e.matmul(
                    PS[1][0:1, 0:W], ones_col_f[:], sq2[c % 2],
                    start=(c == 0), stop=(c == ND - 1)),
                    wait=[("ss", ssq)] + ([("vv", bank_free)] if c == 0 else []))
            mmv = em.now("mm")
            em.v(lambda e, nw=nw: e.tensor_copy(
                row_a[:, nw * W:(nw + 1) * W], PS[0][0:1, 0:W]),
                wait=[("mm", mmv)])
            em.v(lambda e, nw=nw: e.tensor_copy(
                row_b[:, nw * W:(nw + 1) * W], PS[1][0:1, 0:W]),
                wait=[("mm", mmv)])
        em.vd()
        em.v(lambda e: e.tensor_scalar(row_a[:], row_a[:], 1.0 / D, None, ALU.mult))
        em.vd()
        em.v(lambda e: e.tensor_scalar(row_b[:], row_b[:], 1.0 / D, None, ALU.mult))
        em.vd()
        em.v(lambda e: e.tensor_mul(row_c[:], row_a[:], row_a[:]))
        em.vd()
        vv1 = em.v(lambda e: e.tensor_sub(row_b[:], row_b[:], row_c[:]))
        em.s(lambda e: e.activation(row_b[:], row_b[:], ACT.Ln,
                                    bias=EPS_LN), wait=[("vv", vv1)])
        em.sd()
        ss2 = em.s(lambda e: e.activation(row_b[:], row_b[:], ACT.Exp,
                                          scale=-0.5))
        return [("ss", ss2)]

    def replicate_row(row, psum_idx, out_sb, wait_pre):
        last = None
        for nw in range(NW):
            tm = em.t(lambda e, nw=nw: e.matmul(
                PS[psum_idx][:, 0:W], ones_row_f[:],
                row[:, nw * W:(nw + 1) * W], start=True, stop=True),
                wait=list(wait_pre) + [("vv", em.now("vv")), ("gg", GGK)])
            last = em.v(lambda e, nw=nw: e.tensor_copy(
                out_sb[:, nw * W:(nw + 1) * W], PS[psum_idx][:, 0:W]),
                wait=[("mm", tm)])
        return last

    def quant_chain(dst, src, rq_col, u_scr, m_scr, wait):
        em.v(lambda e: e.tensor_scalar(u_scr, src, rq_col[:], MAGIC,
                                       ALU.mult, ALU.add), wait=wait)
        em.vd()
        em.v(lambda e: e.tensor_scalar(m_scr, u_scr, MAGIC, -128.0,
                                       ALU.subtract, ALU.max))
        em.vd()
        return em.v(lambda e: e.tensor_scalar(dst, m_scr, 127.0, None,
                                              ALU.min))

    def ln_quant(wait_pre):
        rs_w = ln_stats(wait_pre)
        vv_mr = em.v(lambda e: e.tensor_mul(row_c[:], row_a[:], row_b[:]),
                     wait=rs_w)
        replicate_row(row_b, 2, rep_a, [("vv", vv_mr)])
        vb = replicate_row(row_c, 3, rep_b, [("vv", vv_mr)])
        for c in range(ND):
            em.v(lambda e, c=c: e.tensor_mul(
                xln[:, c * TPC:(c + 1) * TPC], x_sb[:, c * TPC:(c + 1) * TPC],
                rep_a[:]), wait=[("vv", vb)] if c == 0 else [])
            em.vd()
            em.v(lambda e, c=c: e.tensor_sub(
                xln[:, c * TPC:(c + 1) * TPC], xln[:, c * TPC:(c + 1) * TPC],
                rep_b[:]))
            em.vd()
            em.v(lambda e, c=c: e.tensor_reduce(
                amx8[:, c:c + 1], xln[:, c * TPC:(c + 1) * TPC], AX.X,
                ALU.max, apply_absolute_value=True))
            em.vd()
        vv_am = em.v(lambda e: e.tensor_reduce(
            acol[:], amx8[:, 0:ND], AX.X, ALU.max, apply_absolute_value=True))
        vq = global_scale(vv_am, 0, rq_x, sclx)
        last = None
        for c in range(ND):
            last = quant_chain(arB[:, c * TPC:(c + 1) * TPC],
                               xln[:, c * TPC:(c + 1) * TPC],
                               rq_x, scr_u, scr_m,
                               [("vv", vq)] if c == 0 else [])
        return last

    scmy = sb("scmy", [128, NSC], F32)
    scstg = sb("scstg", [128, R * NSC], F32)

    def wquant_my(src_my, KD, NS, pay_off, sc_off):
        NK = KD // 128
        prev_tr = em.now("mm")
        for s in range(NS):
            dld = em.sy(lambda e, s=s, NK=NK: e.dma_start(
                out=wbuf[:, 0:NK * 128].rearrange("p (c w) -> p c w", c=NK),
                in_=src_my[:, s * 128:(s + 1) * 128]
                .rearrange("(c p) w -> p c w", p=128)),
                wait=[("vv", em.now("vv"))])
            vsteps = {}
            vv0 = None
            for c in range(NK):
                sA = em.s(lambda e, c=c: e.activation(
                    wtmp[c % 2][:, 0:128], wbuf[:, c * 128:(c + 1) * 128],
                    ACT.Abs),
                    wait=[("dsy", dld), ("vv", vsteps.get(c - 2, em.now("vv")))])
                if c == 0:
                    vv0 = em.v(lambda e: e.tensor_copy(
                        wacc[:, 0:128], wtmp[0][:, 0:128]), wait=[("ss", sA)])
                else:
                    em.vd()
                    vv0 = em.v(lambda e, c=c: e.tensor_tensor(
                        wacc[:, 0:128], wacc[:, 0:128], wtmp[c % 2][:, 0:128],
                        ALU.max), wait=[("ss", sA)])
                vsteps[c] = vv0
            sg = sc_off + s
            tj = em.t(lambda e: e.transpose(
                PS[4][:, 0:128], wacc[:, 0:128], ident_f[:]),
                wait=[("vv", em.now("vv")), ("gg", GGK)])
            em.v(lambda e, sg=sg: e.tensor_reduce(
                scmy[:, sg:sg + 1], PS[4][:, 0:128], AX.X, ALU.max),
                wait=[("mm", tj)])
            em.vd()
            vts = em.v(lambda e, sg=sg: e.tensor_scalar(
                scmy[:, sg:sg + 1], scmy[:, sg:sg + 1],
                1.0 / 127.0, None, ALU.mult))
            em.vd()
            em.s(lambda e, sg=sg: e.activation(
                acol[:], scmy[:, sg:sg + 1], ACT.Ln),
                wait=[("vv", vts), ("mm", prev_tr)])
            em.sd()
            sex = em.s(lambda e: e.activation(acol[:], acol[:], ACT.Exp,
                                              scale=-1.0))
            tr = em.t(lambda e: e.transpose(PS[3][0:1, 0:128], acol[:],
                                            ident_f[:]),
                      wait=[("ss", sex), ("gg", GGK)])
            prev_tr = tr
            vrow = em.v(lambda e: e.tensor_copy(arow[:], PS[3][0:1, 0:128]),
                        wait=[("mm", tr)])
            trep = em.t(lambda e: e.matmul(PS[4][:, 0:128], ones_row_f[:],
                                           arow[:], start=True, stop=True),
                        wait=[("vv", vrow)])
            bsl = wbuf[:, 0:NK * 128].rearrange("p (c w) -> p c w", c=NK)
            repb = PS[4][:, 0:128][:, None, :].broadcast_to([128, NK, 128])
            dst3 = wblk[s % 2][:, 0:NK * 128].rearrange("p (c w) -> p c w",
                                                        c=NK)
            nhalf = 2 if NK * 128 > TPC else 1
            kk = NK // nhalf
            vq = None
            for hh in range(nhalf):
                b_h = bsl[:, hh * kk:(hh + 1) * kk, :]
                r_h = repb[:, hh * kk:(hh + 1) * kk, :]
                d_h = dst3[:, hh * kk:(hh + 1) * kk, :]
                u_h = scr_u[:, 0:kk * 128].rearrange("p (c w) -> p c w", c=kk)
                em.v(lambda e, b_h=b_h, r_h=r_h, u_h=u_h: e.tensor_tensor(
                    u_h, b_h, r_h, ALU.mult),
                    wait=[("mm", trep), ("dgp", em.now("dgp"))])
                em.vd()
                vq = em.v(lambda e, u_h=u_h, d_h=d_h: e.tensor_scalar(
                    d_h, u_h, MAGIC, MAGIC, ALU.add, ALU.subtract))
                em.vd()
            em.gd(lambda e, s=s, NK=NK, pay_off=pay_off: e.dma_start(
                out=dwin[pay_off + s * NK * 128:pay_off + (s + 1) * NK * 128, :]
                .rearrange("(c p) w -> p c w", p=128),
                in_=wblk[s % 2][:, 0:NK * 128]
                .rearrange("p (c w) -> p c w", c=NK)), wait=[("vv", vq)])

    wquant_my(dwgm, D, GS, 0, 0)
    wquant_my(dwum, D, GS, GS * D, GS)
    wquant_my(dwdm, I, DS, 2 * GS * D, 2 * GS)
    wsc = em.gd(lambda e: e.dma_start(out=dsci[:], in_=scmy[:]),
                wait=[("vv", em.now("vv"))])
    ag_w = em.cc(lambda e: e.collective_compute(
        "AllGather", ALU.bypass, replica_groups=[list(range(R))],
        ins=[dwin.ap()], outs=[dgath.ap()]),
        wait=[("dgp", em.now("dgp"))])
    ag_s = em.cc(lambda e: e.collective_compute(
        "AllGather", ALU.bypass, replica_groups=[list(range(R))],
        ins=[dsci.ap()], outs=[dsco.ap()]))
    rdsc = None
    for r in range(R):
        rdsc = em.gd(lambda e, r=r: e.dma_start(
            out=scstg[:, r * NSC:(r + 1) * NSC],
            in_=dsco[r * 128:(r + 1) * 128, :]),
            wait=[("cc", ag_s)] if r == 0 else [])
    vv_sc = None
    for r in range(R):
        gw = min(GS, NI - r * GS)
        dw_ = min(DS, NB - r * DS)
        if gw > 0:
            em.v(lambda e, r=r, gw=gw: e.tensor_copy(
                sc_g[:, r * GS:r * GS + gw], scstg[:, r * NSC:r * NSC + gw]),
                wait=[("dgp", rdsc)] if r == 0 else [])
            em.v(lambda e, r=r, gw=gw: e.tensor_copy(
                sc_u[:, r * GS:r * GS + gw],
                scstg[:, r * NSC + GS:r * NSC + GS + gw]))
        if dw_ > 0:
            vv_sc = em.v(lambda e, r=r, dw_=dw_: e.tensor_copy(
                sc_d[:, r * DS:r * DS + dw_],
                scstg[:, r * NSC + 2 * GS:r * NSC + 2 * GS + dw_]))
    em.vd()
    for r in range(R):
        gr = min(GS, NI - r * GS)
        nreal = min(DS, NB - r * DS)
        if gr > 0:
            em.gd(lambda e, r=r, gr=gr: e.dma_start(
                out=dcg[r * GS * D:(r * GS + gr) * D, :],
                in_=dgath[r * PAYR:r * PAYR + gr * D, :]),
                wait=[("cc", ag_w)] if r == 0 else [])
            em.gd(lambda e, r=r, gr=gr: e.dma_start(
                out=dcu[r * GS * D:(r * GS + gr) * D, :],
                in_=dgath[r * PAYR + GS * D:r * PAYR + GS * D + gr * D, :]))
        if nreal > 0:
            em.gd(lambda e, r=r, nreal=nreal: e.dma_start(
                out=dcd[r * DS * I:(r * DS + nreal) * I, :],
                in_=dgath[r * PAYR + 2 * GS * D:
                          r * PAYR + 2 * GS * D + nreal * I, :]))
    wq_done = em.now("dgp")

    vv_intx = ln_quant([("dsy", ld_x), ("vv", em.now("vv"))])
    if stop_stage <= 1:
        return _finish()
    tap("intx", arB[:], [("vv", vv_intx)])
    tap("sclx", clip_sb[:], [("vv", vv_intx)])

    if stop_stage <= 2:
        return _finish()

    def ffn_gateup(wait_intx_vv, px, entry_ss):
        silu_d = {}
        h_d = {}
        gate_done = {}
        up_done = {}
        banksG = [PS[0], PS[1], PS[2], PS[3]]
        banksU = [PS[4], PS[5], px[0], px[1]]
        em.v(lambda e: e.tensor_scalar(cmbg[:], sc_g[:], sclx[:], None,
                                       ALU.mult), wait=[("vv", wait_intx_vv)])
        em.vd()
        vv_cmb = em.v(lambda e: e.tensor_scalar(cmbu[:], sc_u[:], sclx[:],
                                                None, ALU.mult))
        em.vd()
        for j in range(NI):
            wait_w = [("mm", up_done[(j - 2, NW - 1)])] if j >= 2 else \
                [("vv", em.now("vv"))]
            dg = em.sy(lambda e, j=j: e.dma_start(
                out=wg_pp[j % 2][:].rearrange("p (c w) -> p c w", c=ND),
                in_=dcg[j * D:(j + 1) * D, :]
                .rearrange("(c p) w -> p c w", p=128)),
                wait=wait_w + [("dgp", wq_done)])
            du = em.sy(lambda e, j=j: e.dma_start(
                out=wu_pp[j % 2][:].rearrange("p (c w) -> p c w", c=ND),
                in_=dcu[j * D:(j + 1) * D, :]
                .rearrange("(c p) w -> p c w", p=128)))
            for c in range(ND):
                for nw in range(NW):
                    if c == 0:
                        wt = [("dsy", dg), ("vv", wait_intx_vv)]
                        if j >= 1:
                            wt += [("ss", silu_d[(j - 1, nw)])]
                        elif entry_ss:
                            wt += [("ss", entry_ss)]
                    else:
                        wt = []
                    gate_done[(j, nw)] = em.t(
                        lambda e, j=j, nw=nw, c=c: e.matmul(
                            banksG[nw][:, 0:W],
                            wg_pp[j % 2][:, c * 128:(c + 1) * 128],
                            arB[:, c * TPC + nw * W: c * TPC + nw * W + W],
                            start=(c == 0), stop=(c == ND - 1)), wait=wt)
                for nw in range(NW):
                    if c == 0:
                        wt = [("dsy", du)]
                        if j >= 1:
                            wt += [("vv", h_d[(j - 1, nw)])]
                        elif entry_ss:
                            wt += [("ss", entry_ss)]
                    else:
                        wt = []
                    up_done[(j, nw)] = em.t(
                        lambda e, j=j, nw=nw, c=c: e.matmul(
                            banksU[nw][:, 0:W],
                            wu_pp[j % 2][:, c * 128:(c + 1) * 128],
                            arB[:, c * TPC + nw * W: c * TPC + nw * W + W],
                            start=(c == 0), stop=(c == ND - 1)), wait=wt)
            for nw in range(NW):
                gi = j * NW + nw
                if gi >= 2:
                    sg_war = [("vv", h_d[divmod(gi - 2, NW)])]
                else:
                    sg_war = []
                ssx = em.s(lambda e, j=j, nw=nw, gi=gi: e.activation(
                    sg_pp[gi % 2], banksG[nw][:, 0:W], ACT.Silu,
                    scale=cmbg[:, j:j + 1]),
                    wait=[("mm", gate_done[(j, nw)]), ("vv", vv_cmb)] + sg_war)
                silu_d[(j, nw)] = ssx
                hv = em.v(lambda e, j=j, nw=nw, gi=gi: e.scalar_tensor_tensor(
                    arH[:, j * TPC + nw * W: j * TPC + nw * W + W],
                    sg_pp[gi % 2], cmbu[:, j:j + 1], banksU[nw][:, 0:W],
                    ALU.mult, ALU.mult),
                    wait=[("ss", ssx), ("mm", up_done[(j, nw)])])
                h_d[(j, nw)] = hv
                em.g(lambda e, j=j, nw=nw: e.tensor_reduce(
                    habs[0:1, (j * NW + nw):(j * NW + nw) + 1],
                    arH[:, j * TPC + nw * W: j * TPC + nw * W + W],
                    AX.XYZWC, ALU.max, apply_absolute_value=True),
                    wait=[("vv", hv)])
        em.gdr()
        gg_h = em.now("gg")
        vqh_v = em.v(lambda e: e.tensor_reduce(
            gmx[:], habs[0:1, 0:NI * NW], AX.X, ALU.max,
            apply_absolute_value=True), wait=[("gg", gg_h)])
        vqh = gmax_finish(vqh_v, 1, rq_h, sclh)
        return em.v(lambda e: e.tensor_scalar(
            cmb8[:], sc_d[:], sclh[:], 0.5, ALU.mult, ALU.mult),
            wait=[("vv", vqh)])

    def ffn_down(vcmb, px):
        banks = PS + px
        assert NB <= len(banks)
        prev_consumed = {}
        for nw in range(NW):
            epi_prev = em.now("vv")
            consumed = {}
            for i in range(NI):
                buf_war = consumed.get(i - 2, prev_consumed.get(NI - 2 + i, 0))
                dd_q = em.sy if i % 2 == 0 else em.sc
                dd_sem = "dsy" if i % 2 == 0 else "dsc"
                dd = dd_q(lambda e, i=i: e.dma_start(
                    out=wd_pp[i % 2][:].rearrange("p (b w) -> p b w", b=NB),
                    in_=dcd.rearrange("(b k p) w -> b k p w", b=NB, p=128)
                    [:, i, :, :].rearrange("b p w -> p b w")),
                    wait=[("mm", buf_war), ("dgp", wq_done)])
                qv = quant_chain(
                    hb_pp[i % 2][:],
                    arH[:, i * TPC + nw * W: i * TPC + nw * W + W],
                    rq_h, hq_u, hq_m,
                    [("vv", vcmb), ("mm", buf_war),
                     ("vv", epi_prev if i == 0 else 0)])
                last = None
                for b in range(NB):
                    last = em.t(lambda e, i=i, b=b: e.matmul(
                        banks[b][:, 0:W], wd_pp[i % 2][:, b * 128:(b + 1) * 128],
                        hb_pp[i % 2][:], start=(i == 0), stop=(i == NI - 1)),
                        wait=([(dd_sem, dd), ("vv", qv)] +
                              ([("vv", epi_prev)] if i == 0 else []))
                        if b == 0 else [])
                consumed[i] = last
            prev_consumed = consumed
            mm_all = em.now("mm")
            for b in range(NB):
                em.v(lambda e, b=b, nw=nw: e.scalar_tensor_tensor(
                    x_sb[:, b * TPC + nw * W: b * TPC + nw * W + W],
                    banks[b][:, 0:W], cmb8[:, b:b + 1],
                    x_sb[:, b * TPC + nw * W: b * TPC + nw * W + W],
                    ALU.mult, ALU.add), wait=[("mm", mm_all)])
        return em.now("vv")

    psA = ExitStack()
    pxA = [psA.enter_context(nc.psum_tensor(f"pxa{i}", [128, 512], F32))
           for i in range(2)]
    vcmb1 = ffn_gateup(vv_intx, pxA, 0)
    tap("h", arH[:, 0:NI * TPC], [("vv", em.now("vv")), ("gg", em.now("gg"))])
    tap("sclh", clip_sb[:], [("vv", em.now("vv"))])
    vv_x1 = ffn_down(vcmb1, pxA)
    psA.close()
    psB = ExitStack()
    open_stacks.append(psB)
    PB = psB.enter_context(nc.psum_tensor("pb0", [128, 1024], BF16))
    pvx = psB.enter_context(nc.psum_tensor("pvx", [128, 512], F32))

    if stop_stage <= 3:
        return _finish([("vv", vv_x1)])
    tap("x1", x_sb[:], [("vv", vv_x1)])
    rs2_w = ln_stats([("vv", vv_x1)])
    vv_mr2 = em.v(lambda e: e.tensor_mul(row_c[:], row_a[:], row_b[:]),
                  wait=rs2_w)
    replicate_row(row_b, 2, rep_a, [("vv", vv_mr2)])
    vb2 = replicate_row(row_c, 3, rep_b, [("vv", vv_mr2)])
    vv_xln2 = None
    for c in range(ND):
        em.v(lambda e, c=c: e.tensor_mul(
            scr_u[:, 0:TPC] if False else xln[:, c * TPC:(c + 1) * TPC],
            x_sb[:, c * TPC:(c + 1) * TPC], rep_a[:]),
            wait=[("vv", vb2)] if c == 0 else [])
        em.vd()
        vv_xln2 = em.v(lambda e, c=c: e.tensor_sub(
            arB[:, c * TPC:(c + 1) * TPC], xln[:, c * TPC:(c + 1) * TPC],
            rep_b[:]))
        em.vd()

    tap("xln2", arB[:], [("vv", vv_xln2)])
    wa_sb = wproj[:, 0:ND * H]
    dwa_ld = em.gd(lambda e: e.dma_start(
        out=wa_sb[:].rearrange("p (c o) -> p c o", c=ND),
        in_=dwa.rearrange("(c p) o -> p c o", p=128)),
        wait=[("vv", em.now("vv")), ("mm", em.now("mm"))])
    vfh2 = None
    for nw in range(NW):
        tz = None
        for c in range(ND):
            tz = em.t(lambda e, c=c, nw=nw: e.matmul(
                PS[0][0:H, 0:W], wa_sb[:, c * H:(c + 1) * H],
                arB[:, c * TPC + nw * W: c * TPC + nw * W + W],
                start=(c == 0), stop=(c == ND - 1)),
                wait=[("dgp", dwa_ld), ("vv", vv_xln2),
                      ("ss", em.now("ss")), ("vv", em.now("vv"))]
                if c == 0 else [])
        sa = em.s(lambda e, nw=nw: e.activation(
            sq2[0][0:H, :], PS[0][0:H, 0:W], ACT.Abs, bias=dtb_sb[:]),
            wait=[("mm", tz)])
        em.sd()
        em.s(lambda e: e.activation(sq2[0][0:H, :], sq2[0][0:H, :], ACT.Exp,
                                    scale=-1.0))
        em.sd()
        em.s(lambda e: e.activation(sq2[0][0:H, :], sq2[0][0:H, :], ACT.Ln,
                                    bias=1.0))
        em.sd()
        sr = em.s(lambda e, nw=nw: e.activation(
            sq2[1][0:H, :], PS[0][0:H, 0:W], ACT.Relu, bias=dtb_sb[:]))
        em.v(lambda e: e.tensor_add(sq2[0][0:H, :], sq2[0][0:H, :],
                                    sq2[1][0:H, :]), wait=[("ss", sr)])
        em.vd()
        vfh2 = em.v(lambda e, nw=nw: e.tensor_scalar(
            fhead[:, nw * W:(nw + 1) * W], sq2[0][0:H, :], negea_sb[:],
            None, ALU.mult))
    tap("fhead", fhead[:, 0:TPC], [("vv", vfh2)])
    prev_ev = 0
    fhT3 = fheadT.rearrange("p (h t) -> p h t", h=H)
    for nt in range(NT):
        tf = em.t(lambda e, nt=nt: e.transpose(
            PS[1][0:128, 0:H], fhead[:, nt * 128:(nt + 1) * 128],
            ident_f[0:H, 0:H]),
            wait=[("vv", vfh2), ("vv", prev_ev), ("gg", GGK)])
        prev_ev = em.v(lambda e, nt=nt: e.tensor_copy(
            fhT3[:, :, nt:nt + 1], PS[1][0:128, 0:H][:, :, None]),
            wait=[("mm", tf)])
    vv_fhT = em.now("vv")
    sa_fh_last = em.now("ss")

    if stop_stage <= 4:
        return _finish()
    relay_sent = {}

    def correction(h, gate_v, gate_vss, gate_mm, gate_sbf, gate_sv_ss,
                   cur_vkd):
        pp = h % 2
        vqg = em.v(lambda e, pp=pp: e.tensor_tensor(
            qg_scr[:, 0:TPC], q_pp[pp][:], eG_pp[pp][:], ALU.mult),
            wait=[("vv", gate_v), ("ss", gate_vss), ("mm", gate_mm)])
        em.vd()
        d1 = em.gd(lambda e: e.dma_start(
            out=Sq_all.rearrange("p (q w) -> p q w", q=NQ),
            in_=drout.rearrange("(q r) w -> r q w", r=129)[0:128, :, :]),
            wait=[("cc", relay_sent[h]), ("vv", cur_vkd),
                  ("ss", em.now("ss")), ("mm", em.now("mm"))])
        d2 = em.gd(lambda e: e.dma_start(
            out=a8_sb,
            in_=drout.rearrange("(q r) w -> q r w", r=129)[:, 128:129, :]
            .rearrange("q a w -> (q a) w")))
        tqa = em.t(lambda e: e.transpose(
            PS[5][0:128, 0:NQ], a8_sb, ident_f[0:NQ, 0:NQ]),
            wait=[("dgp", d2), ("vv", gate_sbf), ("ss", gate_sv_ss),
                  ("gg", GGK)])
        em.v(lambda e: e.tensor_copy(Aq8[:], PS[5][:, 0:NQ]),
             wait=[("mm", tqa)])
        em.vd()
        em.v(lambda e: e.tensor_tensor(g1c[:], Aq8[:], mrep_sb[:], ALU.mult),
             wait=[("vv", v_mrep)])
        em.vd()
        em.v(lambda e: e.tensor_tensor(g1c[:], g1c[:], mc1_sb[:], ALU.add))
        em.vd()
        em.v(lambda e: e.memset(e8c[:, NQ - 1:NQ], 1.0))
        em.vd()
        for qc in range(NQ - 2, -1, -1):
            em.v(lambda e, qc=qc: e.tensor_mul(
                e8c[:, qc:qc + 1], e8c[:, qc + 1:qc + 2],
                g1c[:, qc + 1:qc + 2]))
            em.vd()
        vw8 = em.v(lambda e: e.tensor_tensor(w8c[:], e8c[:], mrep_sb[:],
                                             ALU.mult))
        em.vd()
        wb = w8c[:, :, None].broadcast_to([128, NQ, 128])
        em.v(lambda e, wb=wb: e.tensor_tensor(
            wS_scr[:, 0:NQ * 128].rearrange("p (q w) -> p q w", q=NQ),
            Sq_all.rearrange("p (q w) -> p q w", q=NQ), wb, ALU.mult),
            wait=[("dgp", d1), ("vv", vw8)])
        em.vd()
        wlen = NQ * 128
        src = wS_scr
        fold_dsts = [fold1, fold2]
        di = 0
        while wlen > 128:
            half = wlen // 2
            dst = Sin_f if half == 128 else fold_dsts[di]
            em.v(lambda e, src=src, dst=dst, half=half: e.tensor_tensor(
                dst[:, 0:half], src[:, 0:half], src[:, half:2 * half],
                ALU.add))
            em.vd()
            src, wlen, di = dst, half, di + 1
        vsb = em.v(lambda e: e.tensor_copy(Sin_bf[:], Sin_f[:, 0:128]))
        em.vd()
        vo = gate_sbf
        tmc = None
        for nw in range(NW):
            tmc = em.t(lambda e, nw=nw: e.matmul(
                PS[4][:, 0:W], Sin_bf[:], qg_scr[:, nw * W:(nw + 1) * W],
                start=True, stop=True),
                wait=[("vv", vsb), ("vv", vqg), ("vv", vo)])
            vo = em.v(lambda e, h=h, nw=nw: e.tensor_tensor(
                o_sb[:, h * TPC + nw * W: h * TPC + nw * W + W],
                o_sb[:, h * TPC + nw * W: h * TPC + nw * W + W],
                PS[4][:, 0:W], ALU.add), wait=[("mm", tmc)])
        return vo, tmc

    prev_sbf = vv_xln2
    prev_oc_last = 0
    prev_skt_last = 0
    prev_scan_td = 0
    prev_corr_mm = em.now("mm")
    prev_corr_v = 0
    prev_khT_mm = 0
    vq_ev = {}
    veg_ev = {}
    vkd_ev = {}
    vf_last_d = 0
    sbf_last = {}
    sv_last_d = {}
    vmm_last = {}

    for h in range(H if n_heads is None else n_heads):
        pp = h % 2
        fbit = f_h
        head_entry_v = max(prev_sbf, prev_corr_v)

        two_buf = wproj_B is not wproj
        wr_ld = em.gd(lambda e, h=h: e.dma_start(
            out=wproj[:, 0:ND * 128].rearrange("p (c o) -> p c o", c=ND),
            in_=dwr[:, h * 128:(h + 1) * 128]
            .rearrange("(c p) o -> p c o", p=128)),
            wait=[("vv", em.now("vv")), ("mm", prev_corr_mm)])
        st_local = {}

        def wload(wdram, gate_mm):
            return em.gd(lambda e, h=h, wdram=wdram: e.dma_start(
                out=wproj_B[:, 0:ND * 128].rearrange("p (c o) -> p c o", c=ND),
                in_=wdram[:, h * 128:(h + 1) * 128]
                .rearrange("(c p) o -> p c o", p=128)),
                wait=[("mm", gate_mm)] +
                ([("dsy", em.now("dsy"))] if dbg else []))

        if two_buf:
            st_local["wq_ld"] = wload(dwq, prev_corr_mm)

        q_mm = {}
        q_silu = {}
        k_mm = {}
        k_silu = {}
        sv_ev = {}

        def emit_qk_group(mm_d, silu_d_, nws, wl_key, qsf_g, bank_war):
            for nw in nws:
                tq = None
                for c in range(ND):
                    if c == 0:
                        wt = [("dgp", st_local[wl_key])] + bank_war(nw)
                    else:
                        wt = []
                    tq = em.t(lambda e, nw=nw, c=c: e.matmul(
                        PS[3 + nw % 2][:, 0:W],
                        wproj_B[:, c * 128:(c + 1) * 128],
                        arB[:, c * TPC + nw * W: c * TPC + nw * W + W],
                        start=(c == 0), stop=(c == ND - 1)), wait=wt)
                mm_d[nw] = tq
                silu_d_[nw] = em.s(lambda e, nw=nw: e.activation(
                    qs_f[:, nw * W:(nw + 1) * W], PS[3 + nw % 2][:, 0:W],
                    ACT.Silu),
                    wait=[("mm", tq)] + ([("vv", qsf_g)] if nw == 0 else []))

        def emit_stats(silu_ev_d, norm_row, dst_bf, prev_t2, prev_vv,
                       sqb, s2_mm, tapme=False):
            em.sd()
            t2p = {}
            rows_v = {}
            for nw in range(NW):
                s2 = em.s(lambda e, nw=nw, sqb=sqb: e.activation(
                    sqb[nw % 2], qs_f[:, nw * W:(nw + 1) * W], ACT.Square),
                    wait=[("mm", t2p.get(nw - 2, prev_t2.get(NW - 2 + nw,
                                                             s2_mm))),
                          ("ss", silu_ev_d[nw])])
                t2p[nw] = em.t(lambda e, nw=nw, sqb=sqb: e.matmul(
                    pvx[0:1, 0:W], ones_col_b[:], sqb[nw % 2],
                    start=True, stop=True),
                    wait=[("ss", s2), ("gg", GGK)] +
                    ([("vv", rows_v[nw - 1])] if nw >= 1 else
                     [("vv", prev_vv)]))
                rows_v[nw] = em.v(lambda e, nw=nw: e.tensor_scalar(
                    row_att[:, nw * W:(nw + 1) * W], pvx[0:1, 0:W],
                    1.0 / 128.0, None, ALU.mult), wait=[("mm", t2p[nw])])
            em.s(lambda e: e.activation(row_att[:], row_att[:], ACT.Ln,
                                        bias=EPS_RMS),
                 wait=[("vv", rows_v[NW - 1])])
            em.sd()
            srr2 = em.s(lambda e: e.activation(row_att[:], row_att[:],
                                               ACT.Exp, scale=-0.5))
            if tapme:
                tap("rsq0", row_att[:], [("ss", srr2)])
            vtt = None
            for nw in range(NW):
                t3 = em.t(lambda e, nw=nw, norm_row=norm_row: e.matmul(
                    pvx[:, 0:W], norm_row[:],
                    row_att[:, nw * W:(nw + 1) * W],
                    start=True, stop=True),
                    wait=[("ss", srr2)] +
                    ([("vv", vtt)] if nw >= 1 else []))
                vtt = em.v(lambda e, nw=nw, dst_bf=dst_bf: e.tensor_tensor(
                    dst_bf[:, nw * W:(nw + 1) * W],
                    qs_f[:, nw * W:(nw + 1) * W], pvx[:, 0:W],
                    ALU.mult), wait=[("mm", t3)])
            em.vd()
            return vtt, t2p

        def emit_v_tiles(nts):
            for nt in nts:
                vb = PS[5] if nt % 2 == 0 else PS[2]
                tv_ = None
                for c in range(ND):
                    if c == 0:
                        wt = [("dgp", st_local["wv_ld"])]
                        if nt < 2:
                            wt += [("vv", head_entry_v), ("ss", prev_oc_last)]
                        else:
                            wt += [("ss", sv_ev[nt - 2])]
                    else:
                        wt = []
                    tv_ = em.t(lambda e, nt=nt, c=c, vb=vb: e.matmul(
                        vb[:, 0:128],
                        arB[:, c * TPC + nt * 128: c * TPC + (nt + 1) * 128],
                        wproj_B[:, c * 128:(c + 1) * 128],
                        start=(c == 0), stop=(c == ND - 1)), wait=wt)
                sv_ev[nt] = em.s(lambda e, nt=nt, vb=vb: e.activation(
                    v_tm[:, nt * 128:(nt + 1) * 128], vb[:, 0:128],
                    ACT.Silu), wait=[("mm", tv_)])
                st_local["tv_last"] = tv_

        def q_bank_war(nw):
            if nw == 0:
                return [("ss", prev_oc_last)]
            if nw == 1:
                return [("vv", head_entry_v)]
            return [("ss", q_silu[nw - 2])]

        def k_bank_war(nw):
            if nw < 2:
                par = [i for i in range(NW) if i % 2 == nw % 2]
                return [("ss", q_silu[par[-1]])]
            return [("ss", k_silu[nw - 2])]

        def do_q_stats():
            vvq, t2q = emit_stats(q_silu, qnw_sb, q_pp[pp], {},
                                  head_entry_v, sq_pp, 0, tapme=(h == 0))
            st_local["vv_q"] = vvq
            st_local["t2q"] = t2q
            if h == 0:
                tap("wp0", wproj_B[:, 0:ND * 128],
                    [("dgp", st_local["wq_ld"])])
                tap("qs0", qs_f[:, 0:TPC], [("ss", em.now("ss"))])

        sched = {
            2: lambda: emit_qk_group(q_mm, q_silu, [0, 1], "wq_ld",
                                     head_entry_v, q_bank_war),
            3: lambda: emit_qk_group(q_mm, q_silu, [2, 3], "wq_ld",
                                     head_entry_v, q_bank_war),
            5: do_q_stats,
            6: lambda: st_local.__setitem__("wk_ld", wload(dwk, q_mm[NW - 1])),
            7: lambda: emit_qk_group(k_mm, k_silu, [0, 1], "wk_ld",
                                     st_local["vv_q"], k_bank_war),
            9: lambda: emit_qk_group(k_mm, k_silu, [2, 3], "wk_ld",
                                     st_local["vv_q"], k_bank_war),
            10: lambda: st_local.__setitem__("wv_ld", wload(dwv, k_mm[NW - 1])),
            11: lambda: emit_v_tiles(range(0, NT // 2)),
            13: lambda: emit_v_tiles(range(NT // 2, NT)),
        } if (two_buf and NT >= 16 and NW == 4) else None

        ev_evac = {}
        stt_ev = {}
        for nt in range(NT):
            pb = nt % 2
            tr_ = None
            for c in range(ND):
                if c == 0:
                    wt = [("dgp", wr_ld)]
                    if nt < 2:
                        wt += [("vv", prev_sbf)]
                        if h == 0:
                            wt += [("vv", vv_fhT), ("ss", sa_fh_last),
                                   ("vv", vv_xln2)]
                    else:
                        wt += [("vv", ev_evac[nt - 2])]
                else:
                    wt = []
                tr_ = em.t(lambda e, nt=nt, c=c, pb=pb: e.matmul(
                    PS[pb][:, 0:128],
                    arB[:, c * TPC + nt * 128: c * TPC + (nt + 1) * 128],
                    wproj[:, c * 128:(c + 1) * 128],
                    start=(c == 0), stop=(c == ND - 1)), wait=wt)
            ev_evac[nt] = em.v(lambda e, pb=pb: e.tensor_copy(
                rt_pp[pb], PS[pb][:, 0:128]), wait=[("mm", tr_)])
            em.vd()
            ssg = em.s(lambda e, pb=pb, nt=nt: e.activation(
                scorz_pp[nt % 2], rt_pp[pb], ACT.Sigmoid),
                wait=[("vv", ev_evac[nt]),
                      ("vv", stt_ev.get(nt - 2, 0))])
            em.v(lambda e, pb=pb: e.max(out=m8[:, 0:8], in_=rt_pp[pb]))
            em.vd()
            em.v(lambda e, pb=pb: e.match_replace(
                out=rko_pp[0], in_to_replace=m8[:, 0:8], in_values=rt_pp[pb],
                imm_value=NEG_BIG))
            em.vd()
            em.v(lambda e: e.max(out=m8[:, 8:16], in_=rko_pp[0]))
            em.vd()
            em.v(lambda e: e.match_replace(
                out=rko_pp[1], in_to_replace=m8[:, 8:16], in_values=rko_pp[0],
                imm_value=NEG_BIG))
            em.vd()
            em.v(lambda e: e.max(out=m8[:, 16:24], in_=rko_pp[1]))
            em.vd()
            em.v(lambda e: e.match_replace(
                out=rko_pp[0], in_to_replace=m8[:, 16:24], in_values=rko_pp[1],
                imm_value=NEG_BIG))
            em.vd()
            em.v(lambda e: e.max(out=m8[:, 24:32], in_=rko_pp[0]))
            em.vd()
            stt_ev[nt] = em.v(lambda e, pb=pb, nt=nt: e.scalar_tensor_tensor(
                fbit[:, nt * 128:(nt + 1) * 128], rt_pp[pb], m8[:, 31:32],
                scorz_pp[nt % 2], ALU.is_ge, ALU.mult,
                accum_out=zall[:, nt:nt + 1]),
                wait=[("ss", ssg)] +
                ([("mm", prev_scan_td)] if nt == 0 else []))
            em.vd()
            if sched and nt in sched:
                sched[nt]()
        r_mm_last = tr_
        if sched is None:
            st_local["wq_ld"] = wload(dwq, r_mm_last)
            emit_qk_group(q_mm, q_silu, list(range(NW)), "wq_ld",
                          head_entry_v, q_bank_war)
            do_q_stats()
            st_local["wk_ld"] = wload(dwk, q_mm[NW - 1])
            emit_qk_group(k_mm, k_silu, list(range(NW)), "wk_ld",
                          st_local["vv_q"], k_bank_war)
            st_local["wv_ld"] = wload(dwv, k_mm[NW - 1])
            emit_v_tiles(range(NT))
        vv_q = st_local["vv_q"]
        vq_ev[h] = vv_q
        if stop_stage == 41:
            return _finish()
        em.s(lambda e: e.activation(zall[:, 0:NT], zall[:, 0:NT], ACT.Ln,
                                    bias=EPS_Z),
             wait=[("vv", stt_ev[NT - 1])])
        em.sd()
        sz2 = em.s(lambda e: e.activation(zall[:, 0:NT], zall[:, 0:NT],
                                          ACT.Exp, scale=-1.0))
        em.v(lambda e, h=h: e.tensor_tensor(
            zf[:, 0:NT], zall[:, 0:NT],
            fhT3[:, h:h + 1, :].rearrange("p a t -> p (a t)"), ALU.mult),
            wait=[("ss", sz2)] + ([("vv", vv_fhT)] if h == 0 else []))
        em.vd()
        vf_ = None
        for nt in range(NT):
            vf_ = em.v(lambda e, nt=nt: e.tensor_scalar(
                fbit[:, nt * 128:(nt + 1) * 128],
                fbit[:, nt * 128:(nt + 1) * 128],
                zf[:, nt:nt + 1], None, ALU.mult))
        vf_last_d = vf_

        vv_g = None
        tg_last = None
        g_ev = {}
        for cc_ in range(NCH):
            gb = PS[cc_ % 2]
            if cc_ < 2:
                wt = [("vv", vf_), ("vv", ev_evac[NT - 2 + cc_])] + \
                    ([("gg", GGK)] if cc_ == 0 else [])
            else:
                wt = [("vv", g_ev[cc_ - 2])]
            tg_last = em.t(lambda e, cc_=cc_, gb=gb: e.matmul(
                gb[:, 0:128], fbit[:, cc_ * 128:(cc_ + 1) * 128],
                triu_h[:], start=True, stop=True), wait=wt)
            vv_g = em.v(lambda e, cc_=cc_, gb=gb: e.tensor_scalar(
                g_scr[:, cc_ * 128:(cc_ + 1) * 128], gb[:, 0:128],
                G_CLAMP, None, ALU.max), wait=[("mm", tg_last)])
            g_ev[cc_] = vv_g
        em.vd()

        sg1 = em.s(lambda e: e.activation(eg_sb[:], g_scr[:], ACT.Exp),
                   wait=[("vv", vv_g)])
        sg2 = em.s(lambda e: e.activation(em_sb[:], g_scr[:], ACT.Exp,
                                          scale=-1.0))
        vgl = em.v(lambda e: e.tensor_copy(
            egp[:],
            g_scr.rearrange("p (c w) -> p c w", c=NCH)[:, :, 127:128]
            .rearrange("p c a -> p (c a)")), wait=[("vv", vv_g)])
        em.vd()
        segc = em.s(lambda e: e.activation(egc_f[:], egp[:], ACT.Exp),
                    wait=[("vv", vgl)])
        tgp1 = em.t(lambda e: e.transpose(
            PS[0][0:NCH, 0:128], egp[:], ident_f[:]),
            wait=[("vv", vgl), ("vv", g_ev[NCH - 2]), ("gg", GGK)])
        vge = em.v(lambda e: e.tensor_copy(glt[:], PS[0][0:NCH, 0:128]),
                   wait=[("mm", tgp1)])
        em.vd()
        tgp2 = em.t(lambda e: e.matmul(
            PS[0][:, 0:NCH], glt[:], striu_s[:], start=True, stop=True),
            wait=[("vv", vge)])
        vgp2 = em.v(lambda e: e.tensor_copy(gprev[:], PS[0][:, 0:NCH]),
                    wait=[("mm", tgp2)])
        em.vd()
        em.sd()
        sg4 = em.s(lambda e: e.activation(egp[:], gprev[:], ACT.Exp),
                   wait=[("vv", vgp2)])
        va_ = em.v(lambda e: e.tensor_tensor(
            acol[:], gprev[:, NCH - 1:NCH], g_scr[:, TPC - 1:TPC], ALU.add))
        sat = em.s(lambda e: e.activation(Atot[:], acol[:], ACT.Exp),
                   wait=[("vv", va_)])
        veg_last = None
        for c_ in range(NCH):
            veg_last = em.s(lambda e, c_=c_, pp=pp: e.activation(
                eG_pp[pp][:, c_ * 128:(c_ + 1) * 128],
                eg_sb[:, c_ * 128:(c_ + 1) * 128], ACT.Copy,
                scale=egp[:, c_:c_ + 1]),
                wait=([("ss", sg4), ("vv", prev_corr_v)] if c_ == 0 else []))
        veg_ev[h] = veg_last

        sq2k = [stage[:, 0:W], stage[:, W:2 * W]] if TPC >= 2 * W else sq_pp
        vv_k, _ = emit_stats(k_silu, knw_sb, k_bf, {}, vv_q, sq2k,
                             tg_last if sq2k is not sq_pp else
                             st_local["t2q"][NW - 1])
        sv_last = sv_ev[NT - 1]
        sv_last_d[h] = sv_last
        vmm_last[h] = st_local["tv_last"]
        if stop_stage in (42, 43):
            return _finish()

        em.v(lambda e: e.tensor_sub(fdm[:, 1:TPC], g_scr[:, 1:TPC],
                                    g_scr[:, 0:TPC - 1]), wait=[("vv", vv_g)])
        em.vd()
        vfdm = em.v(lambda e: e.tensor_copy(
            fdm.rearrange("p (c w) -> p c w", c=NCH)[:, :, 0:1],
            g_scr.rearrange("p (c w) -> p c w", c=NCH)[:, :, 0:1]))
        em.vd()
        sg3 = em.s(lambda e: e.activation(fdm[:], fdm[:], ACT.Exp),
                   wait=[("vv", vfdm)])
        vs_ = em.v(lambda e: e.tensor_scalar(fdm[:], fdm[:], -1.0, 1.0,
                                             ALU.mult, ALU.add),
                   wait=[("ss", sg3)])
        em.vd()
        em.v(lambda e: e.tensor_tensor(kkt[:], fdm[:], k_bf[:], ALU.mult),
             wait=[("ss", sg2)])
        em.vd()
        vkd = em.v(lambda e: e.tensor_tensor(kd_sb[:], kkt[:], em_sb[:],
                                             ALU.mult))
        em.vd()
        vkd_ev[h] = vkd
        khat_s = {}
        for c_ in range(NCH):
            khat_s[c_] = em.s(lambda e, c_=c_: e.activation(
                khat_sb[:, c_ * 128:(c_ + 1) * 128],
                kd_sb[:, c_ * 128:(c_ + 1) * 128], ACT.Copy,
                scale=egc_f[:, c_:c_ + 1]),
                wait=([("vv", vkd), ("mm", prev_khT_mm)] if c_ == 0 else []))
        if stop_stage == 44:
            return _finish()

        if h == 0:
            tap("q0", q_pp[pp][:], [("vv", vv_q)])
            tap("k0", k_bf[:], [("vv", vv_k)])
            tap("v0", v_tm[:], [("ss", sv_last)])
            tap("eg0", eg_sb[:], [("ss", sg1), ("vv", em.now("vv"))])
            tap("eG0", eG_pp[pp][:], [("ss", veg_ev[h])])
        vqd = em.v(lambda e, pp=pp: e.tensor_tensor(
            stage[:, 0:TPC], q_pp[pp][:], eg_sb[:], ALU.mult),
            wait=[("vv", vv_q), ("mm", tg_last)])
        em.vd()

        RK = min(8, NCH)
        khT_rnd = {}
        tk = None
        for r_ in range(NCH // RK):
            for ci in range(RK):
                c = r_ * RK + ci
                wt = [("ss", khat_s[c])]
                if ci == 0:
                    wt += [("gg", GGK)]
                    wt += ([("ss", khT_rnd[r_ - 1])] if r_ >= 1 else
                           [("ss", prev_skt_last)])
                tk = em.t(lambda e, c=c, ci=ci: e.transpose(
                    PB[:, ci * 128:(ci + 1) * 128],
                    khat_sb[:, c * 128:(c + 1) * 128], ident_bf[:]), wait=wt)
            khT_rnd[r_] = em.s(lambda e, r_=r_, RK=RK: e.copy(
                khT_full[:, r_ * RK * 128:(r_ + 1) * RK * 128],
                PB[:, 0:RK * 128]),
                wait=[("mm", tk)] + ([("vv", vkd)] if r_ == 0 else []))
        prev_khT_mm = tk
        if stop_stage == 46:
            return _finish()

        em.v(lambda e: e.memset(S_f[0][:], 0.0))
        em.vd()
        sbt_prev = em.v(lambda e: e.memset(S_bf[0][:], 0.0))
        em.vd()
        sbf_f = {}
        oc_ev = {}
        vpm_ev = {}
        s_idx = 0
        for c in range(NCH):
            pb = c % 2
            if c >= 2:
                wt_tp = [("vv", vpm_ev[c - 2])]
            else:
                wt_tp = []
            tp = em.t(lambda e, c=c, pb=pb: e.matmul(
                PS[pb][:, 0:128], kd_sb[:, c * 128:(c + 1) * 128],
                stage[:, c * 128:(c + 1) * 128], start=True, stop=True),
                wait=[("vv", vkd), ("vv", vqd)] + wt_tp)
            vpm_ev[c] = em.v(lambda e, pb=pb: e.tensor_tensor(
                pm_sb[pb][:], PS[pb][:, 0:128], triu_f[:], ALU.mult),
                wait=[("mm", tp), ("gg", GGK)])
            if c >= 2:
                wt_o1 = [("ss", oc_ev[c - 2])]
            elif pb == 0:
                wt_o1 = [("ss", sv_ev[NT - 1])]
            else:
                wt_o1 = [("vv", vv_k)]
            to1 = em.t(lambda e, c=c, pb=pb: e.matmul(
                PS[2 + pb][:, 0:128], v_tm[:, c * 128:(c + 1) * 128],
                pm_sb[pb][:], start=True, stop=False),
                wait=[("vv", vpm_ev[c]), ("ss", sv_ev[c])] + wt_o1)
            to2 = em.t(lambda e, c=c, pb=pb, s_idx=s_idx: e.matmul(
                PS[2 + pb][:, 0:128], S_bf[s_idx][:],
                stage[:, c * 128:(c + 1) * 128], start=False, stop=True),
                wait=[("vv", sbt_prev)])
            oc_ev[c] = em.s(lambda e, h=h, c=c, pb=pb: e.copy(
                o_sb[:, h * TPC + c * 128: h * TPC + (c + 1) * 128],
                PS[2 + pb][:, 0:128]), wait=[("mm", to2)])
            if c >= 2:
                wt_td = [("vv", sbf_f[c - 2])]
            elif pb == 0:
                wt_td = [("vv", vv_k)]
            else:
                wt_td = [("ss", sv_last)]
            td = em.t(lambda e, c=c, pb=pb: e.matmul(
                PS[4 + pb][:, 0:128], khT_full[:, c * 128:(c + 1) * 128],
                v_tm[:, c * 128:(c + 1) * 128], start=True, stop=True),
                wait=[("ss", khT_rnd[c // RK]), ("ss", sv_ev[c])] + wt_td)
            ns = 1 - s_idx
            aC = eg_sb[:, c * 128 + 127: c * 128 + 128]
            em.vd()
            sbt_prev = em.v(lambda e, pb=pb, s_idx=s_idx, aC=aC:
                            e.scalar_tensor_tensor(
                                S_bf[0][:], S_f[s_idx][:], aC,
                                PS[4 + pb][:, 0:128], ALU.mult, ALU.add),
                            wait=[("mm", td)])
            sbf_f[c] = em.v(lambda e, s_idx=s_idx, ns=ns, pb=pb, aC=aC:
                            e.scalar_tensor_tensor(
                                S_f[ns][:], S_f[s_idx][:], aC,
                                PS[4 + pb][:, 0:128], ALU.mult, ALU.add))
            s_idx = ns
        sbf_last[h] = sbf_f[NCH - 1]
        if stop_stage == 45:
            return _finish()

        if h == 0:
            tap("S0", S_f[s_idx][:], [("vv", sbf_f[NCH - 1])])
            tap("At0", Atot[:], [("ss", sat)])
            tap("oloc", o_sb[:], [("ss", em.now("ss")), ("vv", em.now("vv"))])
        if h > 0 and do_corr:
            prev_corr_v, prev_corr_mm = correction(
                h - 1, vq_ev[h - 1], veg_ev[h - 1], vmm_last[h],
                sbf_f[NCH - 1], sv_last, sbf_f[NCH - 1])

        if do_relay:
            d1r = em.gd(lambda e, s_idx=s_idx: e.dma_start(
                out=drin[0:128, :], in_=S_f[s_idx][:]),
                wait=[("vv", sbf_f[NCH - 1]),
                      ("cc", relay_sent.get(h - 1, 0))])
            d2r = em.gd(lambda e: e.dma_start(
                out=drin[128:129, :].rearrange("a p -> p a"),
                in_=Atot[:]), wait=[("ss", sat)])
            relay_sent[h] = em.cc(lambda e: e.collective_compute(
                "AllGather", ALU.bypass,
                replica_groups=[list(range(C.n_cores))],
                ins=[drin.ap()], outs=[drout.ap()]), wait=[("dgp", d2r)])

        prev_sbf = sbf_f[NCH - 1]
        prev_oc_last = oc_ev[NCH - 1]
        prev_skt_last = khT_rnd[NCH // RK - 1]
        prev_scan_td = td

    if not (do_relay and do_corr):
        return _finish()
    hh_ = (H if n_heads is None else n_heads) - 1
    vv_o, _ = correction(hh_, vq_ev[hh_], veg_ev[hh_], em.now("mm"),
                         sbf_last[hh_], sv_last_d[hh_], em.now("vv"))
    if stop_stage <= 5:
        return _finish()

    tap("o", o_sb[:], [("vv", vv_o)])
    dwol = em.sy(lambda e: e.dma_start(
        out=wo_stage[:].rearrange("p (c o) -> p c o", c=ND),
        in_=dwo.rearrange("(c p) o -> p c o", p=128)),
        wait=[("vv", em.now("vv"))])
    vv_wo = None
    for c in range(ND):
        vv_wo = em.v(lambda e, c=c: e.tensor_scalar(
            wo_bf[:, c * D:(c + 1) * D], wo_stage[:, c * D:(c + 1) * D],
            gnw_sb[:, c:c + 1], None, ALU.mult), wait=[("dsy", dwol)])
    ost = [ar2[:, W:W + W // 2].bitcast(BF16),
           ar2[:, W + W // 2:2 * W].bitcast(BF16)]
    row_ev = {}
    ot_ev = {}
    for nw in range(NW):
        for hh in range(H):
            k_ = nw * H + hh
            s2 = em.s(lambda e, hh=hh, nw=nw: e.activation(
                ost[hh % 2][:, 0:W],
                o_sb[:, hh * TPC + nw * W: hh * TPC + nw * W + W], ACT.Square),
                wait=([("vv", vv_o)] if k_ < 2 else []) +
                [("mm", ot_ev.get(k_ - 2, 0))])
            ot_ev[k_] = em.t(lambda e, hh=hh, nw=nw: e.matmul(
                pvx[0:1, 0:W], ones_col_b[:], ost[hh % 2][:, 0:W],
                start=(hh == 0), stop=(hh == H - 1)),
                wait=[("ss", s2), ("gg", GGK)] +
                ([("vv", row_ev.get(nw - 1, vv_o))] if hh == 0 else []))
        mmo = em.now("mm")
        row_ev[nw] = em.v(lambda e, nw=nw: e.tensor_copy(
            row_att[:, nw * W:(nw + 1) * W], pvx[0:1, 0:W]),
            wait=[("mm", mmo)])
    em.vd()
    em.v(lambda e: e.tensor_scalar(row_att[:], row_att[:], 1.0 / D, None,
                                   ALU.mult))
    em.s(lambda e: e.activation(row_att[:], row_att[:], ACT.Ln,
                                bias=EPS_RMS),
         wait=[("vv", em.now("vv"))])
    em.sd()
    sro2 = em.s(lambda e: e.activation(row_att[:], row_att[:], ACT.Exp,
                                       scale=-0.5))
    vrg = replicate_row(row_att, 4, rep_g, [("ss", sro2)])
    scrm_ev = {}
    for dout in range(ND):
        for nw in range(NW):
            k_ = dout * NW + nw
            pbk = k_ % 2
            tw = None
            for hh in range(H):
                tw = em.t(lambda e, dout=dout, nw=nw, hh=hh, pbk=pbk: e.matmul(
                    PS[pbk][:, 0:W],
                    wo_bf[:, hh * D + dout * 128: hh * D + dout * 128 + 128],
                    o_sb[:, hh * TPC + nw * W: hh * TPC + nw * W + W],
                    start=(hh == 0), stop=(hh == H - 1)),
                    wait=[("vv", vv_wo), ("vv", vv_o),
                          ("vv", scrm_ev.get(k_ - 2, 0))]
                    if hh == 0 else [])
            scrm_ev[k_] = em.v(lambda e, pbk=pbk, nw=nw: e.tensor_tensor(
                scr_m[:, 0:W], PS[pbk][:, 0:W],
                rep_g[:, nw * W:(nw + 1) * W], ALU.mult),
                wait=[("mm", tw), ("vv", vrg)])
            em.vd()
            em.v(lambda e, dout=dout, nw=nw: e.tensor_tensor(
                x_sb[:, dout * TPC + nw * W: dout * TPC + nw * W + W],
                x_sb[:, dout * TPC + nw * W: dout * TPC + nw * W + W],
                scr_m[:, 0:W], ALU.add))
            em.vd()
    attn_done = em.now("vv")

    if stop_stage <= 6:
        return _finish([("vv", attn_done)])
    tap("x2", x_sb[:], [("vv", attn_done)])
    psB.close()
    open_stacks.remove(psB)
    vv_intx2 = ln_quant([("vv", attn_done)])
    psC = ExitStack()
    pxC = [psC.enter_context(nc.psum_tensor(f"pxc{i}", [128, 512], F32))
           for i in range(2)]
    vcmb2 = ffn_gateup(vv_intx2, pxC, em.now("ss"))
    vv_x2 = ffn_down(vcmb2, pxC)
    psC.close()

    return _finish([("vv", vv_x2)])


def _shard_w(wT, nblk, nslot, R):
    outs = []
    rows = wT.shape[0]
    for r in range(R):
        m = np.ones((rows, nslot * 128), np.float32)
        real = max(0, min(nslot, nblk - r * nslot))
        if real:
            m[:, :real * 128] = wT[:, r * nslot * 128:(r * nslot + real) * 128]
        outs.append(np.ascontiguousarray(m))
    return outs


def kernel(L0_attn_hidden_states, L0_attn_Wq, L0_attn_Wk, L0_attn_Wv,
           L0_attn_Wr, L0_attn_Wa, L0_attn_A_log, L0_attn_dt_bias,
           L0_attn_q_norm_w, L0_attn_k_norm_w, L0_attn_g_norm_w, L0_attn_Wo,
           L0_ffn_clip_x, L0_ffn_clip_down, L0_ffn_w_gate, L0_ffn_w_up,
           L0_ffn_w_down):
    cfg = Cfg()
    B, T, Dm = L0_attn_hidden_states.shape
    n = cfg.n_cores
    cpb = cfg.cores_per_batch
    TPC = cfg.TPC
    assert B * T == n * TPC and T == cpb * TPC and Dm == cfg.D

    x = np.ascontiguousarray(L0_attn_hidden_states, dtype=np.float32)
    com = {
        "wq": np.ascontiguousarray(L0_attn_Wq, np.float32),
        "wk": np.ascontiguousarray(L0_attn_Wk, np.float32),
        "wv": np.ascontiguousarray(L0_attn_Wv, np.float32),
        "wr": np.ascontiguousarray(L0_attn_Wr, np.float32),
        "wo": np.ascontiguousarray(L0_attn_Wo, np.float32),
        "wa": np.ascontiguousarray(L0_attn_Wa, np.float32),
        "alog": np.ascontiguousarray(
            np.reshape(L0_attn_A_log, (cfg.H, 1)), np.float32),
        "dtb": np.ascontiguousarray(
            np.reshape(L0_attn_dt_bias, (cfg.H, 1)), np.float32),
        "qnw": np.ascontiguousarray(
            np.reshape(L0_attn_q_norm_w, (1, 128)), np.float32),
        "knw": np.ascontiguousarray(
            np.reshape(L0_attn_k_norm_w, (1, 128)), np.float32),
        "gnw": np.ascontiguousarray(
            np.reshape(L0_attn_g_norm_w, (cfg.ND, 128)).T, np.float32),
        "clipx": np.ascontiguousarray(
            np.reshape(L0_ffn_clip_x, (1, 1)), np.float32),
        "clipd": np.ascontiguousarray(
            np.reshape(L0_ffn_clip_down, (1, 1)), np.float32),
    }
    wgt = np.ascontiguousarray(np.transpose(L0_ffn_w_gate), np.float32)
    wut = np.ascontiguousarray(np.transpose(L0_ffn_w_up), np.float32)
    wdt = np.ascontiguousarray(np.transpose(L0_ffn_w_down), np.float32)
    wgm = _shard_w(wgt, cfg.NI, cfg.GS, n)
    wum = _shard_w(wut, cfg.NI, cfg.GS, n)
    wdm = _shard_w(wdt, cfg.NHB, cfg.DS, n)
    in_maps = []
    for c in range(n):
        b = c // cpb
        pos = c % cpb
        t0 = pos * TPC
        mask = np.zeros((1, n), np.float32)
        mask[0, b * cpb:b * cpb + pos] = 1.0
        m = dict(com)
        m["xt"] = np.ascontiguousarray(x[b, t0:t0 + TPC, :].T)
        m["rmask"] = mask
        m["wgm"] = wgm[c]
        m["wum"] = wum[c]
        m["wdm"] = wdm[c]
        in_maps.append(m)

    nc = build(cfg)
    res = run_bass_kernel_spmd(nc, in_maps, core_ids=list(range(n)))
    out = np.empty((B, T, Dm), np.float32)
    for c in range(n):
        b = c // cpb
        t0 = (c % cpb) * TPC
        out[b, t0:t0 + TPC, :] = np.asarray(res.results[c]["xout"]).T
    return out


# version 25 (final)
# speedup vs baseline: 1.0363x; 1.0130x over previous
import numpy as np
from contextlib import ExitStack
import concourse.bass as bass
import concourse.mybir as mybir
from concourse.bass_utils import run_bass_kernel_spmd

F32 = mybir.dt.float32
BF16 = mybir.dt.bfloat16
FP16 = mybir.dt.float16
ALU = mybir.AluOpType
ACT = mybir.ActivationFunctionType
AX = mybir.AxisListType

NEG_BIG = -1.0e30
MAGIC = 12582912.0
EPS_LN = 1.0e-5
EPS_RMS = 1.0e-5
EPS_Z = 1.0e-9
G_CLAMP = -60.0


class Cfg:
    def __init__(self, n_cores=8, cores_per_batch=4, TPC=2048, D=1024, I=2816,
                 H=8, CH=128):
        self.n_cores = n_cores
        self.cores_per_batch = cores_per_batch
        self.TPC = TPC
        self.D = D
        self.I = I
        self.H = H
        self.CH = CH
        self.ND = D // 128
        self.NI = I // 128
        self.NHB = D // 128
        self.NT = TPC // 128
        self.NCH = TPC // CH
        self.W = min(512, TPC)
        self.NW = TPC // self.W
        self.GS = -(-self.NI // n_cores)
        self.DS = -(-self.NHB // n_cores)


class Emit:
    ENGS = ("sync", "tensor", "vector", "scalar", "gpsimd")

    def __init__(self, nc, stack):
        self.nc = nc
        self.q = {e: [] for e in self.ENGS}
        self.sem = {}
        self.c = {}
        for name in ("dsy", "dgp", "dsc", "mm", "vv", "ss", "gg", "cc"):
            self.sem[name] = stack.enter_context(nc.semaphore(name))
            self.c[name] = 0

    def _emit(self, eng, fn, inc=None, inck=1, wait=()):
        if inc is not None:
            self.c[inc] += inck
        wait = tuple((n, v) for (n, v) in wait if v > 0)
        sem = self.sem

        def body(e, fn=fn, inc=inc, inck=inck, wait=wait):
            for (n, v) in wait:
                e.wait_ge(sem[n], v)
            instr = fn(e)
            if inc is not None:
                instr.then_inc(sem[inc], inck)
        self.q[eng].append(body)
        return self.c[inc] if inc is not None else None

    def sy(self, fn, wait=()):
        w = [("dsy", self.c["dsy"])] + list(wait)
        return self._emit("sync", fn, "dsy", 16, w)

    def gd(self, fn, wait=()):
        w = [("dgp", self.c["dgp"])] + list(wait)
        return self._emit("gpsimd", fn, "dgp", 16, w)

    def sc(self, fn, wait=()):
        w = [("dsc", self.c["dsc"])] + list(wait)
        return self._emit("scalar", fn, "dsc", 16, w)

    def t(self, fn, wait=()):
        return self._emit("tensor", fn, "mm", 1, wait)

    def v(self, fn, wait=()):
        return self._emit("vector", fn, "vv", 1, wait)

    def s(self, fn, wait=()):
        return self._emit("scalar", fn, "ss", 1, wait)

    def g(self, fn, wait=()):
        return self._emit("gpsimd", fn, "gg", 1, wait)

    def cc(self, fn, wait=()):
        return self._emit("gpsimd", fn, "cc", 1, wait)

    def vd(self):
        self.q["vector"].append(lambda v: v.drain())

    def sd(self):
        self.q["scalar"].append(lambda s: s.drain())

    def gdr(self):
        self.q["gpsimd"].append(lambda g: g.drain())

    def now(self, name):
        return self.c[name]

    def run(self):
        with self.nc.Block() as b:
            for eng in self.ENGS:
                fns = self.q[eng]
                if not fns:
                    continue
                getattr(b, eng)(lambda e, fns=fns: [f(e) for f in fns])


def build(cfg: Cfg, dbg=False, stop_stage=99, n_heads=None, do_relay=True,
          do_corr=True):
    nc = bass.Bass("TRN2", target_bir_lowering=False, debug=False)
    st = ExitStack()
    C = cfg
    TPC, D, I, H = C.TPC, C.D, C.I, C.H
    ND, NI, NT, NCH, NW, W, NB = C.ND, C.NI, C.NT, C.NCH, C.NW, C.W, C.NHB

    dx = nc.dram_tensor("xt", [D, TPC], F32, kind="ExternalInput").ap()
    dwq = nc.dram_tensor("wq", [D, D], F32, kind="ExternalInput").ap()
    dwk = nc.dram_tensor("wk", [D, D], F32, kind="ExternalInput").ap()
    dwv = nc.dram_tensor("wv", [D, D], F32, kind="ExternalInput").ap()
    dwr = nc.dram_tensor("wr", [D, D], F32, kind="ExternalInput").ap()
    dwo = nc.dram_tensor("wo", [D, D], F32, kind="ExternalInput").ap()
    dwa = nc.dram_tensor("wa", [D, H], F32, kind="ExternalInput").ap()
    dalog = nc.dram_tensor("alog", [H, 1], F32, kind="ExternalInput").ap()
    ddtb = nc.dram_tensor("dtb", [H, 1], F32, kind="ExternalInput").ap()
    dqnw = nc.dram_tensor("qnw", [1, 128], F32, kind="ExternalInput").ap()
    dknw = nc.dram_tensor("knw", [1, 128], F32, kind="ExternalInput").ap()
    dgnw = nc.dram_tensor("gnw", [128, ND], F32, kind="ExternalInput").ap()
    dclipx = nc.dram_tensor("clipx", [1, 1], F32, kind="ExternalInput").ap()
    dclipd = nc.dram_tensor("clipd", [1, 1], F32, kind="ExternalInput").ap()
    dwgm = nc.dram_tensor("wgm", [D, C.GS * 128], F32,
                          kind="ExternalInput").ap()
    dwum = nc.dram_tensor("wum", [D, C.GS * 128], F32,
                          kind="ExternalInput").ap()
    dwdm = nc.dram_tensor("wdm", [I, C.DS * 128], F32,
                          kind="ExternalInput").ap()
    drmask = nc.dram_tensor("rmask", [1, C.n_cores], F32, kind="ExternalInput").ap()
    dxout = nc.dram_tensor("xout", [D, TPC], F32, kind="ExternalOutput").ap()

    GS, DS, R = C.GS, C.DS, C.n_cores
    PAYR = 2 * GS * D + DS * I
    NSC = 2 * GS + DS
    dsci = nc.dram_tensor("scin", [128, NSC], F32)
    dsco = nc.dram_tensor("scout", [R * 128, NSC], F32)
    dwin = nc.dram_tensor("wqin", [PAYR, 128], BF16)
    dgath = nc.dram_tensor("wqgath", [R * PAYR, 128], BF16)
    dcd = nc.dram_tensor("cache_d", [NB * I, 128], BF16)
    dcg = nc.dram_tensor("cache_g", [NI * D, 128], BF16)
    dcu = nc.dram_tensor("cache_u", [NI * D, 128], BF16)
    drin = nc.dram_tensor("rin", [129, 128], F32)
    drout = nc.dram_tensor("rout", [C.n_cores * 129, 128], F32)
    dgin = nc.dram_tensor("gin", [1, 1], F32)
    dgout = nc.dram_tensor("gout", [1, 1], F32)
    taps = {}
    if dbg:
        def mktap(name, shape, dt):
            taps[name] = nc.dram_tensor("tap_" + name, shape, dt,
                                        kind="ExternalOutput").ap()
        mktap("intx", [128, cfg.ND * TPC], BF16)
        mktap("sclx", [1, 2], F32)
        mktap("h", [128, cfg.NI * TPC], FP16)
        mktap("sclh", [1, 2], F32)
        mktap("x1", [128, cfg.ND * TPC], F32)
        mktap("xln2", [128, cfg.ND * TPC], BF16)
        mktap("fhead", [H, TPC], F32)
        mktap("f0", [128, cfg.NT * 128], FP16)
        mktap("q0", [128, TPC], BF16)
        mktap("wp0", [128, cfg.ND * 128], BF16)
        mktap("qs0", [128, TPC], F32)
        mktap("rsq0", [1, TPC], F32)
        mktap("k0", [128, TPC], BF16)
        mktap("v0", [128, cfg.NT * 128], BF16)
        mktap("eg0", [128, TPC], BF16)
        mktap("eG0", [128, TPC], BF16)
        mktap("S0", [128, 128], F32)
        mktap("At0", [128, 1], F32)
        mktap("oloc", [128, H * TPC], BF16)
        mktap("o", [128, H * TPC], BF16)
        mktap("x2", [128, cfg.ND * TPC], F32)

    def tap(name, src_ap, waits):
        if not dbg or name not in taps:
            return
        tv = em.sy(lambda e: e.dma_start(out=taps[name], in_=src_ap),
                   wait=waits)
        em.v(lambda e: e.tensor_copy(gmx[:], gmx[:]), wait=[("dsy", tv)])
        em.vd()

    em = Emit(nc, st)
    sb = lambda name, shape, dt: st.enter_context(nc.sbuf_tensor(name, shape, dt))

    open_stacks = []

    def _finish(extra_wait=()):
        for s_ in reversed(open_stacks):
            s_.close()
        del open_stacks[:]
        w = list(extra_wait) + [(n, em.now(n)) for n in
                                ("vv", "ss", "mm", "gg", "cc", "dgp")]
        stf = em.sy(lambda e: e.dma_start(
            out=dxout.rearrange("(c p) t -> p c t", p=128),
            in_=x_sb[:].rearrange("p (c t) -> p c t", c=ND)), wait=w)
        em._emit("sync", lambda e: e.nop(), None, 0, [("dsy", stf)])
        em.run()
        st.close()
        return nc

    I2w = (I + 1) // 2
    MBLK = max(ND, NI) * 128
    HSIZE = max(NI * TPC,
                2 * (ND * TPC + 3 * TPC),
                2 * (ND * I2w + 3 * max(I2w, D // 2)) + 2 * MBLK,
                (cfg.H + 11) * TPC + max(ND * 128, TPC) + 2 * TPC,
                max((cfg.H + 11) * TPC + max(ND * 128, TPC),
                    cfg.H * TPC + 3 * ND * D) + 2 * TPC)
    x_sb = sb("x", [128, ND * TPC], F32)
    arH = sb("arH", [128, HSIZE], FP16)
    arB = sb("arB", [128, ND * TPC], BF16)
    ARW = max(TPC, 4 * W)
    ar1 = sb("ar1", [128, ARW], F32)
    ar2 = sb("ar2", [128, ARW], F32)
    ident_bf = sb("ident_bf", [128, 128], BF16)
    ident_f = sb("ident_f", [128, 128], F32)
    triu_f = sb("triu_f", [128, 128], F32)
    triu_h = sb("triu_h", [128, 128], FP16)
    ones_col_f = sb("ones_col_f", [128, 1], F32)
    ones_col_b = sb("ones_col_b", [128, 1], BF16)
    qnw_sb = sb("qnw_s", [1, 128], F32)
    knw_sb = sb("knw_s", [1, 128], F32)
    gnw_sb = sb("gnw_s", [128, ND], F32)
    alog_sb = sb("alog_s", [H, 1], F32)
    dtb_sb = sb("dtb_s", [H, 1], F32)
    negea_sb = sb("negea", [H, 1], F32)
    clip_sb = sb("clip", [1, 2], F32)
    rmask_sb = sb("rmask_s", [1, C.n_cores], F32)
    mrep_sb = sb("mrep", [128, C.n_cores], F32)
    sc_g = sb("sc_g", [128, NI], F32)
    sc_u = sb("sc_u", [128, NI], F32)
    sc_d = sb("sc_d", [128, NB], F32)
    cmbg = sb("cmbg", [128, NI], F32)
    cmbu = sb("cmbu", [128, NI], F32)
    cmb8 = sb("cmb8", [128, NB], F32)
    rq_x = sb("rq_x", [128, 1], F32)
    rq_h = sb("rq_h", [128, 1], F32)
    sclx = sb("sclx", [128, 1], F32)
    sclh = sb("sclh", [128, 1], F32)
    gmx = sb("gmx", [1, 1], F32)
    acol = sb("acol", [128, 1], F32)
    bcol = sb("bcol", [128, 1], F32)
    amx8 = sb("amx8", [128, max(ND, 1)], F32)
    habs = sb("habs", [128, NI * NW], F32)
    fheadT = sb("fheadT", [128, NT * H], F32)
    gprev = sb("gprev", [128, NCH], F32)
    egp = sb("egp", [128, NCH], F32)
    S_f = [sb(f"Sf{i}", [128, 128], F32) for i in range(2)]
    striu_s = sb("striu", [NCH, NCH], F32)
    glt = sb("glt", [NCH, 128], F32)
    egc_f = sb("egcf", [128, NCH], F32)
    _sbf = sb("Sbf0", [128, 128], BF16)
    S_bf = [_sbf, _sbf]
    Aq_sb = sb("Aq", [128, 1], F32)
    Atot = sb("Atot", [128, 1], F32)
    _kh = sb("khT0", [128, 128], BF16)
    khT = [_kh, _kh]
    _pm = sb("pm0", [128, 128], BF16)
    pm_sb = [_pm, _pm]


    PS = [st.enter_context(nc.psum_tensor(f"ps{i}", [128, 512], F32)) for i in range(6)]
    ones_row_f = triu_f[0:1, 0:128]
    arow = ar1[0:1, 0:128]

    ceps = sb("ceps", [128, 1], F32)
    cone = sb("cone", [128, 1], F32)
    cez = sb("cez", [128, 1], F32)
    mc1_sb = sb("mc1", [128, C.n_cores], F32)
    nc.const_aps.register(F32, EPS_LN, ceps[:, 0:1])
    nc.const_aps.register(F32, 1.0, cone[:, 0:1])
    nc.const_aps.register(F32, EPS_Z, cez[:, 0:1])
    cmagic = sb("cmagic", [128, 1], F32)
    cnmagic = sb("cnmagic", [128, 1], F32)
    nc.const_aps.register(F32, MAGIC, cmagic[:, 0:1])
    nc.const_aps.register(F32, -MAGIC, cnmagic[:, 0:1])

    Hb = arH[:, :].bitcast(BF16)
    Hf = arH[:, :].bitcast(F32)
    HF_TOT = HSIZE // 2

    def hfv(o, n):
        assert o + n <= HF_TOT, (o, n)
        return Hf[:, o:o + n]

    def hbv(o, n):
        assert o + n <= HSIZE, (o, n)
        return Hb[:, o:o + n]

    xln = hfv(0, ND * TPC)
    XT = ND * TPC
    row_a = hfv(XT, TPC)[0:1, :]
    row_b = hfv(XT + TPC, TPC)[0:1, :]
    row_c = hfv(XT + 2 * TPC, TPC)[0:1, :]
    I2 = (I + 1) // 2
    wbuf = hfv(0, ND * I2)
    WAC = max(I2, D // 2)
    wacc = hfv(ND * I2, WAC)
    wtmp = [hfv(ND * I2 + WAC + i * WAC, WAC) for i in range(2)]
    wblk = [hbv(2 * (ND * I2 + 3 * WAC) + i * max(ND, NI) * 128,
                max(ND, NI) * 128) for i in range(2)]
    OFF_O = 0
    o_sb = hbv(OFF_O, H * TPC)
    q_pp = [hbv(H * TPC + i * TPC, TPC) for i in range(2)]
    eG_pp = [hbv((H + 2) * TPC + i * TPC, TPC) for i in range(2)]
    k_bf = hbv((H + 4) * TPC, TPC)
    v_tm = hbv((H + 5) * TPC, TPC)
    eg_sb = hbv((H + 6) * TPC, TPC)
    em_sb = hbv((H + 7) * TPC, TPC)
    kd_sb = hbv((H + 8) * TPC, TPC)
    khat_sb = hbv((H + 9) * TPC, TPC)
    stage = hbv((H + 10) * TPC, TPC)
    f_h = arH[:, (H + 10) * TPC:(H + 11) * TPC]
    wproj = hbv((H + 11) * TPC, max(ND * 128, H * ND))
    wproj_B = hbv((H + 11) * TPC + ND * 128, ND * 128) if TPC >= 2 * ND * 128 \
        else wproj
    qg_scr = hbv((H + 11) * TPC, TPC)
    ATT_END_B = (H + 11) * TPC + max(ND * 128, TPC)
    ROW_B = max(ATT_END_B, H * TPC + 3 * ND * D)
    row_att = hfv(ROW_B // 2, TPC)[0:1, :]
    assert ROW_B // 2 + TPC <= HF_TOT
    fhead = hfv(0, TPC)[0:H, :]
    wo_bf = hbv(H * TPC, ND * D)
    wo_stage = hfv((H * TPC + ND * D) // 2, ND * D)

    sq2 = [ar1[:, 0:W], ar1[:, W:2 * W]]
    rep_a = ar1[:, 0:TPC]
    rep_b = ar2[:, 0:TPC]
    scr_u = ar1[:, 0:TPC]
    scr_m = ar2[:, 0:TPC]
    wg_pp = [ar1[:, 0:W].bitcast(BF16), ar1[:, W:2 * W].bitcast(BF16)]
    wu_pp = [ar1[:, 2 * W:3 * W].bitcast(BF16), ar1[:, 3 * W:4 * W].bitcast(BF16)]
    sg_pp = [ar2[:, 0:W], ar2[:, W:2 * W]]
    wd_pp = [ar2[:, 2 * W:4 * W].bitcast(BF16)[:, 0:NB * 128],
             ar2[:, 2 * W:4 * W].bitcast(BF16)[:, NB * 128:2 * NB * 128]]
    hb_pp = [ar1[:, 0:W // 2].bitcast(BF16), ar1[:, W // 2:W].bitcast(BF16)]
    hq_u = ar1[:, 2 * W:3 * W]
    hq_m = ar1[:, 3 * W:4 * W]
    g_scr = ar2[:, 0:TPC]
    fdm = ar1[:, 0:TPC]
    qs_f = ar1[:, 0:TPC]
    kkt = ar2[:, 0:TPC]
    rt_pp = [ar2[:, 0:128], ar2[:, 128:256]]
    rko_pp = [ar2[:, 256:384], ar2[:, 384:512]]
    scorz_pp = [ar2[:, 512:640], ar2[:, 640:768]]
    m8 = ar2[:, 776:808]
    zall = ar2[:, 808:824]
    zf = ar2[:, 824:840]
    sq_pp = [ar2[:, 1184:1440].bitcast(BF16)[:, 0:W],
             ar2[:, 1440:1696].bitcast(BF16)[:, 0:W]]
    rep_g = ar1[:, 0:TPC]
    khT_full = ar2[:, 0:TPC // 2].bitcast(BF16)
    NQ = C.n_cores
    Sq_all = ar2[:, 0:NQ * 128]
    wS_scr = ar2[:, 1024:1024 + NQ * 128]
    fold1 = ar2[:, 0:512]
    fold2 = ar2[:, 512:768]
    Sin_f = ar2[:, 768:896]
    Sin_bf = ar2[:, 896:960].bitcast(BF16)[:, 0:128]
    Aq8 = ar2[:, 960:960 + NQ]
    g1c = ar2[:, 968:968 + NQ]
    e8c = ar2[:, 976:976 + NQ]
    w8c = ar2[:, 984:984 + NQ]
    a8_sb = ar1[0:NQ, 0:128]

    assert NB * 128 * 2 <= 2 * W * 2

    em.sy(lambda e: e.dma_start(out=qnw_sb[:], in_=dqnw[:]))
    em.sy(lambda e: e.dma_start(out=knw_sb[:], in_=dknw[:]))
    em.sy(lambda e: e.dma_start(out=gnw_sb[:], in_=dgnw[:]))
    ld_ab = em.sy(lambda e: e.dma_start(out=alog_sb[:], in_=dalog[:]))
    em.sy(lambda e: e.dma_start(out=dtb_sb[:], in_=ddtb[:]))
    em.sy(lambda e: e.dma_start(out=clip_sb[:, 0:1], in_=dclipx[:]))
    ld_cl = em.sy(lambda e: e.dma_start(out=clip_sb[:, 1:2], in_=dclipd[:]))
    ld_small = em.sy(lambda e: e.dma_start(out=rmask_sb[:], in_=drmask[:]))
    ld_x = em.sy(lambda e: e.dma_start(
        out=x_sb[:].rearrange("p (c t) -> p c t", c=ND),
        in_=dx.rearrange("(c p) t -> p c t", p=128)))

    em.g(lambda e: e.iota(triu_f[:], [[1, 128]], base=0, channel_multiplier=-1,
                          allow_small_or_imprecise_dtypes=True))
    em.gdr()
    em.g(lambda e: e.tensor_scalar(ident_bf[:], triu_f[:], 0.0, None, ALU.is_equal))
    em.gdr()
    em.g(lambda e: e.tensor_scalar(ident_f[:], triu_f[:], 0.0, None, ALU.is_equal))
    em.gdr()
    em.g(lambda e: e.tensor_scalar(triu_h[:], triu_f[:], 0.0, None, ALU.is_ge))
    em.gdr()
    em.g(lambda e: e.tensor_scalar(triu_f[:], triu_f[:], 0.0, None, ALU.is_ge))
    em.gdr()
    em.g(lambda e: e.tensor_tensor(striu_s[:], triu_f[0:NCH, 0:NCH],
                                   ident_f[0:NCH, 0:NCH], ALU.subtract))
    em.gdr()
    em.g(lambda e: e.memset(ones_col_f[:], 1.0))
    em.gdr()
    em.v(lambda e: e.memset(ceps[:], EPS_LN))
    em.vd()
    em.v(lambda e: e.memset(cone[:], 1.0))
    em.vd()
    em.v(lambda e: e.memset(cez[:], EPS_Z))
    em.vd()
    em.v(lambda e: e.memset(cmagic[:], MAGIC))
    em.vd()
    em.v(lambda e: e.memset(cnmagic[:], -MAGIC))
    em.vd()
    GGK = em.g(lambda e: e.memset(ones_col_b[:], 1.0))
    gg_k = GGK

    sA = em.s(lambda e: e.activation(negea_sb[:], alog_sb[:], ACT.Exp),
              wait=[("dsy", ld_ab)])
    em.v(lambda e: e.tensor_scalar(negea_sb[:], negea_sb[:], -1.0, None, ALU.mult),
         wait=[("ss", sA)])
    ss_clip = em.s(lambda e: e.activation(clip_sb[:], clip_sb[:], ACT.Sigmoid),
                   wait=[("dsy", ld_cl)])
    tmr = em.t(lambda e: e.matmul(PS[5][:, 0:C.n_cores], ones_row_f[:],
                                  rmask_sb[:], start=True, stop=True),
               wait=[("dsy", ld_small), ("gg", gg_k)])
    v_mrep = em.v(lambda e: e.tensor_copy(mrep_sb[:], PS[5][:, 0:C.n_cores]),
                  wait=[("mm", tmr)])
    em.vd()
    v_mrep = em.v(lambda e: e.tensor_scalar(mc1_sb[:], mrep_sb[:], -1.0, 1.0,
                                            ALU.mult, ALU.add))

    def gmax_finish(vv_gmx, clip_idx, rq_out, scl_out):
        d1 = em.gd(lambda e: e.dma_start(out=dgin[:], in_=gmx[:]),
                   wait=[("vv", vv_gmx)])
        cc1 = em.cc(lambda e: e.collective_compute(
            "AllReduce", ALU.max, replica_groups=[list(range(C.n_cores))],
            ins=[dgin.ap()], outs=[dgout.ap()]), wait=[("dgp", d1)])
        d2 = em.gd(lambda e: e.dma_start(out=gmx[:], in_=dgout[:]),
                   wait=[("cc", cc1)])
        vC = em.v(lambda e: e.scalar_tensor_tensor(
            gmx[:], gmx[:], 1.0 / 127.0,
            clip_sb[:, clip_idx:clip_idx + 1], ALU.mult, ALU.mult),
            wait=[("dgp", d2), ("ss", ss_clip)])
        tB = em.t(lambda e: e.matmul(PS[4][:, 0:1], ones_row_f[:], gmx[:],
                                     start=True, stop=True),
                  wait=[("vv", vC), ("gg", GGK)])
        em.v(lambda e: e.tensor_copy(scl_out[:], PS[4][:, 0:1]),
             wait=[("mm", tB)])
        em.vd()
        return em.v(lambda e: e.reciprocal(rq_out[:], scl_out[:]))

    def global_scale(ready_vv, clip_idx, rq_out, scl_out):
        tA = em.t(lambda e: e.transpose(PS[4][0:1, 0:128], acol[:], ident_f[:]),
                  wait=[("vv", ready_vv), ("gg", GGK)])
        vB = em.v(lambda e: e.reduce_max(gmx[:], PS[4][0:1, 0:128], AX.X),
                  wait=[("mm", tA)])
        return gmax_finish(vB, clip_idx, rq_out, scl_out)

    def ln_stats(wait_pre):
        wait_pre = list(wait_pre) + [("gg", GGK)]
        for nw in range(NW):
            bank_free = em.now("vv")
            for c in range(ND):
                xsl = x_sb[:, c * TPC + nw * W: c * TPC + nw * W + W]
                ssq = em.s(lambda e, xsl=xsl, c=c: e.activation(
                    sq2[c % 2], xsl, ACT.Square),
                    wait=(wait_pre + [("mm", em.now("mm"))]) if c < 2 else
                    [("mm", em.now("mm"))])
                em.t(lambda e, xsl=xsl, c=c: e.matmul(
                    PS[0][0:1, 0:W], ones_col_f[:], xsl,
                    start=(c == 0), stop=(c == ND - 1)),
                    wait=(wait_pre + [("vv", bank_free)]) if c == 0 else [])
                em.t(lambda e, c=c: e.matmul(
                    PS[1][0:1, 0:W], ones_col_f[:], sq2[c % 2],
                    start=(c == 0), stop=(c == ND - 1)),
                    wait=[("ss", ssq)] + ([("vv", bank_free)] if c == 0 else []))
            mmv = em.now("mm")
            em.v(lambda e, nw=nw: e.tensor_copy(
                row_a[:, nw * W:(nw + 1) * W], PS[0][0:1, 0:W]),
                wait=[("mm", mmv)])
            em.v(lambda e, nw=nw: e.tensor_copy(
                row_b[:, nw * W:(nw + 1) * W], PS[1][0:1, 0:W]),
                wait=[("mm", mmv)])
        em.vd()
        em.v(lambda e: e.tensor_scalar(row_a[:], row_a[:], 1.0 / D, None, ALU.mult))
        em.vd()
        em.v(lambda e: e.tensor_scalar(row_b[:], row_b[:], 1.0 / D, None, ALU.mult))
        em.vd()
        em.v(lambda e: e.tensor_mul(row_c[:], row_a[:], row_a[:]))
        em.vd()
        vv1 = em.v(lambda e: e.tensor_sub(row_b[:], row_b[:], row_c[:]))
        em.s(lambda e: e.activation(row_b[:], row_b[:], ACT.Ln,
                                    bias=EPS_LN), wait=[("vv", vv1)])
        em.sd()
        ss2 = em.s(lambda e: e.activation(row_b[:], row_b[:], ACT.Exp,
                                          scale=-0.5))
        return [("ss", ss2)]

    def replicate_row(row, psum_idx, out_sb, wait_pre):
        last = None
        for nw in range(NW):
            tm = em.t(lambda e, nw=nw: e.matmul(
                PS[psum_idx][:, 0:W], ones_row_f[:],
                row[:, nw * W:(nw + 1) * W], start=True, stop=True),
                wait=list(wait_pre) + [("vv", em.now("vv")), ("gg", GGK)])
            last = em.v(lambda e, nw=nw: e.tensor_copy(
                out_sb[:, nw * W:(nw + 1) * W], PS[psum_idx][:, 0:W]),
                wait=[("mm", tm)])
        return last

    def quant_chain(dst, src, rq_col, u_scr, m_scr, wait):
        em.v(lambda e: e.tensor_scalar(u_scr, src, rq_col[:], MAGIC,
                                       ALU.mult, ALU.add), wait=wait)
        em.vd()
        em.v(lambda e: e.tensor_scalar(m_scr, u_scr, MAGIC, -128.0,
                                       ALU.subtract, ALU.max))
        em.vd()
        return em.v(lambda e: e.tensor_scalar(dst, m_scr, 127.0, None,
                                              ALU.min))

    def ln_quant(wait_pre):
        rs_w = ln_stats(wait_pre)
        vv_mr = em.v(lambda e: e.tensor_mul(row_c[:], row_a[:], row_b[:]),
                     wait=rs_w)
        replicate_row(row_b, 2, rep_a, [("vv", vv_mr)])
        vb = replicate_row(row_c, 3, rep_b, [("vv", vv_mr)])
        for c in range(ND):
            em.v(lambda e, c=c: e.tensor_mul(
                xln[:, c * TPC:(c + 1) * TPC], x_sb[:, c * TPC:(c + 1) * TPC],
                rep_a[:]), wait=[("vv", vb)] if c == 0 else [])
            em.vd()
            em.v(lambda e, c=c: e.tensor_sub(
                xln[:, c * TPC:(c + 1) * TPC], xln[:, c * TPC:(c + 1) * TPC],
                rep_b[:]))
            em.vd()
            em.v(lambda e, c=c: e.tensor_reduce(
                amx8[:, c:c + 1], xln[:, c * TPC:(c + 1) * TPC], AX.X,
                ALU.max, apply_absolute_value=True))
            em.vd()
        vv_am = em.v(lambda e: e.tensor_reduce(
            acol[:], amx8[:, 0:ND], AX.X, ALU.max, apply_absolute_value=True))
        vq = global_scale(vv_am, 0, rq_x, sclx)
        last = None
        for c in range(ND):
            last = quant_chain(arB[:, c * TPC:(c + 1) * TPC],
                               xln[:, c * TPC:(c + 1) * TPC],
                               rq_x, scr_u, scr_m,
                               [("vv", vq)] if c == 0 else [])
        return last

    scmy = sb("scmy", [128, NSC], F32)
    scstg = sb("scstg", [128, R * NSC], F32)

    def wquant_my(src_my, KD, NS, pay_off, sc_off):
        NK = KD // 128
        prev_tr = em.now("mm")
        for s in range(NS):
            dld = em.sy(lambda e, s=s, NK=NK: e.dma_start(
                out=wbuf[:, 0:NK * 128].rearrange("p (c w) -> p c w", c=NK),
                in_=src_my[:, s * 128:(s + 1) * 128]
                .rearrange("(c p) w -> p c w", p=128)),
                wait=[("vv", em.now("vv"))])
            vsteps = {}
            vv0 = None
            for c in range(NK):
                sA = em.s(lambda e, c=c: e.activation(
                    wtmp[c % 2][:, 0:128], wbuf[:, c * 128:(c + 1) * 128],
                    ACT.Abs),
                    wait=[("dsy", dld), ("vv", vsteps.get(c - 2, em.now("vv")))])
                if c == 0:
                    vv0 = em.v(lambda e: e.tensor_copy(
                        wacc[:, 0:128], wtmp[0][:, 0:128]), wait=[("ss", sA)])
                else:
                    em.vd()
                    vv0 = em.v(lambda e, c=c: e.tensor_tensor(
                        wacc[:, 0:128], wacc[:, 0:128], wtmp[c % 2][:, 0:128],
                        ALU.max), wait=[("ss", sA)])
                vsteps[c] = vv0
            sg = sc_off + s
            tj = em.t(lambda e: e.transpose(
                PS[4][:, 0:128], wacc[:, 0:128], ident_f[:]),
                wait=[("vv", em.now("vv")), ("gg", GGK)])
            em.v(lambda e, sg=sg: e.tensor_reduce(
                scmy[:, sg:sg + 1], PS[4][:, 0:128], AX.X, ALU.max),
                wait=[("mm", tj)])
            em.vd()
            vts = em.v(lambda e, sg=sg: e.tensor_scalar(
                scmy[:, sg:sg + 1], scmy[:, sg:sg + 1],
                1.0 / 127.0, None, ALU.mult))
            em.vd()
            em.s(lambda e, sg=sg: e.activation(
                acol[:], scmy[:, sg:sg + 1], ACT.Ln),
                wait=[("vv", vts), ("mm", prev_tr)])
            em.sd()
            sex = em.s(lambda e: e.activation(acol[:], acol[:], ACT.Exp,
                                              scale=-1.0))
            tr = em.t(lambda e: e.transpose(PS[3][0:1, 0:128], acol[:],
                                            ident_f[:]),
                      wait=[("ss", sex), ("gg", GGK)])
            prev_tr = tr
            vrow = em.v(lambda e: e.tensor_copy(arow[:], PS[3][0:1, 0:128]),
                        wait=[("mm", tr)])
            trep = em.t(lambda e: e.matmul(PS[4][:, 0:128], ones_row_f[:],
                                           arow[:], start=True, stop=True),
                        wait=[("vv", vrow)])
            bsl = wbuf[:, 0:NK * 128].rearrange("p (c w) -> p c w", c=NK)
            repb = PS[4][:, 0:128][:, None, :].broadcast_to([128, NK, 128])
            dst3 = wblk[s % 2][:, 0:NK * 128].rearrange("p (c w) -> p c w",
                                                        c=NK)
            nhalf = 2 if NK * 128 > TPC else 1
            kk = NK // nhalf
            vq = None
            for hh in range(nhalf):
                b_h = bsl[:, hh * kk:(hh + 1) * kk, :]
                r_h = repb[:, hh * kk:(hh + 1) * kk, :]
                d_h = dst3[:, hh * kk:(hh + 1) * kk, :]
                u_h = scr_u[:, 0:kk * 128].rearrange("p (c w) -> p c w", c=kk)
                em.v(lambda e, b_h=b_h, r_h=r_h, u_h=u_h: e.tensor_tensor(
                    u_h, b_h, r_h, ALU.mult),
                    wait=[("mm", trep), ("dgp", em.now("dgp"))])
                em.vd()
                vq = em.v(lambda e, u_h=u_h, d_h=d_h: e.tensor_scalar(
                    d_h, u_h, MAGIC, MAGIC, ALU.add, ALU.subtract))
                em.vd()
            em.gd(lambda e, s=s, NK=NK, pay_off=pay_off: e.dma_start(
                out=dwin[pay_off + s * NK * 128:pay_off + (s + 1) * NK * 128, :]
                .rearrange("(c p) w -> p c w", p=128),
                in_=wblk[s % 2][:, 0:NK * 128]
                .rearrange("p (c w) -> p c w", c=NK)), wait=[("vv", vq)])

    wquant_my(dwgm, D, GS, 0, 0)
    wquant_my(dwum, D, GS, GS * D, GS)
    wquant_my(dwdm, I, DS, 2 * GS * D, 2 * GS)
    wsc = em.gd(lambda e: e.dma_start(out=dsci[:], in_=scmy[:]),
                wait=[("vv", em.now("vv"))])
    ag_w = em.cc(lambda e: e.collective_compute(
        "AllGather", ALU.bypass, replica_groups=[list(range(R))],
        ins=[dwin.ap()], outs=[dgath.ap()]),
        wait=[("dgp", em.now("dgp"))])
    ag_s = em.cc(lambda e: e.collective_compute(
        "AllGather", ALU.bypass, replica_groups=[list(range(R))],
        ins=[dsci.ap()], outs=[dsco.ap()]))
    rdsc = None
    for r in range(R):
        rdsc = em.gd(lambda e, r=r: e.dma_start(
            out=scstg[:, r * NSC:(r + 1) * NSC],
            in_=dsco[r * 128:(r + 1) * 128, :]),
            wait=[("cc", ag_s)] if r == 0 else [])
    vv_sc = None
    for r in range(R):
        gw = min(GS, NI - r * GS)
        dw_ = min(DS, NB - r * DS)
        if gw > 0:
            em.v(lambda e, r=r, gw=gw: e.tensor_copy(
                sc_g[:, r * GS:r * GS + gw], scstg[:, r * NSC:r * NSC + gw]),
                wait=[("dgp", rdsc)] if r == 0 else [])
            em.v(lambda e, r=r, gw=gw: e.tensor_copy(
                sc_u[:, r * GS:r * GS + gw],
                scstg[:, r * NSC + GS:r * NSC + GS + gw]))
        if dw_ > 0:
            vv_sc = em.v(lambda e, r=r, dw_=dw_: e.tensor_copy(
                sc_d[:, r * DS:r * DS + dw_],
                scstg[:, r * NSC + 2 * GS:r * NSC + 2 * GS + dw_]))
    em.vd()
    for r in range(R):
        gr = min(GS, NI - r * GS)
        nreal = min(DS, NB - r * DS)
        if gr > 0:
            em.gd(lambda e, r=r, gr=gr: e.dma_start(
                out=dcg[r * GS * D:(r * GS + gr) * D, :],
                in_=dgath[r * PAYR:r * PAYR + gr * D, :]),
                wait=[("cc", ag_w)] if r == 0 else [])
            em.gd(lambda e, r=r, gr=gr: e.dma_start(
                out=dcu[r * GS * D:(r * GS + gr) * D, :],
                in_=dgath[r * PAYR + GS * D:r * PAYR + GS * D + gr * D, :]))
        if nreal > 0:
            em.gd(lambda e, r=r, nreal=nreal: e.dma_start(
                out=dcd[r * DS * I:(r * DS + nreal) * I, :],
                in_=dgath[r * PAYR + 2 * GS * D:
                          r * PAYR + 2 * GS * D + nreal * I, :]))
    wq_done = em.now("dgp")

    vv_intx = ln_quant([("dsy", ld_x), ("vv", em.now("vv"))])
    if stop_stage <= 1:
        return _finish()
    tap("intx", arB[:], [("vv", vv_intx)])
    tap("sclx", clip_sb[:], [("vv", vv_intx)])

    if stop_stage <= 2:
        return _finish()

    def ffn_gateup(wait_intx_vv, px, entry_ss):
        silu_d = {}
        h_d = {}
        gate_done = {}
        up_done = {}
        banksG = [PS[0], PS[1], PS[2], PS[3]]
        banksU = [PS[4], PS[5], px[0], px[1]]
        em.v(lambda e: e.tensor_scalar(cmbg[:], sc_g[:], sclx[:], None,
                                       ALU.mult), wait=[("vv", wait_intx_vv)])
        em.vd()
        vv_cmb = em.v(lambda e: e.tensor_scalar(cmbu[:], sc_u[:], sclx[:],
                                                None, ALU.mult))
        em.vd()
        for j in range(NI):
            wait_w = [("mm", up_done[(j - 2, NW - 1)])] if j >= 2 else \
                [("vv", em.now("vv"))]
            dg = em.sy(lambda e, j=j: e.dma_start(
                out=wg_pp[j % 2][:].rearrange("p (c w) -> p c w", c=ND),
                in_=dcg[j * D:(j + 1) * D, :]
                .rearrange("(c p) w -> p c w", p=128)),
                wait=wait_w + [("dgp", wq_done)])
            du = em.sy(lambda e, j=j: e.dma_start(
                out=wu_pp[j % 2][:].rearrange("p (c w) -> p c w", c=ND),
                in_=dcu[j * D:(j + 1) * D, :]
                .rearrange("(c p) w -> p c w", p=128)))
            for c in range(ND):
                for nw in range(NW):
                    if c == 0:
                        wt = [("dsy", dg), ("vv", wait_intx_vv)]
                        if j >= 1:
                            wt += [("ss", silu_d[(j - 1, nw)])]
                        elif entry_ss:
                            wt += [("ss", entry_ss)]
                    else:
                        wt = []
                    gate_done[(j, nw)] = em.t(
                        lambda e, j=j, nw=nw, c=c: e.matmul(
                            banksG[nw][:, 0:W],
                            wg_pp[j % 2][:, c * 128:(c + 1) * 128],
                            arB[:, c * TPC + nw * W: c * TPC + nw * W + W],
                            start=(c == 0), stop=(c == ND - 1)), wait=wt)
                for nw in range(NW):
                    if c == 0:
                        wt = [("dsy", du)]
                        if j >= 1:
                            wt += [("vv", h_d[(j - 1, nw)])]
                        elif entry_ss:
                            wt += [("ss", entry_ss)]
                    else:
                        wt = []
                    up_done[(j, nw)] = em.t(
                        lambda e, j=j, nw=nw, c=c: e.matmul(
                            banksU[nw][:, 0:W],
                            wu_pp[j % 2][:, c * 128:(c + 1) * 128],
                            arB[:, c * TPC + nw * W: c * TPC + nw * W + W],
                            start=(c == 0), stop=(c == ND - 1)), wait=wt)
            for nw in range(NW):
                gi = j * NW + nw
                if gi >= 2:
                    sg_war = [("vv", h_d[divmod(gi - 2, NW)])]
                else:
                    sg_war = []
                ssx = em.s(lambda e, j=j, nw=nw, gi=gi: e.activation(
                    sg_pp[gi % 2], banksG[nw][:, 0:W], ACT.Silu,
                    scale=cmbg[:, j:j + 1]),
                    wait=[("mm", gate_done[(j, nw)]), ("vv", vv_cmb)] + sg_war)
                silu_d[(j, nw)] = ssx
                hv = em.v(lambda e, j=j, nw=nw, gi=gi: e.scalar_tensor_tensor(
                    arH[:, j * TPC + nw * W: j * TPC + nw * W + W],
                    sg_pp[gi % 2], cmbu[:, j:j + 1], banksU[nw][:, 0:W],
                    ALU.mult, ALU.mult),
                    wait=[("ss", ssx), ("mm", up_done[(j, nw)])])
                h_d[(j, nw)] = hv
                em.g(lambda e, j=j, nw=nw: e.tensor_reduce(
                    habs[0:1, (j * NW + nw):(j * NW + nw) + 1],
                    arH[:, j * TPC + nw * W: j * TPC + nw * W + W],
                    AX.XYZWC, ALU.max, apply_absolute_value=True),
                    wait=[("vv", hv)])
        em.gdr()
        gg_h = em.now("gg")
        vqh_v = em.v(lambda e: e.tensor_reduce(
            gmx[:], habs[0:1, 0:NI * NW], AX.X, ALU.max,
            apply_absolute_value=True), wait=[("gg", gg_h)])
        vqh = gmax_finish(vqh_v, 1, rq_h, sclh)
        return em.v(lambda e: e.tensor_scalar(
            cmb8[:], sc_d[:], sclh[:], 0.5, ALU.mult, ALU.mult),
            wait=[("vv", vqh)])

    def ffn_down(vcmb, px):
        banks = PS + px
        assert NB <= len(banks)
        prev_consumed = {}
        for nw in range(NW):
            epi_prev = em.now("vv")
            consumed = {}
            for i in range(NI):
                buf_war = consumed.get(i - 2, prev_consumed.get(NI - 2 + i, 0))
                dd_q = em.sy if i % 2 == 0 else em.sc
                dd_sem = "dsy" if i % 2 == 0 else "dsc"
                dd = dd_q(lambda e, i=i: e.dma_start(
                    out=wd_pp[i % 2][:].rearrange("p (b w) -> p b w", b=NB),
                    in_=dcd.rearrange("(b k p) w -> b k p w", b=NB, p=128)
                    [:, i, :, :].rearrange("b p w -> p b w")),
                    wait=[("mm", buf_war), ("dgp", wq_done)])
                qv = quant_chain(
                    hb_pp[i % 2][:],
                    arH[:, i * TPC + nw * W: i * TPC + nw * W + W],
                    rq_h, hq_u, hq_m,
                    [("vv", vcmb), ("mm", buf_war),
                     ("vv", epi_prev if i == 0 else 0)])
                last = None
                for b in range(NB):
                    last = em.t(lambda e, i=i, b=b: e.matmul(
                        banks[b][:, 0:W], wd_pp[i % 2][:, b * 128:(b + 1) * 128],
                        hb_pp[i % 2][:], start=(i == 0), stop=(i == NI - 1)),
                        wait=([(dd_sem, dd), ("vv", qv)] +
                              ([("vv", epi_prev)] if i == 0 else []))
                        if b == 0 else [])
                consumed[i] = last
            prev_consumed = consumed
            mm_all = em.now("mm")
            for b in range(NB):
                em.v(lambda e, b=b, nw=nw: e.scalar_tensor_tensor(
                    x_sb[:, b * TPC + nw * W: b * TPC + nw * W + W],
                    banks[b][:, 0:W], cmb8[:, b:b + 1],
                    x_sb[:, b * TPC + nw * W: b * TPC + nw * W + W],
                    ALU.mult, ALU.add), wait=[("mm", mm_all)])
        return em.now("vv")

    psA = ExitStack()
    pxA = [psA.enter_context(nc.psum_tensor(f"pxa{i}", [128, 512], F32))
           for i in range(2)]
    vcmb1 = ffn_gateup(vv_intx, pxA, 0)
    tap("h", arH[:, 0:NI * TPC], [("vv", em.now("vv")), ("gg", em.now("gg"))])
    tap("sclh", clip_sb[:], [("vv", em.now("vv"))])
    vv_x1 = ffn_down(vcmb1, pxA)
    psA.close()
    psB = ExitStack()
    open_stacks.append(psB)
    PB = psB.enter_context(nc.psum_tensor("pb0", [128, 1024], BF16))
    pvx = psB.enter_context(nc.psum_tensor("pvx", [128, 512], F32))

    if stop_stage <= 3:
        return _finish([("vv", vv_x1)])
    tap("x1", x_sb[:], [("vv", vv_x1)])
    rs2_w = ln_stats([("vv", vv_x1)])
    vv_mr2 = em.v(lambda e: e.tensor_mul(row_c[:], row_a[:], row_b[:]),
                  wait=rs2_w)
    replicate_row(row_b, 2, rep_a, [("vv", vv_mr2)])
    vb2 = replicate_row(row_c, 3, rep_b, [("vv", vv_mr2)])
    vv_xln2 = None
    for c in range(ND):
        em.v(lambda e, c=c: e.tensor_mul(
            scr_u[:, 0:TPC] if False else xln[:, c * TPC:(c + 1) * TPC],
            x_sb[:, c * TPC:(c + 1) * TPC], rep_a[:]),
            wait=[("vv", vb2)] if c == 0 else [])
        em.vd()
        vv_xln2 = em.v(lambda e, c=c: e.tensor_sub(
            arB[:, c * TPC:(c + 1) * TPC], xln[:, c * TPC:(c + 1) * TPC],
            rep_b[:]))
        em.vd()

    tap("xln2", arB[:], [("vv", vv_xln2)])
    wa_sb = wproj[:, 0:ND * H]
    dwa_ld = em.gd(lambda e: e.dma_start(
        out=wa_sb[:].rearrange("p (c o) -> p c o", c=ND),
        in_=dwa.rearrange("(c p) o -> p c o", p=128)),
        wait=[("vv", em.now("vv")), ("mm", em.now("mm"))])
    vfh2 = None
    for nw in range(NW):
        tz = None
        for c in range(ND):
            tz = em.t(lambda e, c=c, nw=nw: e.matmul(
                PS[0][0:H, 0:W], wa_sb[:, c * H:(c + 1) * H],
                arB[:, c * TPC + nw * W: c * TPC + nw * W + W],
                start=(c == 0), stop=(c == ND - 1)),
                wait=[("dgp", dwa_ld), ("vv", vv_xln2),
                      ("ss", em.now("ss")), ("vv", em.now("vv"))]
                if c == 0 else [])
        sa = em.s(lambda e, nw=nw: e.activation(
            sq2[0][0:H, :], PS[0][0:H, 0:W], ACT.Abs, bias=dtb_sb[:]),
            wait=[("mm", tz)])
        em.sd()
        em.s(lambda e: e.activation(sq2[0][0:H, :], sq2[0][0:H, :], ACT.Exp,
                                    scale=-1.0))
        em.sd()
        em.s(lambda e: e.activation(sq2[0][0:H, :], sq2[0][0:H, :], ACT.Ln,
                                    bias=1.0))
        em.sd()
        sr = em.s(lambda e, nw=nw: e.activation(
            sq2[1][0:H, :], PS[0][0:H, 0:W], ACT.Relu, bias=dtb_sb[:]))
        em.v(lambda e: e.tensor_add(sq2[0][0:H, :], sq2[0][0:H, :],
                                    sq2[1][0:H, :]), wait=[("ss", sr)])
        em.vd()
        vfh2 = em.v(lambda e, nw=nw: e.tensor_scalar(
            fhead[:, nw * W:(nw + 1) * W], sq2[0][0:H, :], negea_sb[:],
            None, ALU.mult))
    tap("fhead", fhead[:, 0:TPC], [("vv", vfh2)])
    prev_ev = 0
    fhT3 = fheadT.rearrange("p (h t) -> p h t", h=H)
    for nt in range(NT):
        tf = em.t(lambda e, nt=nt: e.transpose(
            PS[1][0:128, 0:H], fhead[:, nt * 128:(nt + 1) * 128],
            ident_f[0:H, 0:H]),
            wait=[("vv", vfh2), ("vv", prev_ev), ("gg", GGK)])
        prev_ev = em.v(lambda e, nt=nt: e.tensor_copy(
            fhT3[:, :, nt:nt + 1], PS[1][0:128, 0:H][:, :, None]),
            wait=[("mm", tf)])
    vv_fhT = em.now("vv")
    sa_fh_last = em.now("ss")

    if stop_stage <= 4:
        return _finish()
    relay_sent = {}

    def correction(h, gate_v, gate_vss, gate_mm, gate_sbf, gate_sv_ss,
                   cur_vkd):
        pp = h % 2
        vqg = em.v(lambda e, pp=pp: e.tensor_tensor(
            qg_scr[:, 0:TPC], q_pp[pp][:], eG_pp[pp][:], ALU.mult),
            wait=[("vv", gate_v), ("ss", gate_vss), ("mm", gate_mm)])
        em.vd()
        d1 = em.gd(lambda e: e.dma_start(
            out=Sq_all.rearrange("p (q w) -> p q w", q=NQ),
            in_=drout.rearrange("(q r) w -> r q w", r=129)[0:128, :, :]),
            wait=[("cc", relay_sent[h]), ("vv", cur_vkd),
                  ("ss", em.now("ss")), ("mm", em.now("mm"))])
        d2 = em.gd(lambda e: e.dma_start(
            out=a8_sb,
            in_=drout.rearrange("(q r) w -> q r w", r=129)[:, 128:129, :]
            .rearrange("q a w -> (q a) w")))
        tqa = em.t(lambda e: e.transpose(
            PS[5][0:128, 0:NQ], a8_sb, ident_f[0:NQ, 0:NQ]),
            wait=[("dgp", d2), ("vv", gate_sbf), ("ss", gate_sv_ss),
                  ("gg", GGK)])
        em.v(lambda e: e.tensor_copy(Aq8[:], PS[5][:, 0:NQ]),
             wait=[("mm", tqa)])
        em.vd()
        em.v(lambda e: e.tensor_tensor(g1c[:], Aq8[:], mrep_sb[:], ALU.mult),
             wait=[("vv", v_mrep)])
        em.vd()
        em.v(lambda e: e.tensor_tensor(g1c[:], g1c[:], mc1_sb[:], ALU.add))
        em.vd()
        em.v(lambda e: e.memset(e8c[:, NQ - 1:NQ], 1.0))
        em.vd()
        for qc in range(NQ - 2, -1, -1):
            em.v(lambda e, qc=qc: e.tensor_mul(
                e8c[:, qc:qc + 1], e8c[:, qc + 1:qc + 2],
                g1c[:, qc + 1:qc + 2]))
            em.vd()
        vw8 = em.v(lambda e: e.tensor_tensor(w8c[:], e8c[:], mrep_sb[:],
                                             ALU.mult))
        em.vd()
        wb = w8c[:, :, None].broadcast_to([128, NQ, 128])
        em.v(lambda e, wb=wb: e.tensor_tensor(
            wS_scr[:, 0:NQ * 128].rearrange("p (q w) -> p q w", q=NQ),
            Sq_all.rearrange("p (q w) -> p q w", q=NQ), wb, ALU.mult),
            wait=[("dgp", d1), ("vv", vw8)])
        em.vd()
        wlen = NQ * 128
        src = wS_scr
        fold_dsts = [fold1, fold2]
        di = 0
        while wlen > 128:
            half = wlen // 2
            dst = Sin_f if half == 128 else fold_dsts[di]
            em.v(lambda e, src=src, dst=dst, half=half: e.tensor_tensor(
                dst[:, 0:half], src[:, 0:half], src[:, half:2 * half],
                ALU.add))
            em.vd()
            src, wlen, di = dst, half, di + 1
        vsb = em.v(lambda e: e.tensor_copy(Sin_bf[:], Sin_f[:, 0:128]))
        em.vd()
        vo = gate_sbf
        tmc = None
        for nw in range(NW):
            tmc = em.t(lambda e, nw=nw: e.matmul(
                PS[4][:, 0:W], Sin_bf[:], qg_scr[:, nw * W:(nw + 1) * W],
                start=True, stop=True),
                wait=[("vv", vsb), ("vv", vqg), ("vv", vo)])
            vo = em.v(lambda e, h=h, nw=nw: e.tensor_tensor(
                o_sb[:, h * TPC + nw * W: h * TPC + nw * W + W],
                o_sb[:, h * TPC + nw * W: h * TPC + nw * W + W],
                PS[4][:, 0:W], ALU.add), wait=[("mm", tmc)])
        return vo, tmc

    prev_sbf = vv_xln2
    prev_oc_last = 0
    prev_skt_last = 0
    prev_scan_td = 0
    prev_corr_mm = em.now("mm")
    prev_corr_v = 0
    prev_khT_mm = 0
    next_ld = {}
    vq_ev = {}
    veg_ev = {}
    vkd_ev = {}
    vf_last_d = 0
    sbf_last = {}
    sv_last_d = {}
    vmm_last = {}

    for h in range(H if n_heads is None else n_heads):
        pp = h % 2
        fbit = f_h
        head_entry_v = max(prev_sbf, prev_corr_v)

        two_buf = wproj_B is not wproj

        def wload_A(th, gate_mm):
            return em.gd(lambda e, th=th: e.dma_start(
                out=wproj[:, 0:ND * 128].rearrange("p (c o) -> p c o", c=ND),
                in_=dwr[:, th * 128:(th + 1) * 128]
                .rearrange("(c p) o -> p c o", p=128)),
                wait=[("vv", em.now("vv")), ("mm", gate_mm)])

        def wload_B(wdram, th, gate_mm):
            return em.gd(lambda e, th=th, wdram=wdram: e.dma_start(
                out=wproj_B[:, 0:ND * 128].rearrange("p (c o) -> p c o", c=ND),
                in_=wdram[:, th * 128:(th + 1) * 128]
                .rearrange("(c p) o -> p c o", p=128)),
                wait=[("mm", gate_mm)] +
                ([("dsy", em.now("dsy"))] if dbg else []))

        st_local = {}
        if "wr_ld" in next_ld:
            wr_ld = next_ld.pop("wr_ld")
            st_local["wq_ld"] = next_ld.pop("wq_ld")
        else:
            wr_ld = wload_A(h, prev_corr_mm)
            if two_buf:
                st_local["wq_ld"] = wload_B(dwq, h, prev_corr_mm)

        def wload(wdram, gate_mm):
            return wload_B(wdram, h, gate_mm)

        q_mm = {}
        q_silu = {}
        k_mm = {}
        k_silu = {}
        sv_ev = {}

        def emit_qk_group(mm_d, silu_d_, nws, wl_key, qsf_g, bank_war):
            for nw in nws:
                tq = None
                for c in range(ND):
                    if c == 0:
                        wt = [("dgp", st_local[wl_key])] + bank_war(nw)
                    else:
                        wt = []
                    tq = em.t(lambda e, nw=nw, c=c: e.matmul(
                        PS[3 + nw % 2][:, 0:W],
                        wproj_B[:, c * 128:(c + 1) * 128],
                        arB[:, c * TPC + nw * W: c * TPC + nw * W + W],
                        start=(c == 0), stop=(c == ND - 1)), wait=wt)
                mm_d[nw] = tq
                silu_d_[nw] = em.s(lambda e, nw=nw: e.activation(
                    qs_f[:, nw * W:(nw + 1) * W], PS[3 + nw % 2][:, 0:W],
                    ACT.Silu),
                    wait=[("mm", tq)] + ([("vv", qsf_g)] if nw == 0 else []))

        def emit_stats(silu_ev_d, norm_row, dst_bf, prev_t2, prev_vv,
                       sqb, s2_mm, tapme=False):
            em.sd()
            t2p = {}
            rows_v = {}
            for nw in range(NW):
                s2 = em.s(lambda e, nw=nw, sqb=sqb: e.activation(
                    sqb[nw % 2], qs_f[:, nw * W:(nw + 1) * W], ACT.Square),
                    wait=[("mm", t2p.get(nw - 2, prev_t2.get(NW - 2 + nw,
                                                             s2_mm))),
                          ("ss", silu_ev_d[nw])])
                t2p[nw] = em.t(lambda e, nw=nw, sqb=sqb: e.matmul(
                    pvx[0:1, 0:W], ones_col_b[:], sqb[nw % 2],
                    start=True, stop=True),
                    wait=[("ss", s2), ("gg", GGK)] +
                    ([("vv", rows_v[nw - 1])] if nw >= 1 else
                     [("vv", prev_vv)]))
                rows_v[nw] = em.v(lambda e, nw=nw: e.tensor_scalar(
                    row_att[:, nw * W:(nw + 1) * W], pvx[0:1, 0:W],
                    1.0 / 128.0, None, ALU.mult), wait=[("mm", t2p[nw])])
            em.s(lambda e: e.activation(row_att[:], row_att[:], ACT.Ln,
                                        bias=EPS_RMS),
                 wait=[("vv", rows_v[NW - 1])])
            em.sd()
            srr2 = em.s(lambda e: e.activation(row_att[:], row_att[:],
                                               ACT.Exp, scale=-0.5))
            if tapme:
                tap("rsq0", row_att[:], [("ss", srr2)])
            vtt = None
            for nw in range(NW):
                t3 = em.t(lambda e, nw=nw, norm_row=norm_row: e.matmul(
                    pvx[:, 0:W], norm_row[:],
                    row_att[:, nw * W:(nw + 1) * W],
                    start=True, stop=True),
                    wait=[("ss", srr2)] +
                    ([("vv", vtt)] if nw >= 1 else []))
                vtt = em.v(lambda e, nw=nw, dst_bf=dst_bf: e.tensor_tensor(
                    dst_bf[:, nw * W:(nw + 1) * W],
                    qs_f[:, nw * W:(nw + 1) * W], pvx[:, 0:W],
                    ALU.mult), wait=[("mm", t3)])
            em.vd()
            return vtt, t2p

        def emit_v_tiles(nts):
            for nt in nts:
                vb = PS[5] if nt % 2 == 0 else PS[2]
                tv_ = None
                for c in range(ND):
                    if c == 0:
                        wt = [("dgp", st_local["wv_ld"])]
                        if nt < 2:
                            wt += [("vv", head_entry_v), ("ss", prev_oc_last)]
                        else:
                            wt += [("ss", sv_ev[nt - 2])]
                    else:
                        wt = []
                    tv_ = em.t(lambda e, nt=nt, c=c, vb=vb: e.matmul(
                        vb[:, 0:128],
                        arB[:, c * TPC + nt * 128: c * TPC + (nt + 1) * 128],
                        wproj_B[:, c * 128:(c + 1) * 128],
                        start=(c == 0), stop=(c == ND - 1)), wait=wt)
                sv_ev[nt] = em.s(lambda e, nt=nt, vb=vb: e.activation(
                    v_tm[:, nt * 128:(nt + 1) * 128], vb[:, 0:128],
                    ACT.Silu), wait=[("mm", tv_)])
                st_local["tv_last"] = tv_

        def q_bank_war(nw):
            if nw == 0:
                return [("ss", prev_oc_last)]
            if nw == 1:
                return [("vv", head_entry_v)]
            return [("ss", q_silu[nw - 2])]

        def k_bank_war(nw):
            if nw < 2:
                par = [i for i in range(NW) if i % 2 == nw % 2]
                return [("ss", q_silu[par[-1]])]
            return [("ss", k_silu[nw - 2])]

        def do_q_stats():
            vvq, t2q = emit_stats(q_silu, qnw_sb, q_pp[pp], {},
                                  head_entry_v, sq_pp, 0, tapme=(h == 0))
            st_local["vv_q"] = vvq
            st_local["t2q"] = t2q
            if h == 0:
                tap("wp0", wproj_B[:, 0:ND * 128],
                    [("dgp", st_local["wq_ld"])])
                tap("qs0", qs_f[:, 0:TPC], [("ss", em.now("ss"))])

        sched = {
            2: lambda: emit_qk_group(q_mm, q_silu, [0, 1], "wq_ld",
                                     head_entry_v, q_bank_war),
            3: lambda: emit_qk_group(q_mm, q_silu, [2, 3], "wq_ld",
                                     head_entry_v, q_bank_war),
            5: do_q_stats,
            6: lambda: st_local.__setitem__("wk_ld", wload(dwk, q_mm[NW - 1])),
            7: lambda: emit_qk_group(k_mm, k_silu, [0, 1], "wk_ld",
                                     st_local["vv_q"], k_bank_war),
            9: lambda: emit_qk_group(k_mm, k_silu, [2, 3], "wk_ld",
                                     st_local["vv_q"], k_bank_war),
            10: lambda: st_local.__setitem__("wv_ld", wload(dwv, k_mm[NW - 1])),
            11: lambda: emit_v_tiles(range(0, NT // 2)),
            13: lambda: emit_v_tiles(range(NT // 2, NT)),
        } if (two_buf and NT >= 16 and NW == 4) else None

        ev_evac = {}
        stt_ev = {}
        for nt in range(NT):
            pb = nt % 2
            tr_ = None
            for c in range(ND):
                if c == 0:
                    wt = [("dgp", wr_ld)]
                    if nt < 2:
                        wt += [("vv", prev_sbf)]
                        if h == 0:
                            wt += [("vv", vv_fhT), ("ss", sa_fh_last),
                                   ("vv", vv_xln2)]
                    else:
                        wt += [("vv", ev_evac[nt - 2])]
                else:
                    wt = []
                tr_ = em.t(lambda e, nt=nt, c=c, pb=pb: e.matmul(
                    PS[pb][:, 0:128],
                    arB[:, c * TPC + nt * 128: c * TPC + (nt + 1) * 128],
                    wproj[:, c * 128:(c + 1) * 128],
                    start=(c == 0), stop=(c == ND - 1)), wait=wt)
            ev_evac[nt] = em.v(lambda e, pb=pb: e.tensor_copy(
                rt_pp[pb], PS[pb][:, 0:128]), wait=[("mm", tr_)])
            em.vd()
            ssg = em.s(lambda e, pb=pb, nt=nt: e.activation(
                scorz_pp[nt % 2], rt_pp[pb], ACT.Sigmoid),
                wait=[("vv", ev_evac[nt]),
                      ("vv", stt_ev.get(nt - 2, 0))])
            em.v(lambda e, pb=pb: e.max(out=m8[:, 0:8], in_=rt_pp[pb]))
            em.vd()
            em.v(lambda e, pb=pb: e.match_replace(
                out=rko_pp[0], in_to_replace=m8[:, 0:8], in_values=rt_pp[pb],
                imm_value=NEG_BIG))
            em.vd()
            em.v(lambda e: e.max(out=m8[:, 8:16], in_=rko_pp[0]))
            em.vd()
            em.v(lambda e: e.match_replace(
                out=rko_pp[1], in_to_replace=m8[:, 8:16], in_values=rko_pp[0],
                imm_value=NEG_BIG))
            em.vd()
            em.v(lambda e: e.max(out=m8[:, 16:24], in_=rko_pp[1]))
            em.vd()
            em.v(lambda e: e.match_replace(
                out=rko_pp[0], in_to_replace=m8[:, 16:24], in_values=rko_pp[1],
                imm_value=NEG_BIG))
            em.vd()
            em.v(lambda e: e.max(out=m8[:, 24:32], in_=rko_pp[0]))
            em.vd()
            stt_ev[nt] = em.v(lambda e, pb=pb, nt=nt: e.scalar_tensor_tensor(
                fbit[:, nt * 128:(nt + 1) * 128], rt_pp[pb], m8[:, 31:32],
                scorz_pp[nt % 2], ALU.is_ge, ALU.mult,
                accum_out=zall[:, nt:nt + 1]),
                wait=[("ss", ssg)] +
                ([("mm", prev_scan_td)] if nt == 0 else []))
            em.vd()
            if sched and nt in sched:
                sched[nt]()
        r_mm_last = tr_
        if sched is None:
            st_local["wq_ld"] = wload(dwq, r_mm_last)
            emit_qk_group(q_mm, q_silu, list(range(NW)), "wq_ld",
                          head_entry_v, q_bank_war)
            do_q_stats()
            st_local["wk_ld"] = wload(dwk, q_mm[NW - 1])
            emit_qk_group(k_mm, k_silu, list(range(NW)), "wk_ld",
                          st_local["vv_q"], k_bank_war)
            st_local["wv_ld"] = wload(dwv, k_mm[NW - 1])
            emit_v_tiles(range(NT))
        vv_q = st_local["vv_q"]
        vq_ev[h] = vv_q
        if stop_stage == 41:
            return _finish()
        em.s(lambda e: e.activation(zall[:, 0:NT], zall[:, 0:NT], ACT.Ln,
                                    bias=EPS_Z),
             wait=[("vv", stt_ev[NT - 1])])
        em.sd()
        sz2 = em.s(lambda e: e.activation(zall[:, 0:NT], zall[:, 0:NT],
                                          ACT.Exp, scale=-1.0))
        em.v(lambda e, h=h: e.tensor_tensor(
            zf[:, 0:NT], zall[:, 0:NT],
            fhT3[:, h:h + 1, :].rearrange("p a t -> p (a t)"), ALU.mult),
            wait=[("ss", sz2)] + ([("vv", vv_fhT)] if h == 0 else []))
        em.vd()
        vf_ = None
        for nt in range(NT):
            vf_ = em.v(lambda e, nt=nt: e.tensor_scalar(
                fbit[:, nt * 128:(nt + 1) * 128],
                fbit[:, nt * 128:(nt + 1) * 128],
                zf[:, nt:nt + 1], None, ALU.mult))
        vf_last_d = vf_

        vv_g = None
        tg_last = None
        g_ev = {}
        for cc_ in range(NCH):
            gb = PS[cc_ % 2]
            if cc_ < 2:
                wt = [("vv", vf_), ("vv", ev_evac[NT - 2 + cc_])] + \
                    ([("gg", GGK)] if cc_ == 0 else [])
            else:
                wt = [("vv", g_ev[cc_ - 2])]
            tg_last = em.t(lambda e, cc_=cc_, gb=gb: e.matmul(
                gb[:, 0:128], fbit[:, cc_ * 128:(cc_ + 1) * 128],
                triu_h[:], start=True, stop=True), wait=wt)
            vv_g = em.v(lambda e, cc_=cc_, gb=gb: e.tensor_scalar(
                g_scr[:, cc_ * 128:(cc_ + 1) * 128], gb[:, 0:128],
                G_CLAMP, None, ALU.max), wait=[("mm", tg_last)])
            g_ev[cc_] = vv_g
        em.vd()

        sg1 = em.s(lambda e: e.activation(eg_sb[:], g_scr[:], ACT.Exp),
                   wait=[("vv", vv_g)])
        sg2 = em.s(lambda e: e.activation(em_sb[:], g_scr[:], ACT.Exp,
                                          scale=-1.0))
        vgl = em.v(lambda e: e.tensor_copy(
            egp[:],
            g_scr.rearrange("p (c w) -> p c w", c=NCH)[:, :, 127:128]
            .rearrange("p c a -> p (c a)")), wait=[("vv", vv_g)])
        em.vd()
        segc = em.s(lambda e: e.activation(egc_f[:], egp[:], ACT.Exp),
                    wait=[("vv", vgl)])
        tgp1 = em.t(lambda e: e.transpose(
            PS[0][0:NCH, 0:128], egp[:], ident_f[:]),
            wait=[("vv", vgl), ("vv", g_ev[NCH - 2]), ("gg", GGK)])
        vge = em.v(lambda e: e.tensor_copy(glt[:], PS[0][0:NCH, 0:128]),
                   wait=[("mm", tgp1)])
        em.vd()
        tgp2 = em.t(lambda e: e.matmul(
            PS[0][:, 0:NCH], glt[:], striu_s[:], start=True, stop=True),
            wait=[("vv", vge)])
        vgp2 = em.v(lambda e: e.tensor_copy(gprev[:], PS[0][:, 0:NCH]),
                    wait=[("mm", tgp2)])
        em.vd()
        em.sd()
        sg4 = em.s(lambda e: e.activation(egp[:], gprev[:], ACT.Exp),
                   wait=[("vv", vgp2)])
        va_ = em.v(lambda e: e.tensor_tensor(
            acol[:], gprev[:, NCH - 1:NCH], g_scr[:, TPC - 1:TPC], ALU.add))
        sat = em.s(lambda e: e.activation(Atot[:], acol[:], ACT.Exp),
                   wait=[("vv", va_)])
        veg_last = None
        for c_ in range(NCH):
            veg_last = em.s(lambda e, c_=c_, pp=pp: e.activation(
                eG_pp[pp][:, c_ * 128:(c_ + 1) * 128],
                eg_sb[:, c_ * 128:(c_ + 1) * 128], ACT.Copy,
                scale=egp[:, c_:c_ + 1]),
                wait=([("ss", sg4), ("vv", prev_corr_v)] if c_ == 0 else []))
        veg_ev[h] = veg_last

        sq2k = [stage[:, 0:W], stage[:, W:2 * W]] if TPC >= 2 * W else sq_pp
        vv_k, _ = emit_stats(k_silu, knw_sb, k_bf, {}, vv_q, sq2k,
                             tg_last if sq2k is not sq_pp else
                             st_local["t2q"][NW - 1])
        sv_last = sv_ev[NT - 1]
        sv_last_d[h] = sv_last
        vmm_last[h] = st_local["tv_last"]
        if stop_stage in (42, 43):
            return _finish()

        em.v(lambda e: e.tensor_sub(fdm[:, 1:TPC], g_scr[:, 1:TPC],
                                    g_scr[:, 0:TPC - 1]), wait=[("vv", vv_g)])
        em.vd()
        vfdm = em.v(lambda e: e.tensor_copy(
            fdm.rearrange("p (c w) -> p c w", c=NCH)[:, :, 0:1],
            g_scr.rearrange("p (c w) -> p c w", c=NCH)[:, :, 0:1]))
        em.vd()
        sg3 = em.s(lambda e: e.activation(fdm[:], fdm[:], ACT.Exp),
                   wait=[("vv", vfdm)])
        vs_ = em.v(lambda e: e.tensor_scalar(fdm[:], fdm[:], -1.0, 1.0,
                                             ALU.mult, ALU.add),
                   wait=[("ss", sg3)])
        em.vd()
        em.v(lambda e: e.tensor_tensor(kkt[:], fdm[:], k_bf[:], ALU.mult),
             wait=[("ss", sg2)])
        em.vd()
        vkd = em.v(lambda e: e.tensor_tensor(kd_sb[:], kkt[:], em_sb[:],
                                             ALU.mult))
        em.vd()
        vkd_ev[h] = vkd
        khat_s = {}
        for c_ in range(NCH):
            khat_s[c_] = em.s(lambda e, c_=c_: e.activation(
                khat_sb[:, c_ * 128:(c_ + 1) * 128],
                kd_sb[:, c_ * 128:(c_ + 1) * 128], ACT.Copy,
                scale=egc_f[:, c_:c_ + 1]),
                wait=([("vv", vkd), ("mm", prev_khT_mm)] if c_ == 0 else []))
        if stop_stage == 44:
            return _finish()

        if h == 0:
            tap("q0", q_pp[pp][:], [("vv", vv_q)])
            tap("k0", k_bf[:], [("vv", vv_k)])
            tap("v0", v_tm[:], [("ss", sv_last)])
            tap("eg0", eg_sb[:], [("ss", sg1), ("vv", em.now("vv"))])
            tap("eG0", eG_pp[pp][:], [("ss", veg_ev[h])])
        vqd = em.v(lambda e, pp=pp: e.tensor_tensor(
            stage[:, 0:TPC], q_pp[pp][:], eg_sb[:], ALU.mult),
            wait=[("vv", vv_q), ("mm", tg_last)])
        em.vd()

        RK = min(8, NCH)
        khT_rnd = {}
        tk = None
        for r_ in range(NCH // RK):
            for ci in range(RK):
                c = r_ * RK + ci
                wt = [("ss", khat_s[c])]
                if ci == 0:
                    wt += [("gg", GGK)]
                    wt += ([("ss", khT_rnd[r_ - 1])] if r_ >= 1 else
                           [("ss", prev_skt_last)])
                tk = em.t(lambda e, c=c, ci=ci: e.transpose(
                    PB[:, ci * 128:(ci + 1) * 128],
                    khat_sb[:, c * 128:(c + 1) * 128], ident_bf[:]), wait=wt)
            khT_rnd[r_] = em.s(lambda e, r_=r_, RK=RK: e.copy(
                khT_full[:, r_ * RK * 128:(r_ + 1) * RK * 128],
                PB[:, 0:RK * 128]),
                wait=[("mm", tk)] + ([("vv", vkd)] if r_ == 0 else []))
        prev_khT_mm = tk
        if stop_stage == 46:
            return _finish()

        em.v(lambda e: e.memset(S_f[0][:], 0.0))
        em.vd()
        sbt_prev = em.v(lambda e: e.memset(S_bf[0][:], 0.0))
        em.vd()
        sbf_f = {}
        oc_ev = {}
        vpm_ev = {}
        s_idx = 0
        for c in range(NCH):
            pb = c % 2
            if c >= 2:
                wt_tp = [("vv", vpm_ev[c - 2])]
            else:
                wt_tp = []
            tp = em.t(lambda e, c=c, pb=pb: e.matmul(
                PS[pb][:, 0:128], kd_sb[:, c * 128:(c + 1) * 128],
                stage[:, c * 128:(c + 1) * 128], start=True, stop=True),
                wait=[("vv", vkd), ("vv", vqd)] + wt_tp)
            vpm_ev[c] = em.v(lambda e, pb=pb: e.tensor_tensor(
                pm_sb[pb][:], PS[pb][:, 0:128], triu_f[:], ALU.mult),
                wait=[("mm", tp), ("gg", GGK)])
            if c >= 2:
                wt_o1 = [("ss", oc_ev[c - 2])]
            elif pb == 0:
                wt_o1 = [("ss", sv_ev[NT - 1])]
            else:
                wt_o1 = [("vv", vv_k)]
            to1 = em.t(lambda e, c=c, pb=pb: e.matmul(
                PS[2 + pb][:, 0:128], v_tm[:, c * 128:(c + 1) * 128],
                pm_sb[pb][:], start=True, stop=False),
                wait=[("vv", vpm_ev[c]), ("ss", sv_ev[c])] + wt_o1)
            to2 = em.t(lambda e, c=c, pb=pb, s_idx=s_idx: e.matmul(
                PS[2 + pb][:, 0:128], S_bf[s_idx][:],
                stage[:, c * 128:(c + 1) * 128], start=False, stop=True),
                wait=[("vv", sbt_prev)])
            oc_ev[c] = em.s(lambda e, h=h, c=c, pb=pb: e.copy(
                o_sb[:, h * TPC + c * 128: h * TPC + (c + 1) * 128],
                PS[2 + pb][:, 0:128]), wait=[("mm", to2)])
            if c >= 2:
                wt_td = [("vv", sbf_f[c - 2])]
            elif pb == 0:
                wt_td = [("vv", vv_k)]
            else:
                wt_td = [("ss", sv_last)]
            td = em.t(lambda e, c=c, pb=pb: e.matmul(
                PS[4 + pb][:, 0:128], khT_full[:, c * 128:(c + 1) * 128],
                v_tm[:, c * 128:(c + 1) * 128], start=True, stop=True),
                wait=[("ss", khT_rnd[c // RK]), ("ss", sv_ev[c])] + wt_td)
            ns = 1 - s_idx
            aC = eg_sb[:, c * 128 + 127: c * 128 + 128]
            em.vd()
            sbt_prev = em.v(lambda e, pb=pb, s_idx=s_idx, aC=aC:
                            e.scalar_tensor_tensor(
                                S_bf[0][:], S_f[s_idx][:], aC,
                                PS[4 + pb][:, 0:128], ALU.mult, ALU.add),
                            wait=[("mm", td)])
            sbf_f[c] = em.v(lambda e, s_idx=s_idx, ns=ns, pb=pb, aC=aC:
                            e.scalar_tensor_tensor(
                                S_f[ns][:], S_f[s_idx][:], aC,
                                PS[4 + pb][:, 0:128], ALU.mult, ALU.add))
            s_idx = ns
        sbf_last[h] = sbf_f[NCH - 1]
        if stop_stage == 45:
            return _finish()

        if h == 0:
            tap("S0", S_f[s_idx][:], [("vv", sbf_f[NCH - 1])])
            tap("At0", Atot[:], [("ss", sat)])
            tap("oloc", o_sb[:], [("ss", em.now("ss")), ("vv", em.now("vv"))])
        if h > 0 and do_corr:
            prev_corr_v, prev_corr_mm = correction(
                h - 1, vq_ev[h - 1], veg_ev[h - 1], vmm_last[h],
                sbf_f[NCH - 1], sv_last, sbf_f[NCH - 1])

        nheads_t = H if n_heads is None else n_heads
        if two_buf and h + 1 < nheads_t:
            next_ld["wr_ld"] = wload_A(h + 1, prev_corr_mm)
            next_ld["wq_ld"] = wload_B(dwq, h + 1, prev_corr_mm)

        if do_relay:
            d1r = em.gd(lambda e, s_idx=s_idx: e.dma_start(
                out=drin[0:128, :], in_=S_f[s_idx][:]),
                wait=[("vv", sbf_f[NCH - 1]),
                      ("cc", relay_sent.get(h - 1, 0))])
            d2r = em.gd(lambda e: e.dma_start(
                out=drin[128:129, :].rearrange("a p -> p a"),
                in_=Atot[:]), wait=[("ss", sat)])
            relay_sent[h] = em.cc(lambda e: e.collective_compute(
                "AllGather", ALU.bypass,
                replica_groups=[list(range(C.n_cores))],
                ins=[drin.ap()], outs=[drout.ap()]), wait=[("dgp", d2r)])

        prev_sbf = sbf_f[NCH - 1]
        prev_oc_last = oc_ev[NCH - 1]
        prev_skt_last = khT_rnd[NCH // RK - 1]
        prev_scan_td = td

    if not (do_relay and do_corr):
        return _finish()
    hh_ = (H if n_heads is None else n_heads) - 1
    vv_o, _ = correction(hh_, vq_ev[hh_], veg_ev[hh_], em.now("mm"),
                         sbf_last[hh_], sv_last_d[hh_], em.now("vv"))
    if stop_stage <= 5:
        return _finish()

    tap("o", o_sb[:], [("vv", vv_o)])
    dwol = em.sy(lambda e: e.dma_start(
        out=wo_stage[:].rearrange("p (c o) -> p c o", c=ND),
        in_=dwo.rearrange("(c p) o -> p c o", p=128)),
        wait=[("vv", em.now("vv"))])
    vv_wo = None
    for c in range(ND):
        vv_wo = em.v(lambda e, c=c: e.tensor_scalar(
            wo_bf[:, c * D:(c + 1) * D], wo_stage[:, c * D:(c + 1) * D],
            gnw_sb[:, c:c + 1], None, ALU.mult), wait=[("dsy", dwol)])
    ost = [ar2[:, W:W + W // 2].bitcast(BF16),
           ar2[:, W + W // 2:2 * W].bitcast(BF16)]
    row_ev = {}
    ot_ev = {}
    for nw in range(NW):
        for hh in range(H):
            k_ = nw * H + hh
            s2 = em.s(lambda e, hh=hh, nw=nw: e.activation(
                ost[hh % 2][:, 0:W],
                o_sb[:, hh * TPC + nw * W: hh * TPC + nw * W + W], ACT.Square),
                wait=([("vv", vv_o)] if k_ < 2 else []) +
                [("mm", ot_ev.get(k_ - 2, 0))])
            ot_ev[k_] = em.t(lambda e, hh=hh, nw=nw: e.matmul(
                pvx[0:1, 0:W], ones_col_b[:], ost[hh % 2][:, 0:W],
                start=(hh == 0), stop=(hh == H - 1)),
                wait=[("ss", s2), ("gg", GGK)] +
                ([("vv", row_ev.get(nw - 1, vv_o))] if hh == 0 else []))
        mmo = em.now("mm")
        row_ev[nw] = em.v(lambda e, nw=nw: e.tensor_copy(
            row_att[:, nw * W:(nw + 1) * W], pvx[0:1, 0:W]),
            wait=[("mm", mmo)])
    em.vd()
    em.v(lambda e: e.tensor_scalar(row_att[:], row_att[:], 1.0 / D, None,
                                   ALU.mult))
    em.s(lambda e: e.activation(row_att[:], row_att[:], ACT.Ln,
                                bias=EPS_RMS),
         wait=[("vv", em.now("vv"))])
    em.sd()
    sro2 = em.s(lambda e: e.activation(row_att[:], row_att[:], ACT.Exp,
                                       scale=-0.5))
    vrg = replicate_row(row_att, 4, rep_g, [("ss", sro2)])
    scrm_ev = {}
    for dout in range(ND):
        for nw in range(NW):
            k_ = dout * NW + nw
            pbk = k_ % 2
            tw = None
            for hh in range(H):
                tw = em.t(lambda e, dout=dout, nw=nw, hh=hh, pbk=pbk: e.matmul(
                    PS[pbk][:, 0:W],
                    wo_bf[:, hh * D + dout * 128: hh * D + dout * 128 + 128],
                    o_sb[:, hh * TPC + nw * W: hh * TPC + nw * W + W],
                    start=(hh == 0), stop=(hh == H - 1)),
                    wait=[("vv", vv_wo), ("vv", vv_o),
                          ("vv", scrm_ev.get(k_ - 2, 0))]
                    if hh == 0 else [])
            scrm_ev[k_] = em.v(lambda e, pbk=pbk, nw=nw: e.tensor_tensor(
                scr_m[:, 0:W], PS[pbk][:, 0:W],
                rep_g[:, nw * W:(nw + 1) * W], ALU.mult),
                wait=[("mm", tw), ("vv", vrg)])
            em.vd()
            em.v(lambda e, dout=dout, nw=nw: e.tensor_tensor(
                x_sb[:, dout * TPC + nw * W: dout * TPC + nw * W + W],
                x_sb[:, dout * TPC + nw * W: dout * TPC + nw * W + W],
                scr_m[:, 0:W], ALU.add))
            em.vd()
    attn_done = em.now("vv")

    if stop_stage <= 6:
        return _finish([("vv", attn_done)])
    tap("x2", x_sb[:], [("vv", attn_done)])
    psB.close()
    open_stacks.remove(psB)
    vv_intx2 = ln_quant([("vv", attn_done)])
    psC = ExitStack()
    pxC = [psC.enter_context(nc.psum_tensor(f"pxc{i}", [128, 512], F32))
           for i in range(2)]
    vcmb2 = ffn_gateup(vv_intx2, pxC, em.now("ss"))
    vv_x2 = ffn_down(vcmb2, pxC)
    psC.close()

    return _finish([("vv", vv_x2)])


def _shard_w(wT, nblk, nslot, R):
    outs = []
    rows = wT.shape[0]
    for r in range(R):
        m = np.ones((rows, nslot * 128), np.float32)
        real = max(0, min(nslot, nblk - r * nslot))
        if real:
            m[:, :real * 128] = wT[:, r * nslot * 128:(r * nslot + real) * 128]
        outs.append(np.ascontiguousarray(m))
    return outs


def kernel(L0_attn_hidden_states, L0_attn_Wq, L0_attn_Wk, L0_attn_Wv,
           L0_attn_Wr, L0_attn_Wa, L0_attn_A_log, L0_attn_dt_bias,
           L0_attn_q_norm_w, L0_attn_k_norm_w, L0_attn_g_norm_w, L0_attn_Wo,
           L0_ffn_clip_x, L0_ffn_clip_down, L0_ffn_w_gate, L0_ffn_w_up,
           L0_ffn_w_down):
    cfg = Cfg()
    B, T, Dm = L0_attn_hidden_states.shape
    n = cfg.n_cores
    cpb = cfg.cores_per_batch
    TPC = cfg.TPC
    assert B * T == n * TPC and T == cpb * TPC and Dm == cfg.D

    x = np.ascontiguousarray(L0_attn_hidden_states, dtype=np.float32)
    com = {
        "wq": np.ascontiguousarray(L0_attn_Wq, np.float32),
        "wk": np.ascontiguousarray(L0_attn_Wk, np.float32),
        "wv": np.ascontiguousarray(L0_attn_Wv, np.float32),
        "wr": np.ascontiguousarray(L0_attn_Wr, np.float32),
        "wo": np.ascontiguousarray(L0_attn_Wo, np.float32),
        "wa": np.ascontiguousarray(L0_attn_Wa, np.float32),
        "alog": np.ascontiguousarray(
            np.reshape(L0_attn_A_log, (cfg.H, 1)), np.float32),
        "dtb": np.ascontiguousarray(
            np.reshape(L0_attn_dt_bias, (cfg.H, 1)), np.float32),
        "qnw": np.ascontiguousarray(
            np.reshape(L0_attn_q_norm_w, (1, 128)), np.float32),
        "knw": np.ascontiguousarray(
            np.reshape(L0_attn_k_norm_w, (1, 128)), np.float32),
        "gnw": np.ascontiguousarray(
            np.reshape(L0_attn_g_norm_w, (cfg.ND, 128)).T, np.float32),
        "clipx": np.ascontiguousarray(
            np.reshape(L0_ffn_clip_x, (1, 1)), np.float32),
        "clipd": np.ascontiguousarray(
            np.reshape(L0_ffn_clip_down, (1, 1)), np.float32),
    }
    wgt = np.ascontiguousarray(np.transpose(L0_ffn_w_gate), np.float32)
    wut = np.ascontiguousarray(np.transpose(L0_ffn_w_up), np.float32)
    wdt = np.ascontiguousarray(np.transpose(L0_ffn_w_down), np.float32)
    wgm = _shard_w(wgt, cfg.NI, cfg.GS, n)
    wum = _shard_w(wut, cfg.NI, cfg.GS, n)
    wdm = _shard_w(wdt, cfg.NHB, cfg.DS, n)
    in_maps = []
    for c in range(n):
        b = c // cpb
        pos = c % cpb
        t0 = pos * TPC
        mask = np.zeros((1, n), np.float32)
        mask[0, b * cpb:b * cpb + pos] = 1.0
        m = dict(com)
        m["xt"] = np.ascontiguousarray(x[b, t0:t0 + TPC, :].T)
        m["rmask"] = mask
        m["wgm"] = wgm[c]
        m["wum"] = wum[c]
        m["wdm"] = wdm[c]
        in_maps.append(m)

    nc = build(cfg)
    res = run_bass_kernel_spmd(nc, in_maps, core_ids=list(range(n)))
    out = np.empty((B, T, Dm), np.float32)
    for c in range(n):
        b = c // cpb
        t0 = (c % cpb) * TPC
        out[b, t0:t0 + TPC, :] = np.asarray(res.results[c]["xout"]).T
    return out
